# version 01
# baseline (speedup 1.0000x reference)
import numpy as np
import ml_dtypes
import concourse.bass as bass
import concourse.mybir as mybir
from concourse.bass_utils import run_bass_kernel_spmd

F32 = mybir.dt.float32
BF16 = mybir.dt.bfloat16
AF = mybir.ActivationFunctionType
ALU = mybir.AluOpType
AX = mybir.AxisListType

D = 1024
T = 8192
B = 2
DI = 2048
NST = 16
E = 8
FF = 4096
NH = 16
HD = 64
DR = 32
DC = 128
CT = 1024
NCH = 2
TCORE = NCH * CT
TC = 512
DGRP = 4
LN_EPS = 1e-5
MASKVAL = -30000.0
S96 = float(1.0 / np.sqrt(np.float64(HD + DR)))

UPTO = "all"


def _chunks_of_core(k):
    return [(0, k), (1, 7 - k)]


def _fm(xslice):
    t, d = xslice.shape
    return np.ascontiguousarray(
        xslice.reshape(t, d // 128, 128).transpose(2, 1, 0).reshape(128, -1))


def make_host_inputs(inputs):
    x = np.asarray(inputs["L0_attn_a_x"], np.float32)
    inv_freq = (1.0 / 10000.0 ** (np.arange(0, DR, 2, dtype=np.float32) / DR)).astype(np.float32)
    fr = np.outer(np.arange(T, dtype=np.float32), inv_freq)
    cos = np.cos(fr.astype(np.float64)).astype(np.float32).T
    sin = np.sin(fr.astype(np.float64)).astype(np.float32).T
    ropes = np.zeros((128, T), np.float32)
    ropes[0:16] = cos * np.float32(S96)
    ropes[16:32] = cos * np.float32(S96)
    ropes[32:48] = -sin * np.float32(S96)
    ropes[48:64] = sin * np.float32(S96)
    ropes[64:80] = cos
    ropes[80:96] = cos
    ropes[96:112] = -sin
    ropes[112:128] = sin
    masks = np.zeros((128, 4 * 512), np.float32)
    ii = np.arange(128)[:, None]
    jj = np.arange(512)[None, :]
    for di, delta in enumerate((0, 128, 256, 384)):
        masks[:, di * 512:(di + 1) * 512] = np.where(ii + delta > jj, MASKVAL, 0.0)
    onehots = np.zeros((48, 16 * 128), np.float32)
    for n in range(16):
        onehots[n, n * 128:(n + 1) * 128] = 1.0
        onehots[32 + n, n * 128:(n + 1) * 128] = 1.0
    conv_w = np.asarray(inputs["L0_attn_a_conv_w"], np.float32)
    conv_w_pd = np.ascontiguousarray(
        conv_w[:, 0, :].reshape(16, 128, 4).transpose(1, 0, 2).reshape(128, 64))
    conv_b_pd = np.ascontiguousarray(
        np.asarray(inputs["L0_attn_a_conv_b"], np.float32).reshape(16, 128).T)
    A_pd = np.ascontiguousarray(
        np.asarray(inputs["L0_attn_a_A_log"], np.float32).reshape(16, 128, 16)
        .transpose(1, 0, 2).reshape(128, 256))
    D_pd = np.ascontiguousarray(
        np.asarray(inputs["L0_attn_a_D_skip"], np.float32).reshape(16, 128).T)
    gamma_pd = np.ascontiguousarray(
        np.asarray(inputs["L0_attn_a_gamma"], np.float32).reshape(8, 128).T)
    beta_pd = np.ascontiguousarray(
        np.asarray(inputs["L0_attn_a_beta"], np.float32).reshape(8, 128).T)
    wx = np.asarray(inputs["L0_attn_a_W_x"], np.float32)
    wx_ext = np.zeros((DI, 65), np.float32)
    wx_ext[:, 0:16] = wx[:, 0:16]
    wx_ext[:, 32:48] = wx[:, 16:32]
    wx_ext[:, 64] = wx[:, 32]
    wqr = np.asarray(inputs["L0_attn_b_W_q_rope"], np.float32)
    wqr_ext = np.ascontiguousarray(np.concatenate(
        [wqr[:, 0:16], wqr[:, 16:32], wqr[:, 16:32], wqr[:, 0:16]], 1))
    wkr = np.asarray(inputs["L0_attn_b_W_k_rope"], np.float32)
    wkr_ext = np.ascontiguousarray(np.concatenate(
        [wkr[:, 0:16], wkr[:, 16:32], wkr[:, 16:32], wkr[:, 0:16]], 1))

    shared = {
        "masksbf": masks.astype(ml_dtypes.bfloat16),
        "ropesbf": ropes.astype(ml_dtypes.bfloat16),
        "onehots": onehots,
        "conv_w_pd": conv_w_pd, "conv_b_pd": conv_b_pd,
        "A_pd": A_pd, "D_pd": D_pd, "gamma_pd": gamma_pd, "beta_pd": beta_pd,
        "wx_ext": wx_ext, "wqr_ext": wqr_ext, "wkr_ext": wkr_ext,
        "neg1": np.full((128, 8), -1.0, np.float32),
        "ident": np.eye(128, dtype=np.float32),
        "w_in": np.asarray(inputs["L0_attn_a_W_in"], np.float32),
        "w_out": np.asarray(inputs["L0_attn_a_W_out"], np.float32),
        "gate_w": np.asarray(inputs["L0_ffn_gate_w"], np.float32),
        "w1": np.asarray(inputs["L0_ffn_w1"], np.float32).reshape(E * D, FF),
        "w2": np.asarray(inputs["L0_ffn_w2"], np.float32).reshape(E * FF, D),
        "w_kvd": np.asarray(inputs["L0_attn_b_W_kv_down"], np.float32),
        "w_kup": np.asarray(inputs["L0_attn_b_W_k_up"], np.float32),
        "w_vup": np.asarray(inputs["L0_attn_b_W_v_up"], np.float32),
        "w_q": np.asarray(inputs["L0_attn_b_W_q"], np.float32),
        "w_o": np.asarray(inputs["L0_attn_b_W_o"], np.float32),
    }
    qaug = np.zeros((4, 1024), np.float32)
    qaug[1, 0:512] = 1.0
    qaug[2, 512:1024] = 1.0
    shared["qaug"] = qaug
    per_core = []
    for k in range(8):
        m = dict(shared)
        xfm = np.empty((128, 8, NCH, CT), np.float32)
        halo = np.zeros((128, NCH, 8, 3), np.float32)
        hmask = np.zeros((128, NCH), np.float32)
        agmask = np.zeros((128, NCH * 7), np.float32)
        for ci, (b, p) in enumerate(_chunks_of_core(k)):
            f = _fm(x[b, p * CT:(p + 1) * CT, :]).reshape(128, 8, CT)
            xfm[:, :, ci, :] = f
            if p > 0:
                halo[:, ci] = _fm(x[b, p * CT - 3:p * CT, :]).reshape(128, 8, 3)
                hmask[:, ci] = 1.0
            for s in range(7):
                if s < p:
                    agmask[:, ci * 7 + s] = 1.0
        kaug = np.zeros((NCH * 4, T), np.float32)
        ropes_own = np.zeros((128, NCH * CT), np.float32)
        for ci, (b, p) in enumerate(_chunks_of_core(k)):
            kaug[ci * 4 + 0, :] = 1.0
            kaug[ci * 4 + 1, p * CT:] = MASKVAL
            kaug[ci * 4 + 2, p * CT + 512:] = MASKVAL
            sl = slice(p * CT, (p + 1) * CT)
            ropes_own[0:64, ci * CT:(ci + 1) * CT] = ropes[0:64, sl]
            ropes_own[64:128, ci * CT:(ci + 1) * CT] = ropes[64:128, sl]
        m["kaug"] = kaug
        m["ropesbf_own"] = ropes_own.astype(ml_dtypes.bfloat16)
        m["xfm"] = np.ascontiguousarray(xfm.reshape(128, -1))
        m["halo"] = np.ascontiguousarray(halo.reshape(128, -1))
        m["hmask"] = hmask
        m["agmask"] = agmask
        per_core.append(m)
    return per_core


def assemble_output(results):
    out = np.empty((B, T, D), np.float32)
    for k in range(8):
        y = results[k]["yfm"].reshape(128, 8, NCH, CT)
        for ci, (b, p) in enumerate(_chunks_of_core(k)):
            out[b, p * CT:(p + 1) * CT, :] = y[:, :, ci].transpose(2, 1, 0).reshape(CT, D)
    return out


def col(ci, g, t0):
    return g * (NCH * CT) + ci * CT + t0



_COMPUTE_OPS = {
    "tensor_copy", "tensor_tensor", "tensor_add", "tensor_sub", "tensor_mul",
    "tensor_max", "tensor_scalar", "tensor_single_scalar", "tensor_scalar_add",
    "tensor_scalar_sub", "tensor_scalar_mul", "tensor_scalar_max",
    "tensor_scalar_min", "tensor_relu", "tensor_reduce", "reduce_max",
    "reduce_sum", "scalar_tensor_tensor", "tensor_tensor_scan", "memset",
    "memzero", "activation", "mul", "add", "copy", "square", "sqrt", "sign",
    "reciprocal", "select", "copy_predicated", "transpose", "bn_stats",
    "bn_aggr", "max", "max_index", "match_replace", "stream_shuffle", "iota",
    "engine_nop", "pool", "pool_avg", "pool_max",
}


class Eng:
    def __init__(self, eng, ctx, semname):
        self._e, self._c, self._s = eng, ctx, semname

    def __getattr__(self, name):
        f = getattr(self._e, name)
        if name in _COMPUTE_OPS:
            def wrap(*a, **k):
                self._c.w(self._e, self._s)
                ins = f(*a, **k)
                ins.then_inc(self._c.sems[self._s], 1)
                self._c.cnt[self._s] += 1
                return ins
            return wrap
        return f

class Ctx:
    def __init__(self, nc, block, sems):
        self.nc, self.block, self.sems = nc, block, sems
        self.cnt = {k: 0 for k in sems}

    def inc(self, instr, name, k=1):
        if name in ("sDVE", "sACT", "sPOOL"):
            return self.cnt[name]
        instr.then_inc(self.sems[name], k)
        self.cnt[name] += k
        return self.cnt[name]

    def w(self, eng, name, val=None):
        v = self.cnt[name] if val is None else val
        if v > 0:
            eng.wait_ge(self.sems[name], v)

    def barrier(self):
        snap = dict(self.cnt)
        for en in ("sync", "tensor", "vector", "scalar", "gpsimd"):
            def f(e, snap=snap):
                for nm, v in snap.items():
                    if v > 0:
                        e.wait_ge(self.sems[nm], v)
            getattr(self.block, en)(f)


class Arena:
    def __init__(self, t, nbytes):
        self.t, self.nbytes, self.off, self.stack = t, nbytes, 0, []
        self.peak = 0

    def push(self):
        self.stack.append(self.off)

    def pop(self):
        self.off = self.stack.pop()

    def alloc(self, rows, cols, dt):
        esz = 2 if dt == BF16 else 4
        self.off = (self.off + 3) // 4 * 4
        base = self.off
        self.off += cols * esz
        self.peak = max(self.peak, self.off)
        assert self.off <= self.nbytes, f"arena overflow {self.off}>{self.nbytes}"
        v = self.t.ap()[0:rows, base // 2: base // 2 + cols * (esz // 2)]
        if dt == F32:
            v = v.bitcast(F32)
        return v


def build_nc(upto=None):
    upto = upto or UPTO
    nc = bass.Bass("TRN2", target_bir_lowering=False)
    d_in = {}
    for nm, shape in [
        ("xfm", [128, 8 * NCH * CT]), ("halo", [128, NCH * 8 * 3]),
        ("hmask", [128, NCH]), ("agmask", [128, NCH * 7]),
        ("onehots", [48, 2048]),
        ("neg1", [128, 8]), ("ident", [128, 128]), ("qaug", [4, 1024]), ("kaug", [NCH * 4, T]),
        ("conv_w_pd", [128, 64]), ("conv_b_pd", [128, 16]), ("A_pd", [128, 256]),
        ("D_pd", [128, 16]), ("gamma_pd", [128, 8]), ("beta_pd", [128, 8]),
        ("wx_ext", [DI, 65]), ("wqr_ext", [D, 64]), ("wkr_ext", [DC, 64]),
        ("w_in", [D, 2 * DI]), ("w_out", [DI, D]), ("gate_w", [D, E]),
        ("w1", [E * D, FF]), ("w2", [E * FF, D]),
        ("w_kvd", [D, DC]), ("w_kup", [DC, D]), ("w_vup", [DC, D]),
        ("w_q", [D, D]), ("w_o", [D, D]),
    ]:
        d_in[nm] = nc.dram_tensor(nm, shape, F32, kind="ExternalInput").ap()
    for nm, shape in [("masksbf", [128, 2048]), ("ropesbf", [128, T]),
                      ("ropesbf_own", [128, NCH * CT])]:
        d_in[nm] = nc.dram_tensor(nm, shape, BF16, kind="ExternalInput").ap()
    yfm = nc.dram_tensor("yfm", [128, 8 * NCH * CT], F32, kind="ExternalOutput").ap()

    d_bf = {}
    for nm, shape in [
        ("w_in", [D, 2 * DI]), ("w_out", [DI, D]), ("wx_ext", [DI, 65]),
        ("gate_w", [D, E]), ("w1", [E * D, FF]), ("w2", [E * FF, D]),
        ("w_kvd", [D, DC]), ("w_kup", [DC, D]), ("w_vup", [DC, D]),
        ("w_q", [D, D]), ("w_o", [D, D]), ("wqr_ext", [D, 64]),
        ("wkr_ext", [DC, 64]),
    ]:
        d_bf[nm] = nc.dram_tensor(nm + "_bf", shape, BF16).ap()
    ag1_in = nc.dram_tensor("ag1_in", [128, NCH * 257], F32)
    ag1_out = nc.dram_tensor("ag1_out", [8 * 128, NCH * 257], F32)
    ag2_in = nc.dram_tensor("ag2_in", [128, TCORE], BF16)
    ag2_out = nc.dram_tensor("ag2_out", [8 * 128, TCORE], BF16)

    from contextlib import ExitStack
    with ExitStack() as _es:
        res = _es.enter_context(nc.sbuf_tensor("res", [128, 8 * NCH * CT], F32))
        arena_t = _es.enter_context(nc.sbuf_tensor("arena", [128, 40960], BF16))
        xnbf = _es.enter_context(nc.sbuf_tensor("xnbf", [128, 8 * CT], BF16))
        rows1 = _es.enter_context(nc.sbuf_tensor("rows1", [66, NCH * CT], F32))
        rows2 = _es.enter_context(nc.sbuf_tensor("rows2", [66, NCH * CT], F32))
        rows3 = _es.enter_context(nc.sbuf_tensor("rows3", [66, NCH * CT], F32))
        small = _es.enter_context(nc.sbuf_tensor("small", [128, 700], F32))
        oneh = _es.enter_context(nc.sbuf_tensor("oneh", [48, 2048], F32))
        ones3 = _es.enter_context(nc.sbuf_tensor("ones3", [66, 128], F32))
        hstart = _es.enter_context(nc.sbuf_tensor("hstart", [128, 256], F32))
        pub = _es.enter_context(nc.sbuf_tensor("pub", [128, NCH * 257], F32))
        pred = _es.enter_context(nc.sbuf_tensor("pred", [128, 257], F32))
        wring_t = _es.enter_context(nc.sbuf_tensor("wring", [128, 4096], BF16))
        PS = _es.enter_context(nc.psum_tensor("PS", [128, 4096], F32))
        _sems = [_es.enter_context(nc.semaphore("s%d" % i)) for i in range(9)]
        _1, _2, _3, _4, _5, _6, _7, _8, _9 = _sems
        block = _es.enter_context(nc.Block())
        def BV(fn):
            block.vector(lambda v: fn(Eng(v, ctx, "sDVE")))

        def BS(fn):
            block.scalar(lambda a: fn(Eng(a, ctx, "sACT")))

        def BG(fn):
            block.gpsimd(lambda g: fn(Eng(g, ctx, "sPOOL")))

        ctx = Ctx(nc, block, {"sDW": _1, "sDX": _2, "sPE": _3, "sACT": _4,
                              "sDVE": _5, "sPOOL": _6, "sOUT": _7, "sCC": _8, "sDG": _9})
        AR = Arena(arena_t, 81920)
        WR = Arena(wring_t, 8192)

        def bank(i, n=1):
            return PS[:, i * 512:(i + n) * 512]

        sm = small.ap()
        convw_sb = sm[:, 0:64]
        convb_sb = sm[:, 64:80]
        A_sb = sm[:, 80:336]
        D_sb = sm[:, 336:352]
        gam_sb = sm[:, 352:360]
        bet_sb = sm[:, 360:368]
        ones_col = sm[:, 368:369]
        eps_col = sm[:, 369:370]
        hmask_sb = sm[:, 370:372]
        agm_sb = sm[:, 372:386]
        sdt_sb = sm[:, 386:388]
        neg1_sb = sm[:, 388:396]
        tiny = sm[:, 400:700]

        def p0_sync(s):
            for dst, src in [
                (res[:, :], d_in["xfm"]), (oneh[:, :], d_in["onehots"]),
                (convw_sb, d_in["conv_w_pd"]), (convb_sb, d_in["conv_b_pd"]),
                (A_sb, d_in["A_pd"]), (D_sb, d_in["D_pd"]),
                (gam_sb, d_in["gamma_pd"]), (bet_sb, d_in["beta_pd"]),
                (hmask_sb, d_in["hmask"]), (agm_sb, d_in["agmask"]),
                (neg1_sb, d_in["neg1"]),
            ]:
                ctx.inc(s.dma_start(out=dst, in_=src), "sDX", 16)
        block.sync(p0_sync)

        def p0_vec(v):
            ctx.w(v, "sDX")
            v.memset(ones_col, 1.0)
            v.memset(eps_col, float(LN_EPS))
            v.memset(ones3[:, :], 1.0)
            ctx.inc(v.memset(hstart[:, :], 0.0), "sDVE")
        BV(p0_vec)

        def p0_scalar(sc):
            ctx.w(sc, "sDVE")
            sc.activation(A_sb, A_sb, AF.Exp)
            ctx.inc(sc.mul(A_sb, A_sb, -1.0), "sACT")
        BS(p0_scalar)

        AR.push()
        stage = AR.alloc(128, 4096, BF16)

        def p0_preconv(g):
            ctx.w(g, "sACT")
            names = ["w_in", "w_out", "wx_ext", "gate_w", "w_kvd", "w_kup",
                     "w_vup", "w_q", "w_o", "wqr_ext", "wkr_ext"]
            if upto in ("ffn1", "attn", "all"):
                names += ["w1", "w2"]
            for nm in names:
                src, dst = d_in[nm], d_bf[nm]
                rows, cols = src.shape
                r0 = 0
                while r0 < rows:
                    r1 = min(rows, r0 + 128)
                    n = r1 - r0
                    wd = ctx.inc(g.dma_start(out=stage[0:n, 0:cols],
                                             in_=src[r0:r1, :]), "sDG", 16)
                    g.wait_ge(ctx.sems["sDG"], wd)
                    ctx.inc(g.dma_start(out=dst[r0:r1, :], in_=stage[0:n, 0:cols]),
                            "sDG", 16)
                    g.wait_ge(ctx.sems["sDG"], ctx.cnt["sDG"])
                    r0 = r1
        BG(p0_preconv)
        AR.pop()
        ctx.barrier()

        winb = d_bf["w_in"]
        woutb = d_bf["w_out"]
        wxb = d_bf["wx_ext"]

        wx_sb = AR.alloc(128, 16 * 65, BF16)
        ident_sb = AR.alloc(128, 128, F32)
        wi_ring = [WR.alloc(128, 1024, BF16) for _ in range(2)]
        wo_ring = [WR.alloc(128, 1024, BF16) for _ in range(2)]

        def p_load_wx(s):
            for di in range(16):
                ctx.inc(s.dma_start(out=wx_sb[:, di * 65:(di + 1) * 65],
                                    in_=wxb[di * 128:(di + 1) * 128, :]), "sDW", 16)
            ctx.inc(s.dma_start(out=ident_sb, in_=d_in["ident"]), "sDW", 16)
        block.sync(p_load_wx)

        def bcast(t, dst, src_row, tn):
            bp = src_row.base_partition()
            onr = ones3[bp:bp + 1, 0:128]
            ins = None
            for i in range((tn + 511) // 512):
                c0, c1 = i * 512, min(tn, (i + 1) * 512)
                ins = t.matmul(dst[:, c0:c1], onr, src_row[:, c0:c1],
                               start=True, stop=True)
            return ins

        def win_slice(di, zhalf=False):
            c0 = (DI if zhalf else 0) + di * 128
            return winb[:, c0:c0 + 128].rearrange("(m p) f -> p m f", p=128)

        def emit_ln2(srcv, tn, mu_r, rs_r, mui_r, rsi_r, dstfn, sq, lnt,
                     mask_col=None):
            def on_sq(a):
                ctx.w(a, "sDVE")
                for i in range(8):
                    ins = a.activation(sq[:, i * tn:(i + 1) * tn], srcv[i], AF.Square)
                ctx.inc(ins, "sACT")
            BS(on_sq)

            def on_pe(t):
                ctx.w(t, "sACT")
                ctx.w(t, "sDVE")
                for i in range(8):
                    t.matmul(PS[0:1, 0:tn], ones_col, srcv[i],
                             start=(i == 0), stop=(i == 7))
                for i in range(8):
                    ins = t.matmul(PS[32:33, 0:tn], ones_col,
                                   sq[:, i * tn:(i + 1) * tn],
                                   start=(i == 0), stop=(i == 7))
                ctx.inc(ins, "sPE")
            block.tensor(on_pe)

            def on_rows(v):
                ctx.w(v, "sPE")
                v.tensor_scalar_mul(mu_r, PS[0:1, 0:tn], 1.0 / D)
                v.tensor_scalar_mul(rs_r, PS[32:33, 0:tn], 1.0 / D)
                tmp = PS[32:33, 3 * 512:3 * 512 + tn]
                v.tensor_mul(tmp, mu_r, mu_r)
                ctx.inc(v.tensor_sub(rs_r, rs_r, tmp), "sDVE")
            BV(on_rows)

            def on_rstd(a):
                ctx.w(a, "sDVE")
                a.activation(rs_r, rs_r, AF.Ln, bias=eps_col[0:1, 0:1])
                ctx.inc(a.activation(rs_r, rs_r, AF.Exp, scale=-0.5), "sACT")
            BS(on_rstd)

            def on_bc(t):
                ctx.w(t, "sACT")
                bcast(t, bank(1), mu_r, tn)
                ctx.inc(bcast(t, bank(2), rs_r, tn), "sPE")
            block.tensor(on_bc)

            def on_ln0(v):
                ctx.w(v, "sPE")
                for i in range(8):
                    o = lnt[:, i * tn:(i + 1) * tn]
                    v.tensor_sub(o, srcv[i], bank(1)[:, 0:tn])
                    ins = v.tensor_mul(o, o, bank(2)[:, 0:tn])
                ctx.inc(ins, "sDVE")
            BV(on_ln0)

            def on_sq2(a):
                ctx.w(a, "sDVE")
                for i in range(8):
                    ins = a.activation(sq[:, i * tn:(i + 1) * tn],
                                       lnt[:, i * tn:(i + 1) * tn], AF.Square)
                ctx.inc(ins, "sACT")
            BS(on_sq2)

            def on_pe2(t):
                ctx.w(t, "sACT")
                for i in range(8):
                    t.matmul(PS[0:1, 0:tn], ones_col,
                             lnt[:, i * tn:(i + 1) * tn],
                             start=(i == 0), stop=(i == 7))
                for i in range(8):
                    ins = t.matmul(PS[32:33, 0:tn], ones_col,
                                   sq[:, i * tn:(i + 1) * tn],
                                   start=(i == 0), stop=(i == 7))
                ctx.inc(ins, "sPE")
            block.tensor(on_pe2)

            def on_rows2(v):
                ctx.w(v, "sPE")
                v.tensor_scalar_mul(mui_r, PS[0:1, 0:tn], 1.0 / D)
                v.tensor_scalar_mul(rsi_r, PS[32:33, 0:tn], 1.0 / D)
                tmp = PS[32:33, 3 * 512:3 * 512 + tn]
                v.tensor_mul(tmp, mui_r, mui_r)
                ctx.inc(v.tensor_sub(rsi_r, rsi_r, tmp), "sDVE")
            BV(on_rows2)

            def on_rstd2(a):
                ctx.w(a, "sDVE")
                a.activation(rsi_r, rsi_r, AF.Ln, bias=eps_col[0:1, 0:1])
                ctx.inc(a.activation(rsi_r, rsi_r, AF.Exp, scale=-0.5), "sACT")
            BS(on_rstd2)

            def on_bc2(t):
                ctx.w(t, "sACT")
                bcast(t, bank(1), mui_r, tn)
                ctx.inc(bcast(t, bank(2), rsi_r, tn), "sPE")
            block.tensor(on_bc2)

            def on_xn(v):
                ctx.w(v, "sPE")
                for i in range(8):
                    o = lnt[:, i * tn:(i + 1) * tn]
                    v.tensor_sub(o, o, bank(1)[:, 0:tn])
                    v.tensor_mul(o, o, bank(2)[:, 0:tn])
                    if mask_col is None:
                        ins = v.tensor_scalar(dstfn(i), o, gam_sb[:, i:i + 1],
                                              bet_sb[:, i:i + 1], ALU.mult, ALU.add)
                    else:
                        v.tensor_scalar(o, o, gam_sb[:, i:i + 1],
                                        bet_sb[:, i:i + 1], ALU.mult, ALU.add)
                        ins = v.tensor_scalar_mul(dstfn(i), o, mask_col)
                ctx.inc(ins, "sDVE")
            BV(on_xn)

        def emit_xinner_conv(ci, t0, xconv, xi_sb, stash, conv_acc):
            for di in range(16):
                ring = wi_ring[di % 2]

                def ld(s, di=di, ring=ring):
                    ctx.w(s, "sPE")
                    ctx.inc(s.dma_start(out=ring[:, 0:1024].rearrange("p (m f) -> p m f", m=8), in_=win_slice(di)),
                            "sDW", 16)
                block.sync(ld)

                def mm(t, di=di, ring=ring, t0=t0):
                    ctx.w(t, "sDW")
                    ctx.w(t, "sDVE")
                    for m in range(8):
                        ins = t.matmul(bank(3)[:, 0:TC],
                                       ring[:, m * 128:(m + 1) * 128],
                                       xnbf[:, m * CT + t0:m * CT + t0 + TC],
                                       start=(m == 0), stop=(m == 7))
                    ctx.inc(ins, "sPE")
                block.tensor(mm)

                def conv(v, di=di):
                    ctx.w(v, "sPE")
                    ctx.w(v, "sACT")
                    v.tensor_copy(xi_sb[:, 0:3], stash[:, di * 3:(di + 1) * 3])
                    v.tensor_copy(xi_sb[:, 3:3 + TC], bank(3)[:, 0:TC])
                    v.tensor_copy(stash[:, di * 3:(di + 1) * 3],
                                  bank(3)[:, TC - 3:TC])
                    v.tensor_scalar_mul(conv_acc[:, 0:TC], xi_sb[:, 0:TC],
                                        convw_sb[:, di * 4:di * 4 + 1])
                    for j in range(1, 4):
                        ins = v.scalar_tensor_tensor(
                            conv_acc[:, 0:TC], xi_sb[:, j:j + TC],
                            convw_sb[:, di * 4 + j:di * 4 + j + 1],
                            conv_acc[:, 0:TC], ALU.mult, ALU.add)
                    ctx.inc(ins, "sDVE")
                BV(conv)

                def silu(a, di=di, t0=t0):
                    ctx.w(a, "sDVE")
                    ctx.inc(a.activation(
                        xconv[:, di * CT + t0:di * CT + t0 + TC],
                        conv_acc[:, 0:TC], AF.Silu,
                        bias=convb_sb[:, di:di + 1]), "sACT")
                BS(silu)

        def emit_halo(ci, halo_f, halo_xn, stash, sq, lnt):
            def ld(s, ci=ci):
                ctx.w(s, "sACT")
                ctx.w(s, "sDVE")
                ctx.w(s, "sPOOL")
                ctx.w(s, "sPE")
                ctx.inc(s.dma_start(out=halo_f[:, :],
                                    in_=d_in["halo"][:, ci * 24:(ci + 1) * 24]),
                        "sDX", 16)
            block.sync(ld)

            def wt(v):
                ctx.w(v, "sDX")
                ctx.inc(v.engine_nop(), "sDVE")
            BV(wt)
            hv = [halo_f[:, i * 3:(i + 1) * 3] for i in range(8)]
            emit_ln2(hv, 3, tiny[0:1, 8:11], tiny[0:1, 12:15],
                     tiny[0:1, 16:19], tiny[0:1, 20:23],
                     lambda i: halo_xn[:, i * 3:(i + 1) * 3],
                     sq, lnt, mask_col=hmask_sb[:, ci:ci + 1])
            for di in range(16):
                ring = wi_ring[di % 2]

                def ld2(s, di=di, ring=ring):
                    ctx.w(s, "sPE")
                    ctx.inc(s.dma_start(out=ring[:, 0:1024].rearrange("p (m f) -> p m f", m=8), in_=win_slice(di)),
                            "sDW", 16)
                block.sync(ld2)

                def mm(t, di=di, ring=ring):
                    ctx.w(t, "sDW")
                    ctx.w(t, "sDVE")
                    for m in range(8):
                        ins = t.matmul(bank(3)[:, 0:3],
                                       ring[:, m * 128:(m + 1) * 128],
                                       halo_xn[:, m * 3:(m + 1) * 3],
                                       start=(m == 0), stop=(m == 7))
                    ctx.inc(ins, "sPE")
                block.tensor(mm)

                def ev(v, di=di):
                    ctx.w(v, "sPE")
                    ctx.inc(v.tensor_copy(stash[:, di * 3:(di + 1) * 3],
                                          bank(3)[:, 0:3]), "sDVE")
                BV(ev)

        for ci in (range(NCH) if upto != "p0" else []):
            AR.push()
            xconv = AR.alloc(128, 16 * CT, BF16)
            AR.push()
            sq_t = AR.alloc(128, 8 * TC, F32)
            ln_t = AR.alloc(128, 8 * TC, F32)
            xi_sb = AR.alloc(128, TC + 3, F32)
            stash = AR.alloc(128, 16 * 3, F32)
            halo_xn = AR.alloc(128, 8 * 3, BF16)
            halo_f = AR.alloc(128, 8 * 3, F32)
            conv_acc = AR.alloc(128, TC, F32)

            emit_halo(ci, halo_f, halo_xn, stash, sq_t, ln_t)
            for tc in range(CT // TC):
                t0 = tc * TC
                srcv = [res[:, col(ci, dh, t0):col(ci, dh, t0) + TC] for dh in range(8)]
                emit_ln2(srcv, TC,
                         rows2[0:1, ci * CT + t0:ci * CT + t0 + TC],
                         rows2[32:33, ci * CT + t0:ci * CT + t0 + TC],
                         rows3[0:1, ci * CT + t0:ci * CT + t0 + TC],
                         rows3[32:33, ci * CT + t0:ci * CT + t0 + TC],
                         lambda i, t0=t0: xnbf[:, i * CT + t0:i * CT + t0 + TC],
                         sq_t, ln_t)
                emit_xinner_conv(ci, t0, xconv, xi_sb, stash, conv_acc)

                def wx_mm(t, ci=ci, t0=t0, xconv=xconv):
                    ctx.w(t, "sACT")
                    ctx.w(t, "sDVE")
                    for di in range(16):
                        ins = t.matmul(PS[0:65, 4 * 512:4 * 512 + TC],
                                       wx_sb[:, di * 65:(di + 1) * 65],
                                       xconv[:, di * CT + t0:di * CT + t0 + TC],
                                       start=(di == 0), stop=(di == 15))
                    ctx.inc(ins, "sPE")
                block.tensor(wx_mm)

                dt_r = rows1[64:65, ci * CT + t0:ci * CT + t0 + TC]
                b_r = rows1[0:16, ci * CT + t0:ci * CT + t0 + TC]
                c_r = rows1[32:48, ci * CT + t0:ci * CT + t0 + TC]
                scr = rows2[64:65, ci * CT + t0:ci * CT + t0 + TC]

                def dtrow1(v, dt_r=dt_r, scr=scr):
                    ctx.w(v, "sPE")
                    ctx.inc(v.tensor_scalar_max(
                        scr, PS[64:65, 4 * 512:4 * 512 + TC], 0.0), "sDVE")
                BV(dtrow1)

                def dtrow2(a, dt_r=dt_r):
                    ctx.w(a, "sDVE")
                    a.activation(dt_r, PS[64:65, 4 * 512:4 * 512 + TC], AF.Abs)
                    a.activation(dt_r, dt_r, AF.Exp, scale=-1.0)
                    ctx.inc(a.activation(dt_r, dt_r, AF.Ln, bias=ones_col[0:1, 0:1]), "sACT")
                BS(dtrow2)

                def dtrow3(v, dt_r=dt_r, scr=scr, b_r=b_r, c_r=c_r):
                    ctx.w(v, "sACT")
                    v.tensor_add(dt_r, dt_r, scr)
                    v.tensor_copy(c_r, PS[32:48, 4 * 512:4 * 512 + TC])
                    ctx.inc(v.tensor_copy(b_r, PS[0:16, 4 * 512:4 * 512 + TC]),
                            "sDVE")
                BV(dtrow3)

                def dtb16(t, dt_r=dt_r):
                    ctx.w(t, "sDVE")
                    ctx.inc(t.matmul(PS[0:16, 5 * 512:5 * 512 + TC],
                                     ones3[64:65, 0:16], dt_r,
                                     start=True, stop=True), "sPE")
                block.tensor(dtb16)

                def wBmul(v, b_r=b_r):
                    ctx.w(v, "sPE")
                    ctx.inc(v.tensor_mul(b_r, b_r, PS[0:16, 5 * 512:5 * 512 + TC]),
                            "sDVE")
                BV(wBmul)
            AR.pop()
            sc_dA = [AR.alloc(128, CT, F32) for _ in range(2)]
            sc_ub = [AR.alloc(128, CT, F32) for _ in range(2)]
            sc_h = [AR.alloc(128, CT, F32) for _ in range(2)]

            def sdt_red(v, ci=ci):
                v.tensor_reduce(sm[64:65, 428:429],
                                rows1[64:65, ci * CT:(ci + 1) * CT], AX.X, ALU.add)
                ctx.inc(v.engine_nop(), "sDVE")
            BV(sdt_red)

            def sdt_bc(t, ci=ci):
                ctx.w(t, "sDVE")
                t.matmul(bank(3)[:, 0:1], ones3[64:65, 0:128], sm[64:65, 428:429],
                         start=True, stop=True)
                ctx.inc(bcast(t, bank(0, 2),
                              rows1[64:65, ci * CT:(ci + 1) * CT], CT), "sPE")
            block.tensor(sdt_bc)

            def sdt_cp(v, ci=ci):
                ctx.w(v, "sPE")
                v.tensor_copy(sdt_sb[:, ci:ci + 1], bank(3)[:, 0:1])
                ctx.inc(v.tensor_copy(pub[:, ci * 257 + 256:ci * 257 + 257],
                                      bank(3)[:, 0:1]), "sDVE")
            BV(sdt_cp)

            for n in range(16):
                def nb(t, n=n, ci=ci):
                    ctx.w(t, "sDVE")
                    ctx.w(t, "sPOOL")
                    for i in range(2):
                        ins = t.matmul(PS[:, (2 + i) * 512:(3 + i) * 512],
                                       oneh[0:16, n * 128:(n + 1) * 128],
                                       rows1[0:16, ci * CT + i * 512:
                                             ci * CT + (i + 1) * 512],
                                       start=True, stop=True)
                    ctx.inc(ins, "sPE")
                block.tensor(nb)

                for db in range(16):
                    buf = (n * 16 + db) % 2
                    dA, ub, h = sc_dA[buf], sc_ub[buf], sc_h[buf]

                    def mk_dA(a, n=n, db=db, dA=dA):
                        ctx.w(a, "sPE")
                        ctx.w(a, "sDVE")
                        ctx.inc(a.activation(
                            dA[:, 0:CT], bank(0, 2), AF.Exp,
                            scale=A_sb[:, db * 16 + n:db * 16 + n + 1]), "sACT")
                    BS(mk_dA)

                    def mk_ub(v, db=db, ub=ub, xconv=xconv):
                        ctx.w(v, "sPE")
                        ctx.inc(v.tensor_mul(ub[:, 0:CT],
                                             xconv[:, db * CT:(db + 1) * CT],
                                             bank(2, 2)), "sDVE")
                    BV(mk_ub)

                    def do_scan(v, n=n, db=db, dA=dA, ub=ub, h=h, ci=ci):
                        ctx.w(v, "sACT")
                        ctx.w(v, "sPOOL")
                        ctx.w(v, "sDVE")
                        ctx.inc(v.tensor_tensor_scan(h[:, 0:CT], dA[:, 0:CT],
                                                     ub[:, 0:CT], 0.0,
                                                     ALU.mult, ALU.add), "sDVE")
                        ctx.w(v, "sDVE")
                        ctx.inc(v.tensor_copy(
                            pub[:, ci * 257 + db * 16 + n:ci * 257 + db * 16 + n + 1],
                            h[:, CT - 1:CT]), "sDVE")
                    BV(do_scan)

            def pubfence(v):
                ctx.w(v, "sPOOL")
                ctx.w(v, "sDVE")
                ctx.inc(v.engine_nop(), "sDVE")
            BV(pubfence)
            AR.pop()

        if upto in ("p0", "p1"):
            ctx.barrier()

            def out_dma0(s):
                ctx.inc(s.dma_start(out=yfm, in_=res[:, :]), "sOUT", 16)
                s.wait_ge(ctx.sems["sOUT"], ctx.cnt["sOUT"])
            block.sync(out_dma0)
            return nc

        def ag1(g):
            ctx.w(g, "sDVE")
            ctx.inc(g.dma_start(out=ag1_in[:, :], in_=pub[:, :]), "sDG", 16)
            g.wait_ge(ctx.sems["sDG"], ctx.cnt["sDG"])
            ctx.inc(g.collective_compute(
                "AllGather", ALU.bypass, replica_groups=[list(range(8))],
                ins=[ag1_in.ap()], outs=[ag1_out.ap()]), "sCC", 1)
            g.wait_ge(ctx.sems["sCC"], ctx.cnt["sCC"])
        BG(ag1)
        ctx.barrier()

        if upto == "ag":
            def out_dma1(s):
                ctx.inc(s.dma_start(out=yfm, in_=res[:, :]), "sOUT", 16)
                s.wait_ge(ctx.sems["sOUT"], ctx.cnt["sOUT"])
            block.sync(out_dma1)
            return nc

        for ci in range(NCH):
            AR.push()
            xconv3 = AR.alloc(128, 16 * CT, BF16)
            y_acc = AR.alloc(128, DGRP * CT, F32)
            xsc = AR.alloc(128, TC, F32)
            P_t = AR.alloc(128, 256, F32)

            def h0(v):
                ctx.inc(v.memset(hstart[:, :], 0.0), "sDVE")
            BV(h0)
            for s_ in range(7):
                rank = s_ if ci == 0 else 7 - s_

                def pld(g, rank=rank, ci=ci):
                    ctx.w(g, "sDVE")
                    ctx.inc(g.dma_start(
                        out=pred[:, :],
                        in_=ag1_out[rank * 128:(rank + 1) * 128,
                                    ci * 257:(ci + 1) * 257]), "sDG", 16)
                BG(pld)

                def hv1(v):
                    ctx.w(v, "sCC")
                    ctx.w(v, "sDG")
                    ctx.inc(v.tensor_scalar_mul(P_t, A_sb, pred[:, 256:257]),
                            "sDVE")
                BV(hv1)

                def ha(a):
                    ctx.w(a, "sDVE")
                    ctx.inc(a.activation(P_t, P_t, AF.Exp), "sACT")
                BS(ha)

                def hv2(v, s_=s_, ci=ci):
                    ctx.w(v, "sACT")
                    m = agm_sb[:, ci * 7 + s_:ci * 7 + s_ + 1]
                    v.tensor_scalar_add(P_t, P_t, -1.0)
                    v.tensor_scalar_mul(P_t, P_t, m)
                    v.tensor_scalar_add(P_t, P_t, 1.0)
                    v.tensor_mul(hstart[:, :], hstart[:, :], P_t)
                    ctx.inc(v.scalar_tensor_tensor(hstart[:, :], pred[:, 0:256], m,
                                                   hstart[:, :], ALU.mult, ALU.add),
                            "sDVE")
                BV(hv2)

            mu0 = rows2[0:1, ci * CT:(ci + 1) * CT]
            rs0 = rows2[32:33, ci * CT:(ci + 1) * CT]
            mui = rows3[0:1, ci * CT:(ci + 1) * CT]
            rsi = rows3[32:33, ci * CT:(ci + 1) * CT]

            for tc in range(2):
                t0 = tc * TC

                def xb1(t, t0=t0):
                    ctx.w(t, "sDVE")
                    bcast(t, bank(1), mu0[:, t0:t0 + TC], TC)
                    bcast(t, bank(2), rs0[:, t0:t0 + TC], TC)
                    bcast(t, bank(5), mui[:, t0:t0 + TC], TC)
                    ctx.inc(bcast(t, bank(6), rsi[:, t0:t0 + TC], TC), "sPE")
                block.tensor(xb1)

                def xv1(v, t0=t0, ci=ci):
                    ctx.w(v, "sPE")
                    for i in range(8):
                        o = xsc[:, 0:TC]
                        rv = res[:, col(ci, i, t0):col(ci, i, t0) + TC]
                        xno = xnbf[:, i * CT + t0:i * CT + t0 + TC]
                        v.tensor_sub(o, rv, bank(1)[:, 0:TC])
                        v.tensor_mul(o, o, bank(2)[:, 0:TC])
                        v.tensor_add(rv, rv, o)
                        v.tensor_sub(o, o, bank(5)[:, 0:TC])
                        v.tensor_mul(o, o, bank(6)[:, 0:TC])
                        ins = v.tensor_scalar(xno, o, gam_sb[:, i:i + 1],
                                              bet_sb[:, i:i + 1], ALU.mult, ALU.add)
                    ctx.inc(ins, "sDVE")
                BV(xv1)

            AR.push()
            sq3 = AR.alloc(128, 8 * 3, F32)
            lh3 = AR.alloc(128, 8 * 3, F32)
            xi3 = AR.alloc(128, TC + 3, F32)
            st3 = AR.alloc(128, 16 * 3, F32)
            hx3 = AR.alloc(128, 8 * 3, BF16)
            hf3 = AR.alloc(128, 8 * 3, F32)
            ca3 = AR.alloc(128, TC, F32)
            emit_halo(ci, hf3, hx3, st3, sq3, lh3)
            for tc in range(2):
                emit_xinner_conv(ci, tc * TC, xconv3, xi3, st3, ca3)
            AR.pop()

            AR.push()
            y3bf = AR.alloc(128, CT, BF16)
            s3_dA = [AR.alloc(128, CT, F32) for _ in range(2)]
            s3_ub = [AR.alloc(128, CT, F32) for _ in range(2)]
            s3_h = [AR.alloc(128, CT, F32) for _ in range(2)]
            zs_t = s3_dA[0]
            t2_t = s3_dA[1]

            for g0 in range(0, 16, DGRP):
                def dtb(t, ci=ci):
                    ctx.w(t, "sACT")
                    ctx.w(t, "sDVE")
                    ctx.inc(bcast(t, bank(0, 2),
                                  rows1[64:65, ci * CT:(ci + 1) * CT], CT), "sPE")
                block.tensor(dtb)

                for n in range(16):
                    def nb(t, n=n, ci=ci):
                        ctx.w(t, "sDVE")
                        ctx.w(t, "sPOOL")
                        for i in range(2):
                            t.matmul(PS[:, (2 + i) * 512:(3 + i) * 512],
                                     oneh[0:16, n * 128:(n + 1) * 128],
                                     rows1[0:16, ci * CT + i * 512:
                                           ci * CT + (i + 1) * 512],
                                     start=True, stop=True)
                        for i in range(2):
                            ins = t.matmul(PS[:, (4 + i) * 512:(5 + i) * 512],
                                           oneh[32:48, n * 128:(n + 1) * 128],
                                           rows1[32:48, ci * CT + i * 512:
                                                 ci * CT + (i + 1) * 512],
                                           start=True, stop=True)
                        ctx.inc(ins, "sPE")
                    block.tensor(nb)

                    for db in range(g0, g0 + DGRP):
                        buf = (n * DGRP + db) % 2
                        dA, ub, h = s3_dA[buf], s3_ub[buf], s3_h[buf]

                        def mk_dA(a, n=n, db=db, dA=dA):
                            ctx.w(a, "sPE")
                            ctx.w(a, "sDVE")
                            ctx.w(a, "sPOOL")
                            ctx.inc(a.activation(
                                dA[:, 0:CT], bank(0, 2), AF.Exp,
                                scale=A_sb[:, db * 16 + n:db * 16 + n + 1]), "sACT")
                        BS(mk_dA)

                        def mk_ub(v, db=db, ub=ub):
                            ctx.w(v, "sPE")
                            ctx.w(v, "sPOOL")
                            ctx.inc(v.tensor_mul(ub[:, 0:CT],
                                                 xconv3[:, db * CT:(db + 1) * CT],
                                                 bank(2, 2)), "sDVE")
                        BV(mk_ub)

                        def do_scan(v, n=n, db=db, dA=dA, ub=ub, h=h):
                            ctx.w(v, "sACT")
                            ctx.w(v, "sPOOL")
                            ctx.w(v, "sDVE")
                            ctx.inc(v.tensor_tensor_scan(
                                h[:, 0:CT], dA[:, 0:CT], ub[:, 0:CT],
                                hstart[:, db * 16 + n:db * 16 + n + 1],
                                ALU.mult, ALU.add), "sDVE")
                            ctx.w(v, "sDVE")
                            ctx.inc(v.tensor_mul(ub[:, 0:CT], h[:, 0:CT],
                                                 bank(4, 2)), "sDVE")
                        BV(do_scan)

                        def y_add(g_, n=n, db=db, ub=ub, g0=g0):
                            ctx.w(g_, "sDVE")
                            ya = y_acc[:, (db - g0) * CT:(db - g0 + 1) * CT]
                            if n == 0:
                                ins = g_.tensor_copy(ya, ub[:, 0:CT])
                            else:
                                ins = g_.tensor_add(ya, ya, ub[:, 0:CT])
                            ctx.inc(ins, "sPOOL")
                        BG(y_add)

                for db in range(g0, g0 + DGRP):
                    ring = wi_ring[db % 2]

                    def zld(s, db=db, ring=ring):
                        ctx.w(s, "sPE")
                        ctx.inc(s.dma_start(out=ring[:, 0:1024].rearrange("p (m f) -> p m f", m=8),
                                            in_=win_slice(db, zhalf=True)),
                                "sDW", 16)
                    block.sync(zld)

                    def zmm(t, db=db, ring=ring):
                        ctx.w(t, "sDW")
                        ctx.w(t, "sACT")
                        ctx.w(t, "sPOOL")
                        for m in range(8):
                            for half in range(2):
                                ins = t.matmul(
                                    PS[:, half * 512:(half + 1) * 512],
                                    ring[:, m * 128:(m + 1) * 128],
                                    xnbf[:, m * CT + half * TC:
                                         m * CT + (half + 1) * TC],
                                    start=(m == 0), stop=(m == 7))
                        ctx.inc(ins, "sPE")
                    block.tensor(zmm)

                    def zsilu(a):
                        ctx.w(a, "sPE")
                        a.activation(zs_t[:, 0:TC], bank(0), AF.Silu)
                        ctx.inc(a.activation(zs_t[:, TC:CT], bank(1), AF.Silu),
                                "sACT")
                    BS(zsilu)

                    def gate(v, db=db, g0=g0):
                        ctx.w(v, "sACT")
                        ctx.w(v, "sPOOL")
                        ctx.w(v, "sPE")
                        ya = y_acc[:, (db - g0) * CT:(db - g0 + 1) * CT]
                        xc = xconv3[:, db * CT:(db + 1) * CT]
                        v.scalar_tensor_tensor(t2_t[:, 0:CT], xc,
                                               D_sb[:, db:db + 1], ya,
                                               ALU.mult, ALU.add)
                        ctx.inc(v.tensor_mul(y3bf[:, 0:CT], t2_t[:, 0:CT],
                                             zs_t[:, 0:CT]), "sDVE")
                    BV(gate)

                    def wld(s, db=db):
                        ctx.w(s, "sPE")
                        ctx.inc(s.dma_start(
                            out=wo_ring[db % 2][:, 0:1024],
                            in_=woutb[db * 128:(db + 1) * 128, :]), "sDW", 16)
                    block.sync(wld)

                    for om in range(8):
                        def wmm(t, om=om, db=db):
                            ctx.w(t, "sDW")
                            ctx.w(t, "sDVE")
                            b0 = 2 + (om % 2) * 2
                            for half in range(2):
                                ins = t.matmul(
                                    PS[:, (b0 + half) * 512:(b0 + half + 1) * 512],
                                    wo_ring[db % 2][:, om * 128:(om + 1) * 128],
                                    y3bf[:, half * TC:(half + 1) * TC],
                                    start=True, stop=True)
                            ctx.inc(ins, "sPE")
                        block.tensor(wmm)

                        def wadd(v, om=om, ci=ci):
                            ctx.w(v, "sPE")
                            b0 = 2 + (om % 2) * 2
                            rv = res[:, col(ci, om, 0):col(ci, om, 0) + CT]
                            ctx.inc(v.tensor_add(
                                rv, rv, PS[:, b0 * 512:(b0 + 2) * 512]), "sDVE")
                        BV(wadd)
            AR.pop()
            AR.pop()
        ctx.barrier()

        def emit_out():
            def out_dma(s):
                ctx.inc(s.dma_start(out=yfm, in_=res[:, :]), "sOUT", 16)
                s.wait_ge(ctx.sems["sOUT"], ctx.cnt["sOUT"])
            block.sync(out_dma)

        if upto == "mamba":
            emit_out()
            return nc

        def emit_ln1(ci, sqf, osc):
            for tc in range(2):
                t0 = tc * TC
                srcv = [res[:, col(ci, dh, t0):col(ci, dh, t0) + TC]
                        for dh in range(8)]
                mu_r = rows2[0:1, ci * CT + t0:ci * CT + t0 + TC]
                rs_r = rows2[32:33, ci * CT + t0:ci * CT + t0 + TC]

                def s_sq(a, srcv=srcv):
                    ctx.w(a, "sDVE")
                    for i in range(8):
                        a.activation(sqf[:, i * TC:(i + 1) * TC], srcv[i], AF.Square)
                BS(s_sq)

                def s_pe(t, srcv=srcv):
                    ctx.w(t, "sACT")
                    ctx.w(t, "sDVE")
                    for i in range(8):
                        t.matmul(PS[0:1, 0:TC], ones_col, srcv[i],
                                 start=(i == 0), stop=(i == 7))
                    for i in range(8):
                        ins = t.matmul(PS[32:33, 0:TC], ones_col,
                                       sqf[:, i * TC:(i + 1) * TC],
                                       start=(i == 0), stop=(i == 7))
                    ctx.inc(ins, "sPE")
                block.tensor(s_pe)

                def s_rows(v, mu_r=mu_r, rs_r=rs_r):
                    ctx.w(v, "sPE")
                    v.tensor_scalar_mul(mu_r, PS[0:1, 0:TC], 1.0 / D)
                    v.tensor_scalar_mul(rs_r, PS[32:33, 0:TC], 1.0 / D)
                    tmp = PS[32:33, 3 * 512:3 * 512 + TC]
                    v.tensor_mul(tmp, mu_r, mu_r)
                    v.tensor_sub(rs_r, rs_r, tmp)
                BV(s_rows)

                def s_rstd(a, rs_r=rs_r):
                    ctx.w(a, "sDVE")
                    a.activation(rs_r, rs_r, AF.Ln, bias=eps_col[0:1, 0:1])
                    a.activation(rs_r, rs_r, AF.Exp, scale=-0.5)
                BS(s_rstd)

                def s_bc(t, mu_r=mu_r, rs_r=rs_r):
                    ctx.w(t, "sACT")
                    bcast(t, bank(1), mu_r, TC)
                    ctx.inc(bcast(t, bank(2), rs_r, TC), "sPE")
                block.tensor(s_bc)

                def s_fin(v, srcv=srcv, t0=t0, ci=ci):
                    ctx.w(v, "sPE")
                    for i in range(8):
                        v.tensor_sub(osc[:, 0:TC], srcv[i], bank(1)[:, 0:TC])
                        v.tensor_mul(xnbf[:, i * CT + t0:i * CT + t0 + TC],
                                     osc[:, 0:TC], bank(2)[:, 0:TC])
                BV(s_fin)

        gate_sb = AR.alloc(128, 64, BF16)
        gateb = d_bf["gate_w"]
        w1b = d_bf["w1"]
        w2b = d_bf["w2"]

        def p_load_gate(s):
            ctx.inc(s.dma_start(
                out=gate_sb[:, :].rearrange("p (m e) -> p m e", m=8),
                in_=gateb.rearrange("(m p) e -> p m e", p=128)), "sDW", 16)
        block.sync(p_load_gate)

        def emit_ffn(ci):
            AR.push()
            sqf = AR.alloc(128, 8 * TC, F32)
            osc = AR.alloc(128, TC, F32)
            emit_ln1(ci, sqf, osc)
            AR.pop()
            AR.push()
            glt = AR.alloc(128, CT, F32)
            h_e = AR.alloc(128, 32 * CT, BF16)
            w2r = AR.alloc(128, 32 * 128, BF16)

            lg = tiny[:, 0:8]
            m8 = tiny[:, 8:16]
            mv = tiny[:, 16:24]
            pr = tiny[:, 24:32]
            cmb = tiny[:, 32:40]
            ngm = tiny[:, 40:41]
            ssum = tiny[:, 41:42]
            den = tiny[:, 42:43]
            for tt in range(8):
                def g_pe(t, tt=tt, ci=ci):
                    ctx.w(t, "sDVE")
                    ctx.w(t, "sDW")
                    for m in range(8):
                        ins = t.matmul(
                            PS[:, 5 * 512:5 * 512 + 8],
                            xnbf[:, m * CT + tt * 128:m * CT + (tt + 1) * 128],
                            gate_sb[:, m * 8:(m + 1) * 8],
                            start=(m == 0), stop=(m == 7))
                    ctx.inc(ins, "sPE")
                block.tensor(g_pe)

                def g_v1(v):
                    ctx.w(v, "sPE")
                    v.tensor_copy(lg, PS[:, 5 * 512:5 * 512 + 8])
                    v.reduce_max(ngm, lg, AX.X, negate=True)
                    v.tensor_scalar_add(lg, lg, ngm[:, 0:1])
                BV(g_v1)

                def g_a1(a):
                    ctx.w(a, "sDVE")
                    a.activation(lg, lg, AF.Exp)
                BS(g_a1)

                def g_v2(v):
                    ctx.w(v, "sACT")
                    v.reduce_sum(ssum, lg, AX.X)
                    v.reciprocal(ssum, ssum)
                    v.tensor_scalar_mul(lg, lg, ssum[:, 0:1])
                    v.max(out=m8, in_=lg)
                    v.tensor_copy(mv[:, 0:2], m8[:, 0:2])
                    v.tensor_copy(mv[:, 2:8], neg1_sb[:, 2:8])
                    v.match_replace(pr, mv, lg, -1.0)
                    v.tensor_sub(pr, lg, pr)
                    v.tensor_scalar_min(pr, pr, 1.0)
                    v.tensor_add(den, m8[:, 0:1], m8[:, 1:2])
                    v.reciprocal(den, den)
                    v.tensor_mul(cmb, pr, lg)
                    v.tensor_scalar_mul(cmb, cmb, den[:, 0:1])
                BV(g_v2)

                def g_tr(t, tt=tt):
                    ctx.w(t, "sDVE")
                    ctx.inc(t.transpose(PS[0:8, 4 * 512:4 * 512 + 128], cmb,
                                        ident_sb), "sPE")
                block.tensor(g_tr)

                def g_cp(v, tt=tt, ci=ci):
                    ctx.w(v, "sPE")
                    v.tensor_copy(rows1[0:8, ci * CT + tt * 128:
                                        ci * CT + (tt + 1) * 128],
                                  PS[0:8, 4 * 512:4 * 512 + 128])
                BV(g_cp)

            for e in range(E):
                def c_bc(t, e=e, ci=ci):
                    ctx.w(t, "sDVE")
                    for i in range(2):
                        ins = t.matmul(PS[:, (6 + i) * 512:(7 + i) * 512],
                                       oneh[0:16, e * 128:(e + 1) * 128],
                                       rows1[0:16, ci * CT + i * 512:
                                             ci * CT + (i + 1) * 512],
                                       start=True, stop=True)
                    ctx.inc(ins, "sPE")
                block.tensor(c_bc)

                for fi in range(32):
                    def w1ld(s, e=e, fi=fi):
                        ctx.w(s, "sPE")
                        ctx.inc(s.dma_start(
                            out=wi_ring[fi % 2][:, 0:1024].rearrange(
                                "p (m f) -> p m f", m=8),
                            in_=w1b[e * D:(e + 1) * D,
                                    fi * 128:(fi + 1) * 128].rearrange(
                                "(m p) f -> p m f", p=128)), "sDW", 16)
                    block.sync(w1ld)

                    def w1mm(t, e=e, fi=fi):
                        ctx.w(t, "sDW")
                        ctx.w(t, "sACT")
                        for m in range(8):
                            for half in range(2):
                                ins = t.matmul(
                                    PS[:, half * 512:(half + 1) * 512],
                                    wi_ring[fi % 2][:, m * 128:(m + 1) * 128],
                                    xnbf[:, m * CT + half * TC:
                                         m * CT + (half + 1) * TC],
                                    start=(m == 0), stop=(m == 7))
                        ctx.inc(ins, "sPE")
                    block.tensor(w1mm)

                    def w1ge(a, fi=fi):
                        ctx.w(a, "sPE")
                        ctx.w(a, "sDVE")
                        a.activation(glt[:, 0:TC], bank(0), AF.Gelu)
                        a.activation(glt[:, TC:CT], bank(1), AF.Gelu)
                    BS(w1ge)

                    def w1cb(v, fi=fi):
                        ctx.w(v, "sACT")
                        ctx.w(v, "sPE")
                        v.tensor_mul(h_e[:, fi * CT:(fi + 1) * CT],
                                     glt[:, 0:CT], bank(6, 2))
                    BV(w1cb)

                for om in range(8):
                    def w2ld(s, e=e, om=om):
                        ctx.w(s, "sPE")
                        ctx.inc(s.dma_start(
                            out=w2r[:, :].rearrange("p (fh o) -> p fh o", fh=32),
                            in_=w2b[e * FF:(e + 1) * FF,
                                    om * 128:(om + 1) * 128].rearrange(
                                "(fh p) o -> p fh o", p=128)), "sDW", 16)
                    block.sync(w2ld)

                    def w2mm(t, om=om):
                        ctx.w(t, "sDW")
                        ctx.w(t, "sDVE")
                        for fh in range(32):
                            for half in range(2):
                                ins = t.matmul(
                                    PS[:, (2 + half) * 512:(3 + half) * 512],
                                    w2r[:, fh * 128:(fh + 1) * 128],
                                    h_e[:, fh * CT + half * TC:
                                        fh * CT + (half + 1) * TC],
                                    start=(fh == 0), stop=(fh == 31))
                        ctx.inc(ins, "sPE")
                    block.tensor(w2mm)

                    def w2add(v, om=om, ci=ci):
                        ctx.w(v, "sPE")
                        rv = res[:, col(ci, om, 0):col(ci, om, 0) + CT]
                        v.tensor_add(rv, rv, bank(2, 2))
                    BV(w2add)
            AR.pop()

        for ci in range(NCH):
            emit_ffn(ci)
        ctx.barrier()
        if upto == "ffn1":
            emit_out()
            return nc

        kvdb = d_bf["w_kvd"]
        kupb_d = d_bf["w_kup"]
        vupb_d = d_bf["w_vup"]
        wqb_d = d_bf["w_q"]
        wob_d = d_bf["w_o"]
        wqrb_d = d_bf["wqr_ext"]
        wkrb_d = d_bf["wkr_ext"]

        AR.push()
        sqf = AR.alloc(128, 8 * TC, F32)
        osc = AR.alloc(128, TC, F32)
        kvd_sb = AR.alloc(128, 1024, BF16)
        ckv_own = AR.alloc(128, NCH * CT, BF16)

        def kvd_ld(s):
            ctx.inc(s.dma_start(
                out=kvd_sb[:, :].rearrange("p (m f) -> p m f", m=8),
                in_=kvdb.rearrange("(m p) f -> p m f", p=128)), "sDW", 16)
        block.sync(kvd_ld)

        for ci in range(NCH):
            emit_ln1(ci, sqf, osc)

            def ckv_mm(t, ci=ci):
                ctx.w(t, "sDW")
                ctx.w(t, "sDVE")
                for m in range(8):
                    for half in range(2):
                        ins = t.matmul(PS[:, half * 512:(half + 1) * 512],
                                       kvd_sb[:, m * 128:(m + 1) * 128],
                                       xnbf[:, m * CT + half * TC:
                                            m * CT + (half + 1) * TC],
                                       start=(m == 0), stop=(m == 7))
                ctx.inc(ins, "sPE")
            block.tensor(ckv_mm)

            def ckv_ev(v, ci=ci):
                ctx.w(v, "sPE")
                v.tensor_copy(ckv_own[:, ci * CT:(ci + 1) * CT], bank(0, 2))
            BV(ckv_ev)

            def ckv_st(s, ci=ci):
                ctx.w(s, "sDVE")
                ctx.inc(s.dma_start(out=ag2_in[:, ci * CT:(ci + 1) * CT],
                                    in_=ckv_own[:, ci * CT:(ci + 1) * CT]),
                        "sDX", 16)
            block.sync(ckv_st)

        def ag2(g):
            ctx.w(g, "sDX")
            ctx.inc(g.collective_compute(
                "AllGather", ALU.bypass, replica_groups=[list(range(8))],
                ins=[ag2_in.ap()], outs=[ag2_out.ap()]), "sCC", 1)
            g.wait_ge(ctx.sems["sCC"], ctx.cnt["sCC"])
        block.gpsimd(ag2)
        AR.pop()
        ctx.barrier()

        AR.push()
        kupb = AR.alloc(128, 1024, BF16)
        vupb = AR.alloc(128, 1024, BF16)
        wkrb = AR.alloc(128, 64, BF16)
        ckv_own2 = AR.alloc(128, NCH * CT, BF16)
        k97 = AR.alloc(128, T, BF16)
        ckv_kv = AR.alloc(128, T, BF16)
        ropesK_b = [AR.alloc(64, TC, BF16) for _ in range(2)]
        ropeown = AR.alloc(128, CT, BF16)
        masks_sb = AR.alloc(128, 2048, BF16)
        v_tok = AR.alloc(128, 64 * 65, BF16)
        q97 = AR.alloc(128, CT, BF16)
        kdiag = AR.alloc(128, CT, BF16)
        PT = [AR.alloc(128, CT, BF16) for _ in range(2)]
        vdiag = AR.alloc(128, 8 * 65, BF16)
        attnp = AR.alloc(128, CT, BF16)
        attn_tmp = AR.alloc(64, TC, F32)
        qsq = AR.alloc(128, TC, F32)
        T1 = AR.alloc(32, TC, F32)
        T2 = AR.alloc(32, TC, F32)
        wq_r = wring_t.ap()[:, 0:512]
        wqrb = wring_t.ap()[:, 512:1024]
        kvd_sb2 = wring_t.ap()[:, 1024:2048]

        def attn_ld0(s):
            ctx.w(s, "sPE")
            ctx.w(s, "sDVE")
            ctx.w(s, "sACT")
            ctx.inc(s.dma_start(out=kupb[:, :], in_=kupb_d[:, :]), "sDW", 16)
            ctx.inc(s.dma_start(out=vupb[:, :], in_=vupb_d[:, :]), "sDW", 16)
            ctx.inc(s.dma_start(out=wkrb[:, :], in_=wkrb_d[:, :]), "sDW", 16)
            ctx.inc(s.dma_start(
                out=wqrb.rearrange("p (m f) -> p m f", m=8),
                in_=wqrb_d.rearrange("(m p) f -> p m f", p=128)), "sDW", 16)
            ctx.inc(s.dma_start(out=masks_sb[:, :], in_=d_in["masksbf"]), "sDW", 16)
            ctx.inc(s.dma_start(
                out=kvd_sb2.rearrange("p (m f) -> p m f", m=8),
                in_=kvdb.rearrange("(m p) f -> p m f", p=128)), "sDW", 16)
        block.sync(attn_ld0)

        sqa = k97.bitcast(F32)
        osca = ckv_kv[:, 0:1024].bitcast(F32)

        for ci in range(NCH):
            emit_ln1(ci, sqa, osca)

            def ckv_mm2(t, ci=ci):
                ctx.w(t, "sDW")
                ctx.w(t, "sDVE")
                for m in range(8):
                    for half in range(2):
                        ins = t.matmul(PS[:, half * 512:(half + 1) * 512],
                                       kvd_sb2[:, m * 128:(m + 1) * 128],
                                       xnbf[:, m * CT + half * TC:
                                            m * CT + (half + 1) * TC],
                                       start=(m == 0), stop=(m == 7))
                ctx.inc(ins, "sPE")
            block.tensor(ckv_mm2)

            def ckv_ev2(v, ci=ci):
                ctx.w(v, "sPE")
                v.tensor_copy(ckv_own2[:, ci * CT:(ci + 1) * CT], bank(0, 2))
            BV(ckv_ev2)

        for ci in range(NCH):
            emit_ln1(ci, sqa, osca)

            def kv_loads(g, ci=ci):
                ctx.w(g, "sACT")
                ctx.w(g, "sPE")
                ctx.w(g, "sDVE")
                for kb in range(8):
                    rank = kb if ci == 0 else 7 - kb
                    ctx.inc(g.dma_start(
                        out=ckv_kv[:, kb * CT:(kb + 1) * CT],
                        in_=ag2_out[rank * 128:(rank + 1) * 128,
                                    ci * CT:(ci + 1) * CT]), "sDG", 16)
            block.gpsimd(kv_loads)

            def ro_ld(s, ci=ci):
                ctx.w(s, "sPE")
                ctx.w(s, "sACT")
                ctx.inc(s.dma_start(
                    out=ropeown[:, :],
                    in_=d_in["ropesbf_own"][:, ci * CT:(ci + 1) * CT]),
                    "sDX", 16)
            block.sync(ro_ld)

            def kq_init(v, ci=ci):
                ctx.w(v, "sDG")
                ctx.w(v, "sDX")
                ctx.w(v, "sPE")
                v.memset(k97[0:32, :], 0.0)
                v.memset(q97[0:32, :], 0.0)
                v.memset(kdiag[0:32, :], 0.0)
                v.memset(kdiag[0:1, :], 1.0)
                v.memset(v_tok[:, :], 1.0)
                v.memset(vdiag[:, :], 1.0)
            BV(kq_init)

            def aug_ld(g, ci=ci):
                ctx.w(g, "sDVE")
                ctx.inc(g.dma_start(out=k97[0:4, :],
                                    in_=d_in["kaug"][ci * 4:(ci + 1) * 4, :]),
                        "sDG", 16)
                ctx.inc(g.dma_start(out=q97[0:4, :],
                                    in_=d_in["qaug"][:, :]), "sDG", 16)
            block.gpsimd(aug_ld)

            for s_ in range(16):
                def kr_ld(s, s_=s_):
                    ctx.w(s, "sDVE")
                    ctx.inc(s.dma_start(
                        out=ropesK_b[s_ % 2][:, :],
                        in_=d_in["ropesbf"][64:128, s_ * 512:(s_ + 1) * 512]),
                        "sDX", 16)
                block.sync(kr_ld)

                def kr_mm(t, s_=s_):
                    ctx.w(t, "sDVE")
                    ctx.w(t, "sDG")
                    ctx.inc(t.matmul(
                        PS[0:64, (s_ % 2) * 1024:(s_ % 2) * 1024 + 512],
                        wkrb[:, 0:64],
                        ckv_kv[:, s_ * 512:(s_ + 1) * 512],
                        start=True, stop=True), "sPE")
                block.tensor(kr_mm)

                def kr_rot(v, s_=s_):
                    ctx.w(v, "sPE")
                    ctx.w(v, "sDX")
                    b0 = (s_ % 2) * 1024
                    v.tensor_mul(T1[:, 0:TC], PS[0:32, b0:b0 + 512],
                                 ropesK_b[s_ % 2][0:32, :])
                    v.tensor_mul(T2[:, 0:TC], PS[32:64, b0:b0 + 512],
                                 ropesK_b[s_ % 2][32:64, :])
                    v.tensor_add(k97[32:64, s_ * 512:(s_ + 1) * 512],
                                 T1[:, 0:TC], T2[:, 0:TC])
                BV(kr_rot)

            for half in range(2):
                def kdr_mm(t, half=half, ci=ci):
                    ctx.w(t, "sDVE")
                    ctx.inc(t.matmul(
                        PS[0:64, half * 1024:half * 1024 + 512],
                        wkrb[:, 0:64],
                        ckv_own2[:, ci * CT + half * TC:
                                 ci * CT + (half + 1) * TC],
                        start=True, stop=True), "sPE")
                block.tensor(kdr_mm)

                def kdr_rot(v, half=half):
                    ctx.w(v, "sPE")
                    b0 = half * 1024
                    v.tensor_mul(T1[:, 0:TC], PS[0:32, b0:b0 + 512],
                                 ropeown[64:96, half * TC:(half + 1) * TC])
                    v.tensor_mul(T2[:, 0:TC], PS[32:64, b0:b0 + 512],
                                 ropeown[96:128, half * TC:(half + 1) * TC])
                    v.tensor_add(kdiag[32:64, half * TC:(half + 1) * TC],
                                 T1[:, 0:TC], T2[:, 0:TC])
                BV(kdr_rot)

            for half in range(2):
                def qr_mm(t, half=half):
                    ctx.w(t, "sDVE")
                    ctx.w(t, "sDW")
                    for m in range(8):
                        ins = t.matmul(
                            PS[0:64, half * 1024:half * 1024 + 512],
                            wqrb[:, m * 64:(m + 1) * 64],
                            xnbf[:, m * CT + half * TC:
                                 m * CT + (half + 1) * TC],
                            start=(m == 0), stop=(m == 7))
                    ctx.inc(ins, "sPE")
                block.tensor(qr_mm)

                def qr_rot(v, half=half):
                    ctx.w(v, "sPE")
                    b0 = half * 1024
                    v.tensor_mul(T1[:, 0:TC], PS[0:32, b0:b0 + 512],
                                 ropeown[0:32, half * TC:(half + 1) * TC])
                    v.tensor_mul(T2[:, 0:TC], PS[32:64, b0:b0 + 512],
                                 ropeown[32:64, half * TC:(half + 1) * TC])
                    v.tensor_add(q97[32:64, half * TC:(half + 1) * TC],
                                 T1[:, 0:TC], T2[:, 0:TC])
                BV(qr_rot)

            q2r = rows2[0:1, 0:1024]
            kn2 = rows2[0:1, 1024:2048]
            kmaxc = rows2[0:1, 2047:2048]

            for h in range(NH):
                def q_ld(s, h=h):
                    ctx.w(s, "sPE")
                    ctx.inc(s.dma_start(
                        out=wq_r.rearrange("p (m f) -> p m f", m=8),
                        in_=wqb_d[:, h * 64:(h + 1) * 64].rearrange(
                            "(m p) f -> p m f", p=128)), "sDW", 16)
                block.sync(q_ld)

                def q_mm(t, h=h):
                    ctx.w(t, "sDW")
                    ctx.w(t, "sACT")
                    ctx.w(t, "sDVE")
                    for m in range(8):
                        for half in range(2):
                            ins = t.matmul(
                                PS[64:128, half * 512:half * 512 + 512],
                                wq_r[:, m * 64:(m + 1) * 64],
                                xnbf[:, m * CT + half * TC:
                                     m * CT + (half + 1) * TC],
                                start=(m == 0), stop=(m == 7))
                    ctx.inc(ins, "sPE")
                block.tensor(q_mm)

                def q_ev(a, h=h):
                    ctx.w(a, "sPE")
                    a.activation(q97[64:128, :], PS[64:128, 0:1024], AF.Copy,
                                 scale=float(S96))
                BS(q_ev)

                for half in range(2):
                    def q_sq(a, half=half):
                        ctx.w(a, "sDVE")
                        ctx.w(a, "sPE")
                        if half == 0:
                            ctx.w(a, "sACT")
                        a.activation(qsq[64:128, :],
                                     q97[64:128, half * TC:(half + 1) * TC],
                                     AF.Square)
                        a.activation(qsq[32:64, :],
                                     q97[32:64, half * TC:(half + 1) * TC],
                                     AF.Square)
                    BS(q_sq)

                    def q_n(t, half=half):
                        ctx.w(t, "sACT")
                        t.matmul(PS[0:1, 4 * 512 + half * 512:
                                     4 * 512 + half * 512 + 512],
                                 ones_col[64:128, 0:1], qsq[64:128, :],
                                 start=True, stop=False)
                        ins = t.matmul(PS[0:1, 4 * 512 + half * 512:
                                          4 * 512 + half * 512 + 512],
                                       ones_col[32:64, 0:1], qsq[32:64, :],
                                       start=False, stop=True)
                        ctx.inc(ins, "sPE")
                    block.tensor(q_n)

                def q_n2(a, h=h):
                    ctx.w(a, "sPE")
                    a.activation(q2r, PS[0:1, 4 * 512:4 * 512 + 1024], AF.Sqrt)
                BS(q_n2)

                for s_ in range(16):
                    def k_mm(t, h=h, s_=s_):
                        ctx.w(t, "sACT")
                        ctx.w(t, "sDVE")
                        ctx.inc(t.matmul(
                            PS[64:128, (s_ % 2) * 512:(s_ % 2) * 512 + 512],
                            kupb[:, h * 64:(h + 1) * 64],
                            ckv_kv[:, s_ * 512:(s_ + 1) * 512],
                            start=True, stop=True), "sPE")
                    block.tensor(k_mm)

                    def k_ev(a, s_=s_):
                        ctx.w(a, "sPE")
                        a.activation(k97[64:128, s_ * 512:(s_ + 1) * 512],
                                     PS[64:128, (s_ % 2) * 512:
                                        (s_ % 2) * 512 + 512], AF.Copy)
                    BS(k_ev)

                    def k_sq(a, s_=s_):
                        ctx.w(a, "sDVE")
                        ctx.w(a, "sPE")
                        a.activation(qsq[64:128, :],
                                     k97[64:128, s_ * 512:(s_ + 1) * 512],
                                     AF.Square)
                        a.activation(qsq[32:64, :],
                                     k97[32:64, s_ * 512:(s_ + 1) * 512],
                                     AF.Square)
                    BS(k_sq)

                    def kn_mm(t, s_=s_):
                        ctx.w(t, "sACT")
                        ctx.w(t, "sDVE")
                        t.matmul(PS[0:1, 4 * 512:4 * 512 + 512],
                                 ones_col[64:128, 0:1], qsq[64:128, :],
                                 start=True, stop=False)
                        ins = t.matmul(PS[0:1, 4 * 512:4 * 512 + 512],
                                       ones_col[32:64, 0:1], qsq[32:64, :],
                                       start=False, stop=True)
                        ctx.inc(ins, "sPE")
                    block.tensor(kn_mm)

                    def kn_max(v, s_=s_):
                        ctx.w(v, "sPE")
                        if s_ == 0:
                            v.tensor_copy(kn2[:, 0:512],
                                          PS[0:1, 4 * 512:4 * 512 + 512])
                        else:
                            v.tensor_max(kn2[:, 0:512], kn2[:, 0:512],
                                         PS[0:1, 4 * 512:4 * 512 + 512])
                    BV(kn_max)

                def kmax_fin(v):
                    v.reduce_max(kmaxc, kn2[:, 0:512], AX.X)
                BV(kmax_fin)

                def kmax_sq(a):
                    ctx.w(a, "sDVE")
                    a.activation(kmaxc, kmaxc, AF.Sqrt)
                BS(kmax_sq)

                def mrow(v, h=h):
                    ctx.w(v, "sACT")
                    v.tensor_scalar_mul(kmaxc, kmaxc, -1.0)
                    v.tensor_scalar_mul(q2r, q2r, kmaxc[0:1, 0:1])
                    v.tensor_copy(q97[0:1, :], q2r)
                BV(mrow)

                def kd_mm(t, h=h, ci=ci):
                    ctx.w(t, "sDVE")
                    ctx.w(t, "sACT")
                    for half in range(2):
                        ins = t.matmul(
                            PS[64:128, half * 512:half * 512 + 512],
                            kupb[:, h * 64:(h + 1) * 64],
                            ckv_own2[:, ci * CT + half * TC:
                                     ci * CT + (half + 1) * TC],
                            start=True, stop=True)
                    ctx.inc(ins, "sPE")
                block.tensor(kd_mm)

                def kd_ev(a, h=h):
                    ctx.w(a, "sPE")
                    a.activation(kdiag[64:128, :], PS[64:128, 0:1024], AF.Copy)
                BS(kd_ev)

                for tb4 in range(2):
                    def vd_mm(t, h=h, ci=ci, tb4=tb4):
                        ctx.w(t, "sDVE")
                        ctx.w(t, "sACT")
                        for j in range(4):
                            tb = tb4 * 4 + j
                            ins = t.matmul(PS[:, 5 * 512 + j * 64:
                                              5 * 512 + (j + 1) * 64],
                                           ckv_own2[:, ci * CT + tb * 128:
                                                    ci * CT + (tb + 1) * 128],
                                           vupb[:, h * 64:(h + 1) * 64],
                                           start=True, stop=True)
                        ctx.inc(ins, "sPE")
                    block.tensor(vd_mm)

                    def vd_ev(v, tb4=tb4):
                        ctx.w(v, "sPE")
                        v.tensor_copy(
                            vdiag[:, tb4 * 4 * 65:tb4 * 4 * 65 + 260].rearrange(
                                "p (b c) -> p b c", b=4)[:, :, 0:64],
                            PS[:, 5 * 512:5 * 512 + 256].rearrange(
                                "p (b c) -> p b c", b=4))
                    BV(vd_ev)

                for tb4 in range(16):
                    def v_mm(t, tb4=tb4, h=h):
                        ctx.w(t, "sDVE")
                        for j in range(4):
                            tb = tb4 * 4 + j
                            ins = t.matmul(PS[:, 5 * 512 + j * 64:
                                              5 * 512 + (j + 1) * 64],
                                           ckv_kv[:, tb * 128:(tb + 1) * 128],
                                           vupb[:, h * 64:(h + 1) * 64],
                                           start=True, stop=True)
                        ctx.inc(ins, "sPE")
                    block.tensor(v_mm)

                    def v_ev(v, tb4=tb4):
                        ctx.w(v, "sPE")
                        v.tensor_copy(
                            v_tok[:, tb4 * 4 * 65:tb4 * 4 * 65 + 260].rearrange(
                                "p (b c) -> p b c", b=4)[:, :, 0:64],
                            PS[:, 5 * 512:5 * 512 + 256].rearrange(
                                "p (b c) -> p b c", b=4))
                    BV(v_ev)

                for kvb in range(64):
                    def s_mm(t, kvb=kvb):
                        ctx.w(t, "sACT")
                        if kvb == 0:
                            ctx.w(t, "sDVE")
                            ctx.w(t, "sDG")
                        for qt in range(2):
                            ins = t.matmul(PS[:, qt * 512:(qt + 1) * 512],
                                           k97[:, kvb * 128:(kvb + 1) * 128],
                                           q97[:, qt * 512:(qt + 1) * 512],
                                           start=True, stop=True)
                        ctx.inc(ins, "sPE")
                    block.tensor(s_mm)

                    def s_exp(a, kvb=kvb):
                        ctx.w(a, "sPE")
                        a.activation(PT[kvb % 2][:, :], PS[:, 0:1024], AF.Exp)
                    BS(s_exp)

                    def o_mm(t, kvb=kvb):
                        ctx.w(t, "sACT")
                        for qt in range(2):
                            ins = t.matmul(
                                PS[0:65, (6 + qt) * 512:(6 + qt) * 512 + 512],
                                v_tok[:, kvb * 65:(kvb + 1) * 65],
                                PT[kvb % 2][:, qt * 512:(qt + 1) * 512],
                                start=(kvb == 0), stop=False)
                        ctx.inc(ins, "sPE")
                    block.tensor(o_mm)

                for qt in range(2):
                    for b4 in range(4):
                        def d_mm(t, qt=qt, b4=b4):
                            ctx.w(t, "sACT")
                            ctx.w(t, "sDVE")
                            ctx.inc(t.matmul(PS[:, 0:512],
                                             kdiag[:, qt * 512 + b4 * 128:
                                                   qt * 512 + (b4 + 1) * 128],
                                             q97[:, qt * 512:(qt + 1) * 512],
                                             start=True, stop=True), "sPE")
                        block.tensor(d_mm)

                        def d_msk(v, b4=b4):
                            ctx.w(v, "sPE")
                            v.tensor_add(PS[:, 0:512], PS[:, 0:512],
                                         masks_sb[:, b4 * 512:(b4 + 1) * 512])
                        BV(d_msk)

                        def d_exp(a, qt=qt, b4=b4):
                            ctx.w(a, "sDVE")
                            a.activation(PT[b4 % 2][:, 0:512], PS[:, 0:512],
                                         AF.Exp)
                        BS(d_exp)

                        def d_omm(t, qt=qt, b4=b4):
                            ctx.w(t, "sACT")
                            kb = qt * 4 + b4
                            ctx.inc(t.matmul(
                                PS[0:65, (6 + qt) * 512:(6 + qt) * 512 + 512],
                                vdiag[:, kb * 65:(kb + 1) * 65],
                                PT[b4 % 2][:, 0:512],
                                start=False, stop=(b4 == 3)), "sPE")
                        block.tensor(d_omm)

                for qt in range(2):
                    rno = rows2[64:65, qt * 512:(qt + 1) * 512]

                    def n_rc(v, qt=qt):
                        ctx.w(v, "sPE")
                        v.reciprocal(rno, PS[64:65, (6 + qt) * 512:
                                             (6 + qt) * 512 + 512])
                    BV(n_rc)

                    def n_bc(t, qt=qt):
                        ctx.w(t, "sDVE")
                        ctx.inc(t.matmul(PS[0:64, 2 * 512:2 * 512 + 512],
                                         ones3[64:65, 0:64], rno,
                                         start=True, stop=True), "sPE")
                    block.tensor(n_bc)

                    def n_ev(a, qt=qt):
                        ctx.w(a, "sPE")
                        a.activation(attn_tmp[:, 0:512],
                                     PS[0:64, (6 + qt) * 512:(6 + qt) * 512 + 512],
                                     AF.Copy)
                    BS(n_ev)

                    def n_mul(v, qt=qt, h=h):
                        ctx.w(v, "sACT")
                        ctx.w(v, "sPE")
                        v.tensor_mul(
                            attnp[(h % 2) * 64:(h % 2) * 64 + 64,
                                  qt * 512:(qt + 1) * 512],
                            attn_tmp[:, 0:512], PS[0:64, 2 * 512:2 * 512 + 512])
                    BV(n_mul)

                if h % 2 == 1:
                    hp = h // 2

                    def wo_ld(s, hp=hp):
                        ctx.w(s, "sPE")
                        ctx.w(s, "sDVE")
                        ctx.inc(s.dma_start(
                            out=wo_ring[hp % 2][:, 0:1024],
                            in_=wob_d[hp * 128:(hp + 1) * 128, :]), "sDW", 16)
                    block.sync(wo_ld)

                    for om in range(8):
                        def wo_mm(t, om=om, hp=hp):
                            ctx.w(t, "sDW")
                            ctx.w(t, "sDVE")
                            for half in range(2):
                                ins = t.matmul(
                                    PS[:, (2 + half) * 512:(3 + half) * 512],
                                    wo_ring[hp % 2][:, om * 128:(om + 1) * 128],
                                    attnp[:, half * TC:(half + 1) * TC],
                                    start=True, stop=True)
                            ctx.inc(ins, "sPE")
                        block.tensor(wo_mm)

                        def wo_add(v, om=om, ci=ci):
                            ctx.w(v, "sPE")
                            rv = res[:, col(ci, om, 0):col(ci, om, 0) + CT]
                            v.tensor_add(rv, rv, bank(2, 2))
                        BV(wo_add)
        AR.pop()
        ctx.barrier()
        if upto == "attn":
            emit_out()
            return nc

        for ci in range(NCH):
            emit_ffn(ci)
        ctx.barrier()

        emit_out()
    return nc


def kernel(**inputs):
    per_core = make_host_inputs(inputs)
    nc = build_nc()
    r = run_bass_kernel_spmd(nc, per_core, core_ids=list(range(8)))
    return assemble_output(r.results)


# version 02
# speedup vs baseline: 1.2719x; 1.2719x over previous
import numpy as np
import ml_dtypes
import concourse.bass as bass
import concourse.mybir as mybir
from concourse.bass_utils import run_bass_kernel_spmd

F32 = mybir.dt.float32
BF16 = mybir.dt.bfloat16
AF = mybir.ActivationFunctionType
ALU = mybir.AluOpType
AX = mybir.AxisListType

D = 1024
T = 8192
B = 2
DI = 2048
NST = 16
E = 8
FF = 4096
NH = 16
HD = 64
DR = 32
DC = 128
CT = 1024
NCH = 2
TCORE = NCH * CT
TC = 512
DGRP = 4
LN_EPS = 1e-5
MASKVAL = -30000.0
S96 = float(1.0 / np.sqrt(np.float64(HD + DR)))

UPTO = "all"


def _chunks_of_core(k):
    return [(0, k), (1, 7 - k)]


def _fm(xslice):
    t, d = xslice.shape
    return np.ascontiguousarray(
        xslice.reshape(t, d // 128, 128).transpose(2, 1, 0).reshape(128, -1))


def make_host_inputs(inputs):
    x = np.asarray(inputs["L0_attn_a_x"], np.float32)
    inv_freq = (1.0 / 10000.0 ** (np.arange(0, DR, 2, dtype=np.float32) / DR)).astype(np.float32)
    fr = np.outer(np.arange(T, dtype=np.float32), inv_freq)
    cos = np.cos(fr.astype(np.float64)).astype(np.float32).T
    sin = np.sin(fr.astype(np.float64)).astype(np.float32).T
    ropes = np.zeros((128, T), np.float32)
    ropes[0:16] = cos * np.float32(S96)
    ropes[16:32] = cos * np.float32(S96)
    ropes[32:48] = -sin * np.float32(S96)
    ropes[48:64] = sin * np.float32(S96)
    ropes[64:80] = cos
    ropes[80:96] = cos
    ropes[96:112] = -sin
    ropes[112:128] = sin
    masks = np.zeros((128, 4 * 512), np.float32)
    ii = np.arange(128)[:, None]
    jj = np.arange(512)[None, :]
    for di, delta in enumerate((0, 128, 256, 384)):
        masks[:, di * 512:(di + 1) * 512] = np.where(ii + delta > jj, MASKVAL, 0.0)
    onehots = np.zeros((48, 16 * 128), np.float32)
    for n in range(16):
        onehots[n, n * 128:(n + 1) * 128] = 1.0
        onehots[32 + n, n * 128:(n + 1) * 128] = 1.0
    conv_w = np.asarray(inputs["L0_attn_a_conv_w"], np.float32)
    conv_w_pd = np.ascontiguousarray(
        conv_w[:, 0, :].reshape(16, 128, 4).transpose(1, 0, 2).reshape(128, 64))
    conv_b_pd = np.ascontiguousarray(
        np.asarray(inputs["L0_attn_a_conv_b"], np.float32).reshape(16, 128).T)
    A_pd = np.ascontiguousarray(
        np.asarray(inputs["L0_attn_a_A_log"], np.float32).reshape(16, 128, 16)
        .transpose(1, 0, 2).reshape(128, 256))
    D_pd = np.ascontiguousarray(
        np.asarray(inputs["L0_attn_a_D_skip"], np.float32).reshape(16, 128).T)
    gamma_pd = np.ascontiguousarray(
        np.asarray(inputs["L0_attn_a_gamma"], np.float32).reshape(8, 128).T)
    beta_pd = np.ascontiguousarray(
        np.asarray(inputs["L0_attn_a_beta"], np.float32).reshape(8, 128).T)
    wx = np.asarray(inputs["L0_attn_a_W_x"], np.float32)
    wx_ext = np.zeros((DI, 65), np.float32)
    wx_ext[:, 0:16] = wx[:, 0:16]
    wx_ext[:, 32:48] = wx[:, 16:32]
    wx_ext[:, 64] = wx[:, 32]
    wqr = np.asarray(inputs["L0_attn_b_W_q_rope"], np.float32)
    wqr_ext = np.ascontiguousarray(np.concatenate(
        [wqr[:, 0:16], wqr[:, 16:32], wqr[:, 16:32], wqr[:, 0:16]], 1))
    wkr = np.asarray(inputs["L0_attn_b_W_k_rope"], np.float32)
    wkr_ext = np.ascontiguousarray(np.concatenate(
        [wkr[:, 0:16], wkr[:, 16:32], wkr[:, 16:32], wkr[:, 0:16]], 1))

    shared = {
        "masksbf": masks.astype(ml_dtypes.bfloat16),
        "ropesbf": ropes.astype(ml_dtypes.bfloat16),
        "onehots": onehots,
        "conv_w_pd": conv_w_pd, "conv_b_pd": conv_b_pd,
        "A_pd": A_pd, "D_pd": D_pd, "gamma_pd": gamma_pd, "beta_pd": beta_pd,
        "wx_ext": wx_ext, "wqr_ext": wqr_ext, "wkr_ext": wkr_ext,
        "neg1": np.full((128, 8), -1.0, np.float32),
        "ident": np.eye(128, dtype=np.float32),
        "w_in": np.asarray(inputs["L0_attn_a_W_in"], np.float32),
        "w_out": np.asarray(inputs["L0_attn_a_W_out"], np.float32),
        "gate_w": np.asarray(inputs["L0_ffn_gate_w"], np.float32),
        "w1": np.asarray(inputs["L0_ffn_w1"], np.float32).reshape(E * D, FF),
        "w2": np.asarray(inputs["L0_ffn_w2"], np.float32).reshape(E * FF, D),
        "w_kvd": np.asarray(inputs["L0_attn_b_W_kv_down"], np.float32),
        "w_kup": np.asarray(inputs["L0_attn_b_W_k_up"], np.float32),
        "w_vup": np.asarray(inputs["L0_attn_b_W_v_up"], np.float32),
        "w_q": np.asarray(inputs["L0_attn_b_W_q"], np.float32),
        "w_o": np.asarray(inputs["L0_attn_b_W_o"], np.float32),
    }
    qaug = np.zeros((4, 1024), np.float32)
    qaug[1, 0:512] = 1.0
    qaug[2, 512:1024] = 1.0
    shared["qaug"] = qaug
    per_core = []
    for k in range(8):
        m = dict(shared)
        xfm = np.empty((128, 8, NCH, CT), np.float32)
        halo = np.zeros((128, NCH, 8, 3), np.float32)
        hmask = np.zeros((128, NCH), np.float32)
        agmask = np.zeros((128, NCH * 7), np.float32)
        for ci, (b, p) in enumerate(_chunks_of_core(k)):
            f = _fm(x[b, p * CT:(p + 1) * CT, :]).reshape(128, 8, CT)
            xfm[:, :, ci, :] = f
            if p > 0:
                halo[:, ci] = _fm(x[b, p * CT - 3:p * CT, :]).reshape(128, 8, 3)
                hmask[:, ci] = 1.0
            for s in range(7):
                if s < p:
                    agmask[:, ci * 7 + s] = 1.0
        kaug = np.zeros((NCH * 4, T), np.float32)
        ropes_own = np.zeros((128, NCH * CT), np.float32)
        for ci, (b, p) in enumerate(_chunks_of_core(k)):
            kaug[ci * 4 + 0, :] = 1.0
            kaug[ci * 4 + 1, p * CT:] = MASKVAL
            kaug[ci * 4 + 2, p * CT + 512:] = MASKVAL
            sl = slice(p * CT, (p + 1) * CT)
            ropes_own[0:64, ci * CT:(ci + 1) * CT] = ropes[0:64, sl]
            ropes_own[64:128, ci * CT:(ci + 1) * CT] = ropes[64:128, sl]
        m["kaug"] = kaug
        m["ropesbf_own"] = ropes_own.astype(ml_dtypes.bfloat16)
        m["xfm"] = np.ascontiguousarray(xfm.reshape(128, -1))
        m["halo"] = np.ascontiguousarray(halo.reshape(128, -1))
        m["hmask"] = hmask
        m["agmask"] = agmask
        per_core.append(m)
    return per_core


def assemble_output(results):
    out = np.empty((B, T, D), np.float32)
    for k in range(8):
        y = results[k]["yfm"].reshape(128, 8, NCH, CT)
        for ci, (b, p) in enumerate(_chunks_of_core(k)):
            out[b, p * CT:(p + 1) * CT, :] = y[:, :, ci].transpose(2, 1, 0).reshape(CT, D)
    return out


def col(ci, g, t0):
    return g * (NCH * CT) + ci * CT + t0



_COMPUTE_OPS = {
    "tensor_copy", "tensor_tensor", "tensor_add", "tensor_sub", "tensor_mul",
    "tensor_max", "tensor_scalar", "tensor_single_scalar", "tensor_scalar_add",
    "tensor_scalar_sub", "tensor_scalar_mul", "tensor_scalar_max",
    "tensor_scalar_min", "tensor_relu", "tensor_reduce", "reduce_max",
    "reduce_sum", "scalar_tensor_tensor", "tensor_tensor_scan", "memset",
    "memzero", "activation", "mul", "add", "copy", "square", "sqrt", "sign",
    "reciprocal", "select", "copy_predicated", "transpose", "bn_stats",
    "bn_aggr", "max", "max_index", "match_replace", "stream_shuffle", "iota",
    "engine_nop", "pool", "pool_avg", "pool_max",
}


class Eng:
    def __init__(self, eng, ctx, semname):
        self._e, self._c, self._s = eng, ctx, semname

    def __getattr__(self, name):
        f = getattr(self._e, name)
        if name in _COMPUTE_OPS:
            def wrap(*a, **k):
                self._c.w(self._e, self._s)
                ins = f(*a, **k)
                ins.then_inc(self._c.sems[self._s], 1)
                self._c.cnt[self._s] += 1
                return ins
            return wrap
        return f

class Ctx:
    def __init__(self, nc, block, sems):
        self.nc, self.block, self.sems = nc, block, sems
        self.cnt = {k: 0 for k in sems}

    def inc(self, instr, name, k=1):
        if name in ("sDVE", "sACT", "sPOOL"):
            return self.cnt[name]
        instr.then_inc(self.sems[name], k)
        self.cnt[name] += k
        return self.cnt[name]

    def w(self, eng, name, val=None):
        v = self.cnt[name] if val is None else val
        if v > 0:
            eng.wait_ge(self.sems[name], v)

    BARRIER_SEMS = ("sPE", "sACT", "sDVE", "sPOOL", "sCC", "sW12", "sOUT")

    def barrier(self):
        snap = {k: self.cnt[k] for k in self.BARRIER_SEMS if k in self.cnt}
        for en in ("sync", "tensor", "vector", "scalar", "gpsimd"):
            def f(e, snap=snap):
                for nm, v in snap.items():
                    if v > 0:
                        e.wait_ge(self.sems[nm], v)
            getattr(self.block, en)(f)


class Arena:
    def __init__(self, t, nbytes):
        self.t, self.nbytes, self.off, self.stack = t, nbytes, 0, []
        self.peak = 0

    def push(self):
        self.stack.append(self.off)

    def pop(self):
        self.off = self.stack.pop()

    def alloc(self, rows, cols, dt):
        esz = 2 if dt == BF16 else 4
        self.off = (self.off + 3) // 4 * 4
        base = self.off
        self.off += cols * esz
        self.peak = max(self.peak, self.off)
        assert self.off <= self.nbytes, f"arena overflow {self.off}>{self.nbytes}"
        v = self.t.ap()[0:rows, base // 2: base // 2 + cols * (esz // 2)]
        if dt == F32:
            v = v.bitcast(F32)
        return v


def build_nc(upto=None):
    upto = upto or UPTO
    nc = bass.Bass("TRN2", target_bir_lowering=False)
    d_in = {}
    for nm, shape in [
        ("xfm", [128, 8 * NCH * CT]), ("halo", [128, NCH * 8 * 3]),
        ("hmask", [128, NCH]), ("agmask", [128, NCH * 7]),
        ("onehots", [48, 2048]),
        ("neg1", [128, 8]), ("ident", [128, 128]), ("qaug", [4, 1024]), ("kaug", [NCH * 4, T]),
        ("conv_w_pd", [128, 64]), ("conv_b_pd", [128, 16]), ("A_pd", [128, 256]),
        ("D_pd", [128, 16]), ("gamma_pd", [128, 8]), ("beta_pd", [128, 8]),
        ("wx_ext", [DI, 65]), ("wqr_ext", [D, 64]), ("wkr_ext", [DC, 64]),
        ("w_in", [D, 2 * DI]), ("w_out", [DI, D]), ("gate_w", [D, E]),
        ("w1", [E * D, FF]), ("w2", [E * FF, D]),
        ("w_kvd", [D, DC]), ("w_kup", [DC, D]), ("w_vup", [DC, D]),
        ("w_q", [D, D]), ("w_o", [D, D]),
    ]:
        d_in[nm] = nc.dram_tensor(nm, shape, F32, kind="ExternalInput").ap()
    for nm, shape in [("masksbf", [128, 2048]), ("ropesbf", [128, T]),
                      ("ropesbf_own", [128, NCH * CT])]:
        d_in[nm] = nc.dram_tensor(nm, shape, BF16, kind="ExternalInput").ap()
    yfm = nc.dram_tensor("yfm", [128, 8 * NCH * CT], F32, kind="ExternalOutput").ap()

    d_bf = {}
    for nm, shape in [
        ("w_in", [D, 2 * DI]), ("w_out", [DI, D]), ("wx_ext", [DI, 65]),
        ("gate_w", [D, E]), ("w1", [E * D, FF]), ("w2", [E * FF, D]),
        ("w_kvd", [D, DC]), ("w_kup", [DC, D]), ("w_vup", [DC, D]),
        ("w_q", [D, D]), ("w_o", [D, D]), ("wqr_ext", [D, 64]),
        ("wkr_ext", [DC, 64]),
    ]:
        d_bf[nm] = nc.dram_tensor(nm + "_bf", shape, BF16).ap()
    ag1_in = nc.dram_tensor("ag1_in", [128, NCH * 257], F32)
    ag1_out = nc.dram_tensor("ag1_out", [8 * 128, NCH * 257], F32)
    ag2_in = nc.dram_tensor("ag2_in", [128, TCORE], BF16)
    ag2_out = nc.dram_tensor("ag2_out", [8 * 128, TCORE], BF16)

    from contextlib import ExitStack
    with ExitStack() as _es:
        res = _es.enter_context(nc.sbuf_tensor("res", [128, 8 * NCH * CT], F32))
        arena_t = _es.enter_context(nc.sbuf_tensor("arena", [128, 40960], BF16))
        xnbf = _es.enter_context(nc.sbuf_tensor("xnbf", [128, 8 * CT], BF16))
        rows1 = _es.enter_context(nc.sbuf_tensor("rows1", [66, NCH * CT], F32))
        rows2 = _es.enter_context(nc.sbuf_tensor("rows2", [66, NCH * CT], F32))
        rows3 = _es.enter_context(nc.sbuf_tensor("rows3", [66, NCH * CT], F32))
        small = _es.enter_context(nc.sbuf_tensor("small", [128, 700], F32))
        oneh = _es.enter_context(nc.sbuf_tensor("oneh", [48, 2048], F32))
        ones3 = _es.enter_context(nc.sbuf_tensor("ones3", [66, 128], F32))
        hstart = _es.enter_context(nc.sbuf_tensor("hstart", [128, 256], F32))
        pub = _es.enter_context(nc.sbuf_tensor("pub", [128, NCH * 257], F32))
        pred = _es.enter_context(nc.sbuf_tensor("pred", [128, 257], F32))
        wring_t = _es.enter_context(nc.sbuf_tensor("wring", [128, 4096], BF16))
        PS = _es.enter_context(nc.psum_tensor("PS", [128, 4096], F32))
        _sems = [_es.enter_context(nc.semaphore("s%d" % i)) for i in range(13)]
        _sems13 = _sems
        _1, _2, _3, _4, _5, _6, _7, _8, _9, _10 = _sems[:10]
        _11, _12, _13 = _sems[10:]
        block = _es.enter_context(nc.Block())
        def BV(fn):
            block.vector(lambda v: fn(Eng(v, ctx, "sDVE")))

        def BS(fn):
            block.scalar(lambda a: fn(Eng(a, ctx, "sACT")))

        def BG(fn):
            block.gpsimd(lambda g: fn(Eng(g, ctx, "sPOOL")))

        ctx = Ctx(nc, block, {"sDW": _1, "sDX": _2, "sPE": _3, "sACT": _4,
                              "sDVE": _5, "sPOOL": _6, "sOUT": _7, "sCC": _8, "sDG": _9, "sW12": _10,
                              "sLA": _11, "sLB": _12, "sLD": _13})
        AR = Arena(arena_t, 81920)
        WR = Arena(wring_t, 8192)

        def bank(i, n=1):
            return PS[:, i * 512:(i + n) * 512]

        sm = small.ap()
        convw_sb = sm[:, 0:64]
        convb_sb = sm[:, 64:80]
        A_sb = sm[:, 80:336]
        D_sb = sm[:, 336:352]
        gam_sb = sm[:, 352:360]
        bet_sb = sm[:, 360:368]
        ones_col = sm[:, 368:369]
        eps_col = sm[:, 369:370]
        hmask_sb = sm[:, 370:372]
        agm_sb = sm[:, 372:386]
        sdt_sb = sm[:, 386:388]
        neg1_sb = sm[:, 388:396]
        tiny = sm[:, 400:700]

        def p0_sync(s):
            for dst, src in [
                (res[:, :], d_in["xfm"]), (oneh[:, :], d_in["onehots"]),
                (convw_sb, d_in["conv_w_pd"]), (convb_sb, d_in["conv_b_pd"]),
                (A_sb, d_in["A_pd"]), (D_sb, d_in["D_pd"]),
                (gam_sb, d_in["gamma_pd"]), (bet_sb, d_in["beta_pd"]),
                (hmask_sb, d_in["hmask"]), (agm_sb, d_in["agmask"]),
                (neg1_sb, d_in["neg1"]),
            ]:
                ctx.inc(s.dma_start(out=dst, in_=src), "sDX", 16)
        block.sync(p0_sync)

        def p0_vec(v):
            ctx.w(v, "sDX")
            v.memset(ones_col, 1.0)
            v.memset(eps_col, float(LN_EPS))
            v.memset(ones3[:, :], 1.0)
            ctx.inc(v.memset(hstart[:, :], 0.0), "sDVE")
        BV(p0_vec)

        def p0_scalar(sc):
            ctx.w(sc, "sDVE")
            sc.activation(A_sb, A_sb, AF.Exp)
            ctx.inc(sc.mul(A_sb, A_sb, -1.0), "sACT")
        BS(p0_scalar)

        def p0_preconv(g):
            for nm in ["w_in", "w_out", "wx_ext", "gate_w", "w_kvd", "w_kup",
                       "w_vup", "w_q", "w_o", "wqr_ext", "wkr_ext"]:
                srct, dst = d_in[nm], d_bf[nm]
                ctx.inc(g.dma_start(out=dst[:, :], in_=srct[:, :]), "sDG", 16)
        BG(p0_preconv)
        ctx.barrier()

        w12_done = [0]
        if upto in ("ffn1", "attn", "all"):
            def p0_preconv2(g):
                for nm in ("w1", "w2"):
                    srct, dst = d_in[nm], d_bf[nm]
                    rows = srct.shape[0]
                    for r0 in range(0, rows, rows // 4):
                        last = (nm == "w2" and r0 + rows // 4 >= rows)
                        ctx.inc(g.dma_start(out=dst[r0:r0 + rows // 4, :],
                                            in_=srct[r0:r0 + rows // 4, :]),
                                "sW12" if last else "sDG", 16)
                w12_done[0] = ctx.cnt["sW12"]
            BG(p0_preconv2)

        winb = d_bf["w_in"]
        woutb = d_bf["w_out"]
        wxb = d_bf["wx_ext"]

        wx_sb = AR.alloc(128, 16 * 65, BF16)
        ident_sb = AR.alloc(128, 128, F32)
        wi_ring = [WR.alloc(128, 1024, BF16) for _ in range(2)]
        wo_ring = [WR.alloc(128, 1024, BF16) for _ in range(2)]

        preconv_done = ctx.cnt["sDG"]

        def p_load_wx(s):
            ctx.w(s, "sDG", preconv_done)
            for di in range(16):
                ctx.inc(s.dma_start(out=wx_sb[:, di * 65:(di + 1) * 65],
                                    in_=wxb[di * 128:(di + 1) * 128, :]), "sDW", 16)
            ctx.inc(s.dma_start(out=ident_sb, in_=d_in["ident"]), "sDW", 16)
        block.sync(p_load_wx)

        def bcast(t, dst, src_row, tn):
            bp = src_row.base_partition()
            onr = ones3[bp:bp + 1, 0:128]
            ins = None
            for i in range((tn + 511) // 512):
                c0, c1 = i * 512, min(tn, (i + 1) * 512)
                ins = t.matmul(dst[:, c0:c1], onr, src_row[:, c0:c1],
                               start=True, stop=True)
            return ins

        def win_slice(di, zhalf=False):
            c0 = (DI if zhalf else 0) + di * 128
            return winb[:, c0:c0 + 128].rearrange("(m p) f -> p m f", p=128)

        def emit_ln2(srcv, tn, mu_r, rs_r, mui_r, rsi_r, dstfn, sq, lnt,
                     mask_col=None):
            def on_sq(a):
                ctx.w(a, "sDVE")
                for i in range(8):
                    ins = a.activation(sq[:, i * tn:(i + 1) * tn], srcv[i], AF.Square)
                ctx.inc(ins, "sACT")
            BS(on_sq)

            def on_pe(t):
                ctx.w(t, "sACT")
                ctx.w(t, "sDVE")
                for i in range(8):
                    t.matmul(PS[0:1, 0:tn], ones_col, srcv[i],
                             start=(i == 0), stop=(i == 7))
                for i in range(8):
                    ins = t.matmul(PS[32:33, 0:tn], ones_col,
                                   sq[:, i * tn:(i + 1) * tn],
                                   start=(i == 0), stop=(i == 7))
                ctx.inc(ins, "sPE")
            block.tensor(on_pe)

            def on_rows(v):
                ctx.w(v, "sPE")
                v.tensor_scalar_mul(mu_r, PS[0:1, 0:tn], 1.0 / D)
                v.tensor_scalar_mul(rs_r, PS[32:33, 0:tn], 1.0 / D)
                tmp = PS[32:33, 3 * 512:3 * 512 + tn]
                v.tensor_mul(tmp, mu_r, mu_r)
                ctx.inc(v.tensor_sub(rs_r, rs_r, tmp), "sDVE")
            BV(on_rows)

            def on_rstd(a):
                ctx.w(a, "sDVE")
                a.activation(rs_r, rs_r, AF.Ln, bias=eps_col[0:1, 0:1])
                ctx.inc(a.activation(rs_r, rs_r, AF.Exp, scale=-0.5), "sACT")
            BS(on_rstd)

            def on_bc(t):
                ctx.w(t, "sACT")
                bcast(t, bank(1), mu_r, tn)
                ctx.inc(bcast(t, bank(2), rs_r, tn), "sPE")
            block.tensor(on_bc)

            def on_ln0(v):
                ctx.w(v, "sPE")
                for i in range(8):
                    o = lnt[:, i * tn:(i + 1) * tn]
                    v.tensor_sub(o, srcv[i], bank(1)[:, 0:tn])
                    ins = v.tensor_mul(o, o, bank(2)[:, 0:tn])
                ctx.inc(ins, "sDVE")
            BV(on_ln0)

            def on_sq2(a):
                ctx.w(a, "sDVE")
                for i in range(8):
                    ins = a.activation(sq[:, i * tn:(i + 1) * tn],
                                       lnt[:, i * tn:(i + 1) * tn], AF.Square)
                ctx.inc(ins, "sACT")
            BS(on_sq2)

            def on_pe2(t):
                ctx.w(t, "sACT")
                for i in range(8):
                    t.matmul(PS[0:1, 0:tn], ones_col,
                             lnt[:, i * tn:(i + 1) * tn],
                             start=(i == 0), stop=(i == 7))
                for i in range(8):
                    ins = t.matmul(PS[32:33, 0:tn], ones_col,
                                   sq[:, i * tn:(i + 1) * tn],
                                   start=(i == 0), stop=(i == 7))
                ctx.inc(ins, "sPE")
            block.tensor(on_pe2)

            def on_rows2(v):
                ctx.w(v, "sPE")
                v.tensor_scalar_mul(mui_r, PS[0:1, 0:tn], 1.0 / D)
                v.tensor_scalar_mul(rsi_r, PS[32:33, 0:tn], 1.0 / D)
                tmp = PS[32:33, 3 * 512:3 * 512 + tn]
                v.tensor_mul(tmp, mui_r, mui_r)
                ctx.inc(v.tensor_sub(rsi_r, rsi_r, tmp), "sDVE")
            BV(on_rows2)

            def on_rstd2(a):
                ctx.w(a, "sDVE")
                a.activation(rsi_r, rsi_r, AF.Ln, bias=eps_col[0:1, 0:1])
                ctx.inc(a.activation(rsi_r, rsi_r, AF.Exp, scale=-0.5), "sACT")
            BS(on_rstd2)

            def on_bc2(t):
                ctx.w(t, "sACT")
                bcast(t, bank(1), mui_r, tn)
                ctx.inc(bcast(t, bank(2), rsi_r, tn), "sPE")
            block.tensor(on_bc2)

            def on_xn(v):
                ctx.w(v, "sPE")
                for i in range(8):
                    o = lnt[:, i * tn:(i + 1) * tn]
                    v.tensor_sub(o, o, bank(1)[:, 0:tn])
                    v.tensor_mul(o, o, bank(2)[:, 0:tn])
                    if mask_col is None:
                        ins = v.tensor_scalar(dstfn(i), o, gam_sb[:, i:i + 1],
                                              bet_sb[:, i:i + 1], ALU.mult, ALU.add)
                    else:
                        v.tensor_scalar(o, o, gam_sb[:, i:i + 1],
                                        bet_sb[:, i:i + 1], ALU.mult, ALU.add)
                        ins = v.tensor_scalar_mul(dstfn(i), o, mask_col)
                ctx.inc(ins, "sDVE")
            BV(on_xn)

        def emit_xinner_conv(ci, t0, xconv, xi_sb, stash, conv_acc):
            for di in range(16):
                ring = wi_ring[di % 2]

                def ld(s, di=di, ring=ring):
                    ctx.w(s, "sPE")
                    ctx.inc(s.dma_start(out=ring[:, 0:1024].rearrange("p (m f) -> p m f", m=8), in_=win_slice(di)),
                            "sDW", 16)
                block.sync(ld)

                def mm(t, di=di, ring=ring, t0=t0):
                    ctx.w(t, "sDW")
                    ctx.w(t, "sDVE")
                    for m in range(8):
                        ins = t.matmul(bank(3)[:, 0:TC],
                                       ring[:, m * 128:(m + 1) * 128],
                                       xnbf[:, m * CT + t0:m * CT + t0 + TC],
                                       start=(m == 0), stop=(m == 7))
                    ctx.inc(ins, "sPE")
                block.tensor(mm)

                def conv(v, di=di):
                    ctx.w(v, "sPE")
                    ctx.w(v, "sACT")
                    v.tensor_copy(xi_sb[:, 0:3], stash[:, di * 3:(di + 1) * 3])
                    v.tensor_copy(xi_sb[:, 3:3 + TC], bank(3)[:, 0:TC])
                    v.tensor_copy(stash[:, di * 3:(di + 1) * 3],
                                  bank(3)[:, TC - 3:TC])
                    v.tensor_scalar_mul(conv_acc[:, 0:TC], xi_sb[:, 0:TC],
                                        convw_sb[:, di * 4:di * 4 + 1])
                    for j in range(1, 4):
                        ins = v.scalar_tensor_tensor(
                            conv_acc[:, 0:TC], xi_sb[:, j:j + TC],
                            convw_sb[:, di * 4 + j:di * 4 + j + 1],
                            conv_acc[:, 0:TC], ALU.mult, ALU.add)
                    ctx.inc(ins, "sDVE")
                BV(conv)

                def silu(a, di=di, t0=t0):
                    ctx.w(a, "sDVE")
                    ctx.inc(a.activation(
                        xconv[:, di * CT + t0:di * CT + t0 + TC],
                        conv_acc[:, 0:TC], AF.Silu,
                        bias=convb_sb[:, di:di + 1]), "sACT")
                BS(silu)

        def emit_halo(ci, halo_f, halo_xn, stash, sq, lnt):
            def ld(s, ci=ci):
                ctx.w(s, "sACT")
                ctx.w(s, "sDVE")
                ctx.w(s, "sPOOL")
                ctx.w(s, "sPE")
                ctx.inc(s.dma_start(out=halo_f[:, :],
                                    in_=d_in["halo"][:, ci * 24:(ci + 1) * 24]),
                        "sDX", 16)
            block.sync(ld)

            def wt(v):
                ctx.w(v, "sDX")
                ctx.inc(v.engine_nop(), "sDVE")
            BV(wt)
            hv = [halo_f[:, i * 3:(i + 1) * 3] for i in range(8)]
            emit_ln2(hv, 3, tiny[0:1, 8:11], tiny[0:1, 12:15],
                     tiny[0:1, 16:19], tiny[0:1, 20:23],
                     lambda i: halo_xn[:, i * 3:(i + 1) * 3],
                     sq, lnt, mask_col=hmask_sb[:, ci:ci + 1])
            for di in range(16):
                ring = wi_ring[di % 2]

                def ld2(s, di=di, ring=ring):
                    ctx.w(s, "sPE")
                    ctx.inc(s.dma_start(out=ring[:, 0:1024].rearrange("p (m f) -> p m f", m=8), in_=win_slice(di)),
                            "sDW", 16)
                block.sync(ld2)

                def mm(t, di=di, ring=ring):
                    ctx.w(t, "sDW")
                    ctx.w(t, "sDVE")
                    for m in range(8):
                        ins = t.matmul(bank(3)[:, 0:3],
                                       ring[:, m * 128:(m + 1) * 128],
                                       halo_xn[:, m * 3:(m + 1) * 3],
                                       start=(m == 0), stop=(m == 7))
                    ctx.inc(ins, "sPE")
                block.tensor(mm)

                def ev(v, di=di):
                    ctx.w(v, "sPE")
                    ctx.inc(v.tensor_copy(stash[:, di * 3:(di + 1) * 3],
                                          bank(3)[:, 0:3]), "sDVE")
                BV(ev)

        for ci in (range(NCH) if upto != "p0" else []):
            AR.push()
            xconv = AR.alloc(128, 16 * CT, BF16)
            AR.push()
            sq_t = AR.alloc(128, 8 * TC, F32)
            ln_t = AR.alloc(128, 8 * TC, F32)
            xi_sb = AR.alloc(128, TC + 3, F32)
            stash = AR.alloc(128, 16 * 3, F32)
            halo_xn = AR.alloc(128, 8 * 3, BF16)
            halo_f = AR.alloc(128, 8 * 3, F32)
            conv_acc = AR.alloc(128, TC, F32)

            emit_halo(ci, halo_f, halo_xn, stash, sq_t, ln_t)
            for tc in range(CT // TC):
                t0 = tc * TC
                srcv = [res[:, col(ci, dh, t0):col(ci, dh, t0) + TC] for dh in range(8)]
                emit_ln2(srcv, TC,
                         rows2[0:1, ci * CT + t0:ci * CT + t0 + TC],
                         rows2[32:33, ci * CT + t0:ci * CT + t0 + TC],
                         rows3[0:1, ci * CT + t0:ci * CT + t0 + TC],
                         rows3[32:33, ci * CT + t0:ci * CT + t0 + TC],
                         lambda i, t0=t0: xnbf[:, i * CT + t0:i * CT + t0 + TC],
                         sq_t, ln_t)
                emit_xinner_conv(ci, t0, xconv, xi_sb, stash, conv_acc)

                def wx_mm(t, ci=ci, t0=t0, xconv=xconv):
                    ctx.w(t, "sACT")
                    ctx.w(t, "sDVE")
                    for di in range(16):
                        ins = t.matmul(PS[0:65, 4 * 512:4 * 512 + TC],
                                       wx_sb[:, di * 65:(di + 1) * 65],
                                       xconv[:, di * CT + t0:di * CT + t0 + TC],
                                       start=(di == 0), stop=(di == 15))
                    ctx.inc(ins, "sPE")
                block.tensor(wx_mm)

                dt_r = rows1[64:65, ci * CT + t0:ci * CT + t0 + TC]
                b_r = rows1[0:16, ci * CT + t0:ci * CT + t0 + TC]
                c_r = rows1[32:48, ci * CT + t0:ci * CT + t0 + TC]
                scr = rows2[64:65, ci * CT + t0:ci * CT + t0 + TC]

                def dtrow1(v, dt_r=dt_r, scr=scr):
                    ctx.w(v, "sPE")
                    ctx.inc(v.tensor_scalar_max(
                        scr, PS[64:65, 4 * 512:4 * 512 + TC], 0.0), "sDVE")
                BV(dtrow1)

                def dtrow2(a, dt_r=dt_r):
                    ctx.w(a, "sDVE")
                    a.activation(dt_r, PS[64:65, 4 * 512:4 * 512 + TC], AF.Abs)
                    a.activation(dt_r, dt_r, AF.Exp, scale=-1.0)
                    ctx.inc(a.activation(dt_r, dt_r, AF.Ln, bias=ones_col[0:1, 0:1]), "sACT")
                BS(dtrow2)

                def dtrow3(v, dt_r=dt_r, scr=scr, b_r=b_r, c_r=c_r):
                    ctx.w(v, "sACT")
                    v.tensor_add(dt_r, dt_r, scr)
                    v.tensor_copy(c_r, PS[32:48, 4 * 512:4 * 512 + TC])
                    ctx.inc(v.tensor_copy(b_r, PS[0:16, 4 * 512:4 * 512 + TC]),
                            "sDVE")
                BV(dtrow3)

                def dtb16(t, dt_r=dt_r):
                    ctx.w(t, "sDVE")
                    ctx.inc(t.matmul(PS[0:16, 5 * 512:5 * 512 + TC],
                                     ones3[64:65, 0:16], dt_r,
                                     start=True, stop=True), "sPE")
                block.tensor(dtb16)

                def wBmul(v, b_r=b_r):
                    ctx.w(v, "sPE")
                    ctx.inc(v.tensor_mul(b_r, b_r, PS[0:16, 5 * 512:5 * 512 + TC]),
                            "sDVE")
                BV(wBmul)
            AR.pop()
            sc_dA = [AR.alloc(128, CT, F32) for _ in range(2)]
            sc_ub = [AR.alloc(128, CT, F32) for _ in range(2)]
            sc_h = [AR.alloc(128, CT, F32) for _ in range(2)]

            def sdt_red(v, ci=ci):
                v.tensor_reduce(sm[64:65, 428:429],
                                rows1[64:65, ci * CT:(ci + 1) * CT], AX.X, ALU.add)
                ctx.inc(v.engine_nop(), "sDVE")
            BV(sdt_red)

            def sdt_bc(t, ci=ci):
                ctx.w(t, "sDVE")
                t.matmul(bank(3)[:, 0:1], ones3[64:65, 0:128], sm[64:65, 428:429],
                         start=True, stop=True)
                ctx.inc(bcast(t, bank(0, 2),
                              rows1[64:65, ci * CT:(ci + 1) * CT], CT), "sPE")
            block.tensor(sdt_bc)

            def sdt_cp(v, ci=ci):
                ctx.w(v, "sPE")
                v.tensor_copy(sdt_sb[:, ci:ci + 1], bank(3)[:, 0:1])
                ctx.inc(v.tensor_copy(pub[:, ci * 257 + 256:ci * 257 + 257],
                                      bank(3)[:, 0:1]), "sDVE")
            BV(sdt_cp)

            for n in range(16):
                def nb(t, n=n, ci=ci):
                    ctx.w(t, "sDVE")
                    ctx.w(t, "sPOOL")
                    for i in range(2):
                        ins = t.matmul(PS[:, (2 + i) * 512:(3 + i) * 512],
                                       oneh[0:16, n * 128:(n + 1) * 128],
                                       rows1[0:16, ci * CT + i * 512:
                                             ci * CT + (i + 1) * 512],
                                       start=True, stop=True)
                    ctx.inc(ins, "sPE")
                block.tensor(nb)

                for db in range(16):
                    buf = (n * 16 + db) % 2
                    dA, ub, h = sc_dA[buf], sc_ub[buf], sc_h[buf]

                    def mk_dA(a, n=n, db=db, dA=dA):
                        ctx.w(a, "sPE")
                        ctx.w(a, "sDVE")
                        ctx.inc(a.activation(
                            dA[:, 0:CT], bank(0, 2), AF.Exp,
                            scale=A_sb[:, db * 16 + n:db * 16 + n + 1]), "sACT")
                    BS(mk_dA)

                    def mk_ub(v, db=db, ub=ub, xconv=xconv):
                        ctx.w(v, "sPE")
                        ctx.inc(v.tensor_mul(ub[:, 0:CT],
                                             xconv[:, db * CT:(db + 1) * CT],
                                             bank(2, 2)), "sDVE")
                    BV(mk_ub)

                    def do_scan(v, n=n, db=db, dA=dA, ub=ub, h=h, ci=ci):
                        ctx.w(v, "sACT")
                        ctx.w(v, "sPOOL")
                        ctx.w(v, "sDVE")
                        ctx.inc(v.tensor_tensor_scan(h[:, 0:CT], dA[:, 0:CT],
                                                     ub[:, 0:CT], 0.0,
                                                     ALU.mult, ALU.add), "sDVE")
                        ctx.w(v, "sDVE")
                        ctx.inc(v.tensor_copy(
                            pub[:, ci * 257 + db * 16 + n:ci * 257 + db * 16 + n + 1],
                            h[:, CT - 1:CT]), "sDVE")
                    BV(do_scan)

            def pubfence(v):
                ctx.w(v, "sPOOL")
                ctx.w(v, "sDVE")
                ctx.inc(v.engine_nop(), "sDVE")
            BV(pubfence)
            AR.pop()

        if upto in ("p0", "p1"):
            ctx.barrier()

            def out_dma0(s):
                ctx.inc(s.dma_start(out=yfm, in_=res[:, :]), "sOUT", 16)
                s.wait_ge(ctx.sems["sOUT"], ctx.cnt["sOUT"])
            block.sync(out_dma0)
            return nc

        def ag1(g):
            ctx.w(g, "sDVE")
            ctx.inc(g.dma_start(out=ag1_in[:, :], in_=pub[:, :]), "sDG", 16)
            g.wait_ge(ctx.sems["sDG"], ctx.cnt["sDG"])
            ctx.inc(g.collective_compute(
                "AllGather", ALU.bypass, replica_groups=[list(range(8))],
                ins=[ag1_in.ap()], outs=[ag1_out.ap()]), "sCC", 1)
            g.wait_ge(ctx.sems["sCC"], ctx.cnt["sCC"])
        BG(ag1)
        ctx.barrier()

        if upto == "ag":
            def out_dma1(s):
                ctx.inc(s.dma_start(out=yfm, in_=res[:, :]), "sOUT", 16)
                s.wait_ge(ctx.sems["sOUT"], ctx.cnt["sOUT"])
            block.sync(out_dma1)
            return nc

        for ci in range(NCH):
            AR.push()
            xconv3 = AR.alloc(128, 16 * CT, BF16)
            y_acc = AR.alloc(128, DGRP * CT, F32)
            xsc = AR.alloc(128, TC, F32)
            P_t = AR.alloc(128, 256, F32)

            def h0(v):
                ctx.inc(v.memset(hstart[:, :], 0.0), "sDVE")
            BV(h0)
            for s_ in range(7):
                rank = s_ if ci == 0 else 7 - s_

                def pld(g, rank=rank, ci=ci):
                    ctx.w(g, "sDVE")
                    ctx.inc(g.dma_start(
                        out=pred[:, :],
                        in_=ag1_out[rank * 128:(rank + 1) * 128,
                                    ci * 257:(ci + 1) * 257]), "sDG", 16)
                BG(pld)

                def hv1(v):
                    ctx.w(v, "sCC")
                    ctx.w(v, "sDG")
                    ctx.inc(v.tensor_scalar_mul(P_t, A_sb, pred[:, 256:257]),
                            "sDVE")
                BV(hv1)

                def ha(a):
                    ctx.w(a, "sDVE")
                    ctx.inc(a.activation(P_t, P_t, AF.Exp), "sACT")
                BS(ha)

                def hv2(v, s_=s_, ci=ci):
                    ctx.w(v, "sACT")
                    m = agm_sb[:, ci * 7 + s_:ci * 7 + s_ + 1]
                    v.tensor_scalar_add(P_t, P_t, -1.0)
                    v.tensor_scalar_mul(P_t, P_t, m)
                    v.tensor_scalar_add(P_t, P_t, 1.0)
                    v.tensor_mul(hstart[:, :], hstart[:, :], P_t)
                    ctx.inc(v.scalar_tensor_tensor(hstart[:, :], pred[:, 0:256], m,
                                                   hstart[:, :], ALU.mult, ALU.add),
                            "sDVE")
                BV(hv2)

            mu0 = rows2[0:1, ci * CT:(ci + 1) * CT]
            rs0 = rows2[32:33, ci * CT:(ci + 1) * CT]
            mui = rows3[0:1, ci * CT:(ci + 1) * CT]
            rsi = rows3[32:33, ci * CT:(ci + 1) * CT]

            for tc in range(2):
                t0 = tc * TC

                def xb1(t, t0=t0):
                    ctx.w(t, "sDVE")
                    bcast(t, bank(1), mu0[:, t0:t0 + TC], TC)
                    bcast(t, bank(2), rs0[:, t0:t0 + TC], TC)
                    bcast(t, bank(5), mui[:, t0:t0 + TC], TC)
                    ctx.inc(bcast(t, bank(6), rsi[:, t0:t0 + TC], TC), "sPE")
                block.tensor(xb1)

                def xv1(v, t0=t0, ci=ci):
                    ctx.w(v, "sPE")
                    for i in range(8):
                        o = xsc[:, 0:TC]
                        rv = res[:, col(ci, i, t0):col(ci, i, t0) + TC]
                        xno = xnbf[:, i * CT + t0:i * CT + t0 + TC]
                        v.tensor_sub(o, rv, bank(1)[:, 0:TC])
                        v.tensor_mul(o, o, bank(2)[:, 0:TC])
                        v.tensor_add(rv, rv, o)
                        v.tensor_sub(o, o, bank(5)[:, 0:TC])
                        v.tensor_mul(o, o, bank(6)[:, 0:TC])
                        ins = v.tensor_scalar(xno, o, gam_sb[:, i:i + 1],
                                              bet_sb[:, i:i + 1], ALU.mult, ALU.add)
                    ctx.inc(ins, "sDVE")
                BV(xv1)

            AR.push()
            sq3 = AR.alloc(128, 8 * 3, F32)
            lh3 = AR.alloc(128, 8 * 3, F32)
            xi3 = AR.alloc(128, TC + 3, F32)
            st3 = AR.alloc(128, 16 * 3, F32)
            hx3 = AR.alloc(128, 8 * 3, BF16)
            hf3 = AR.alloc(128, 8 * 3, F32)
            ca3 = AR.alloc(128, TC, F32)
            emit_halo(ci, hf3, hx3, st3, sq3, lh3)
            for tc in range(2):
                emit_xinner_conv(ci, tc * TC, xconv3, xi3, st3, ca3)
            AR.pop()

            AR.push()
            y3bf = AR.alloc(128, CT, BF16)
            s3_dA = [AR.alloc(128, CT, F32) for _ in range(2)]
            s3_ub = [AR.alloc(128, CT, F32) for _ in range(2)]
            s3_h = [AR.alloc(128, CT, F32) for _ in range(2)]
            zs_t = s3_dA[0]
            t2_t = s3_dA[1]

            for g0 in range(0, 16, DGRP):
                def dtb(t, ci=ci):
                    ctx.w(t, "sACT")
                    ctx.w(t, "sDVE")
                    ctx.inc(bcast(t, bank(0, 2),
                                  rows1[64:65, ci * CT:(ci + 1) * CT], CT), "sPE")
                block.tensor(dtb)

                scn3_h = {}
                dA3_h = {}
                ya_h = {}
                scn3_base = ctx.cnt["sDVE"]
                ya_base = ctx.cnt["sPOOL"]
                for n in range(16):
                    def nb(t, n=n, ci=ci):
                        ctx.w(t, "sDVE")
                        ctx.w(t, "sPOOL")
                        for i in range(2):
                            t.matmul(PS[:, (2 + i) * 512:(3 + i) * 512],
                                     oneh[0:16, n * 128:(n + 1) * 128],
                                     rows1[0:16, ci * CT + i * 512:
                                           ci * CT + (i + 1) * 512],
                                     start=True, stop=True)
                        for i in range(2):
                            ins = t.matmul(PS[:, (4 + i) * 512:(5 + i) * 512],
                                           oneh[32:48, n * 128:(n + 1) * 128],
                                           rows1[32:48, ci * CT + i * 512:
                                                 ci * CT + (i + 1) * 512],
                                           start=True, stop=True)
                        ctx.inc(ins, "sPE")
                    block.tensor(nb)
                    nb_pe3 = ctx.cnt["sPE"]

                    for db in range(g0, g0 + DGRP):
                        idx = n * DGRP + (db - g0)
                        buf = idx % 2
                        dA, ub, h = s3_dA[buf], s3_ub[buf], s3_h[buf]

                        def mk_dA(a, n=n, db=db, dA=dA, idx=idx, nb_pe3=nb_pe3):
                            ctx.w(a, "sPE", nb_pe3)
                            ctx.w(a, "sDVE", scn3_h.get(idx - 2, scn3_base))
                            ctx.inc(a.activation(
                                dA[:, 0:CT], bank(0, 2), AF.Exp,
                                scale=A_sb[:, db * 16 + n:db * 16 + n + 1]), "sACT")
                            dA3_h[idx] = ctx.cnt["sACT"]
                        BS(mk_dA)

                        def mk_ub(v, db=db, ub=ub, idx=idx, nb_pe3=nb_pe3):
                            ctx.w(v, "sPE", nb_pe3)
                            ctx.w(v, "sPOOL", ya_h.get(idx - 2, ya_base))
                            ctx.inc(v.tensor_mul(ub[:, 0:CT],
                                                 xconv3[:, db * CT:(db + 1) * CT],
                                                 bank(2, 2)), "sDVE")
                        BV(mk_ub)

                        def do_scan(v, n=n, db=db, dA=dA, ub=ub, h=h, idx=idx):
                            ctx.w(v, "sACT", dA3_h[idx])
                            v.tensor_tensor_scan(
                                h[:, 0:CT], dA[:, 0:CT], ub[:, 0:CT],
                                hstart[:, db * 16 + n:db * 16 + n + 1],
                                ALU.mult, ALU.add)
                            v.tensor_mul(ub[:, 0:CT], h[:, 0:CT],
                                         bank(4, 2))
                            scn3_h[idx] = ctx.cnt["sDVE"]
                        BV(do_scan)

                        def y_add(g_, n=n, db=db, ub=ub, g0=g0, idx=idx):
                            ctx.w(g_, "sDVE", scn3_h[idx])
                            ya = y_acc[:, (db - g0) * CT:(db - g0 + 1) * CT]
                            if n == 0:
                                ins = g_.tensor_copy(ya, ub[:, 0:CT])
                            else:
                                ins = g_.tensor_add(ya, ya, ub[:, 0:CT])
                            ya_h[idx] = ctx.inc(ins, "sPOOL")
                        BG(y_add)

                for db in range(g0, g0 + DGRP):
                    ring = wi_ring[db % 2]

                    def zld(s, db=db, ring=ring):
                        ctx.w(s, "sPE")
                        ctx.inc(s.dma_start(out=ring[:, 0:1024].rearrange("p (m f) -> p m f", m=8),
                                            in_=win_slice(db, zhalf=True)),
                                "sDW", 16)
                    block.sync(zld)

                    def zmm(t, db=db, ring=ring):
                        ctx.w(t, "sDW")
                        ctx.w(t, "sACT")
                        ctx.w(t, "sPOOL")
                        for m in range(8):
                            for half in range(2):
                                ins = t.matmul(
                                    PS[:, half * 512:(half + 1) * 512],
                                    ring[:, m * 128:(m + 1) * 128],
                                    xnbf[:, m * CT + half * TC:
                                         m * CT + (half + 1) * TC],
                                    start=(m == 0), stop=(m == 7))
                        ctx.inc(ins, "sPE")
                    block.tensor(zmm)

                    def zsilu(a):
                        ctx.w(a, "sPE")
                        a.activation(zs_t[:, 0:TC], bank(0), AF.Silu)
                        ctx.inc(a.activation(zs_t[:, TC:CT], bank(1), AF.Silu),
                                "sACT")
                    BS(zsilu)

                    def gate(v, db=db, g0=g0):
                        ctx.w(v, "sACT")
                        ctx.w(v, "sPOOL")
                        ctx.w(v, "sPE")
                        ya = y_acc[:, (db - g0) * CT:(db - g0 + 1) * CT]
                        xc = xconv3[:, db * CT:(db + 1) * CT]
                        v.scalar_tensor_tensor(t2_t[:, 0:CT], xc,
                                               D_sb[:, db:db + 1], ya,
                                               ALU.mult, ALU.add)
                        ctx.inc(v.tensor_mul(y3bf[:, 0:CT], t2_t[:, 0:CT],
                                             zs_t[:, 0:CT]), "sDVE")
                    BV(gate)

                    def wld(s, db=db):
                        ctx.w(s, "sPE")
                        ctx.inc(s.dma_start(
                            out=wo_ring[db % 2][:, 0:1024],
                            in_=woutb[db * 128:(db + 1) * 128, :]), "sDW", 16)
                    block.sync(wld)

                    for om in range(8):
                        def wmm(t, om=om, db=db):
                            ctx.w(t, "sDW")
                            ctx.w(t, "sDVE")
                            b0 = 2 + (om % 2) * 2
                            for half in range(2):
                                ins = t.matmul(
                                    PS[:, (b0 + half) * 512:(b0 + half + 1) * 512],
                                    wo_ring[db % 2][:, om * 128:(om + 1) * 128],
                                    y3bf[:, half * TC:(half + 1) * TC],
                                    start=True, stop=True)
                            ctx.inc(ins, "sPE")
                        block.tensor(wmm)

                        def wadd(v, om=om, ci=ci):
                            ctx.w(v, "sPE")
                            b0 = 2 + (om % 2) * 2
                            rv = res[:, col(ci, om, 0):col(ci, om, 0) + CT]
                            ctx.inc(v.tensor_add(
                                rv, rv, PS[:, b0 * 512:(b0 + 2) * 512]), "sDVE")
                        BV(wadd)
            AR.pop()
            AR.pop()
        ctx.barrier()

        def emit_out():
            def out_dma(s):
                ctx.inc(s.dma_start(out=yfm, in_=res[:, :]), "sOUT", 16)
                s.wait_ge(ctx.sems["sOUT"], ctx.cnt["sOUT"])
            block.sync(out_dma)

        if upto == "mamba":
            emit_out()
            return nc

        def emit_ln1(ci, sqf, osc):
            for tc in range(2):
                t0 = tc * TC
                srcv = [res[:, col(ci, dh, t0):col(ci, dh, t0) + TC]
                        for dh in range(8)]
                mu_r = rows2[0:1, ci * CT + t0:ci * CT + t0 + TC]
                rs_r = rows2[32:33, ci * CT + t0:ci * CT + t0 + TC]

                def s_sq(a, srcv=srcv):
                    ctx.w(a, "sDVE")
                    for i in range(8):
                        a.activation(sqf[:, i * TC:(i + 1) * TC], srcv[i], AF.Square)
                BS(s_sq)

                def s_pe(t, srcv=srcv):
                    ctx.w(t, "sACT")
                    ctx.w(t, "sDVE")
                    for i in range(8):
                        t.matmul(PS[0:1, 0:TC], ones_col, srcv[i],
                                 start=(i == 0), stop=(i == 7))
                    for i in range(8):
                        ins = t.matmul(PS[32:33, 0:TC], ones_col,
                                       sqf[:, i * TC:(i + 1) * TC],
                                       start=(i == 0), stop=(i == 7))
                    ctx.inc(ins, "sPE")
                block.tensor(s_pe)

                def s_rows(v, mu_r=mu_r, rs_r=rs_r):
                    ctx.w(v, "sPE")
                    v.tensor_scalar_mul(mu_r, PS[0:1, 0:TC], 1.0 / D)
                    v.tensor_scalar_mul(rs_r, PS[32:33, 0:TC], 1.0 / D)
                    tmp = PS[32:33, 3 * 512:3 * 512 + TC]
                    v.tensor_mul(tmp, mu_r, mu_r)
                    v.tensor_sub(rs_r, rs_r, tmp)
                BV(s_rows)

                def s_rstd(a, rs_r=rs_r):
                    ctx.w(a, "sDVE")
                    a.activation(rs_r, rs_r, AF.Ln, bias=eps_col[0:1, 0:1])
                    a.activation(rs_r, rs_r, AF.Exp, scale=-0.5)
                BS(s_rstd)

                def s_bc(t, mu_r=mu_r, rs_r=rs_r):
                    ctx.w(t, "sACT")
                    bcast(t, bank(1), mu_r, TC)
                    ctx.inc(bcast(t, bank(2), rs_r, TC), "sPE")
                block.tensor(s_bc)

                def s_fin(v, srcv=srcv, t0=t0, ci=ci):
                    ctx.w(v, "sPE")
                    for i in range(8):
                        v.tensor_sub(osc[:, 0:TC], srcv[i], bank(1)[:, 0:TC])
                        v.tensor_mul(xnbf[:, i * CT + t0:i * CT + t0 + TC],
                                     osc[:, 0:TC], bank(2)[:, 0:TC])
                BV(s_fin)

        gate_sb = AR.alloc(128, 64, BF16)
        gateb = d_bf["gate_w"]
        w1b = d_bf["w1"]
        w2b = d_bf["w2"]

        def p_load_gate(s):
            ctx.inc(s.dma_start(
                out=gate_sb[:, :].rearrange("p (m e) -> p m e", m=8),
                in_=gateb.rearrange("(m p) e -> p m e", p=128)), "sDW", 16)
        block.sync(p_load_gate)

        lq_state = [0]
        lq_vals = {}
        lq_done = {}
        lq_tick = {}

        def emit_ffn(ci):
            AR.push()
            sqf = AR.alloc(128, 8 * TC, F32)
            osc = AR.alloc(128, TC, F32)
            emit_ln1(ci, sqf, osc)
            AR.pop()
            AR.push()
            glt = AR.alloc(128, CT, F32)
            h_e = AR.alloc(128, 32 * CT, BF16)
            w2r = AR.alloc(128, 32 * 128, BF16)

            lg = tiny[:, 0:8]
            m8 = tiny[:, 8:16]
            mv = tiny[:, 16:24]
            pr = tiny[:, 24:32]
            cmb = tiny[:, 32:40]
            ngm = tiny[:, 40:41]
            ssum = tiny[:, 41:42]
            den = tiny[:, 42:43]
            for tt in range(8):
                def g_pe(t, tt=tt, ci=ci):
                    ctx.w(t, "sDVE")
                    ctx.w(t, "sDW")
                    for m in range(8):
                        ins = t.matmul(
                            PS[:, 5 * 512:5 * 512 + 8],
                            xnbf[:, m * CT + tt * 128:m * CT + (tt + 1) * 128],
                            gate_sb[:, m * 8:(m + 1) * 8],
                            start=(m == 0), stop=(m == 7))
                    ctx.inc(ins, "sPE")
                block.tensor(g_pe)

                def g_v1(v):
                    ctx.w(v, "sPE")
                    v.tensor_copy(lg, PS[:, 5 * 512:5 * 512 + 8])
                    v.reduce_max(ngm, lg, AX.X, negate=True)
                    v.tensor_scalar_add(lg, lg, ngm[:, 0:1])
                BV(g_v1)

                def g_a1(a):
                    ctx.w(a, "sDVE")
                    a.activation(lg, lg, AF.Exp)
                BS(g_a1)

                def g_v2(v):
                    ctx.w(v, "sACT")
                    v.reduce_sum(ssum, lg, AX.X)
                    v.reciprocal(ssum, ssum)
                    v.tensor_scalar_mul(lg, lg, ssum[:, 0:1])
                    v.max(out=m8, in_=lg)
                    v.tensor_copy(mv[:, 0:2], m8[:, 0:2])
                    v.tensor_copy(mv[:, 2:8], neg1_sb[:, 2:8])
                    v.match_replace(pr, mv, lg, -1.0)
                    v.tensor_sub(pr, lg, pr)
                    v.tensor_scalar_min(pr, pr, 1.0)
                    v.tensor_add(den, m8[:, 0:1], m8[:, 1:2])
                    v.reciprocal(den, den)
                    v.tensor_mul(cmb, pr, lg)
                    v.tensor_scalar_mul(cmb, cmb, den[:, 0:1])
                BV(g_v2)

                def g_tr(t, tt=tt):
                    ctx.w(t, "sDVE")
                    ctx.inc(t.transpose(PS[0:8, 4 * 512:4 * 512 + 128], cmb,
                                        ident_sb), "sPE")
                block.tensor(g_tr)

                def g_cp(v, tt=tt, ci=ci):
                    ctx.w(v, "sPE")
                    v.tensor_copy(rows1[0:8, ci * CT + tt * 128:
                                        ci * CT + (tt + 1) * 128],
                                  PS[0:8, 4 * 512:4 * 512 + 128])
                BV(g_cp)

            for e in range(E):
                def c_bc(t, e=e, ci=ci):
                    ctx.w(t, "sDVE")
                    for i in range(2):
                        ins = t.matmul(PS[:, (6 + i) * 512:(7 + i) * 512],
                                       oneh[0:16, e * 128:(e + 1) * 128],
                                       rows1[0:16, ci * CT + i * 512:
                                             ci * CT + (i + 1) * 512],
                                       start=True, stop=True)
                    ctx.inc(ins, "sPE")
                block.tensor(c_bc)

                pe_hist = {}
                act_hist = {}
                dve_hist = {}
                pe_base = ctx.cnt["sPE"]
                act_base = ctx.cnt["sACT"]
                dve_base = ctx.cnt["sDVE"]

                def lq_issue(s, out, in_):
                    i = lq_state[0]
                    sem = "sLA" if i % 2 == 0 else "sLB"
                    if i >= 1:
                        psem = "sLA" if (i - 1) % 2 == 0 else "sLB"
                        s.wait_ge(ctx.sems[psem], lq_vals[i - 1])
                        s.sem_inc(ctx.sems["sLD"], 1)
                        ctx.cnt["sLD"] += 1
                        lq_done[i - 1] = ctx.cnt["sLD"]
                    lq_vals[i] = ctx.inc(s.dma_start(out=out, in_=in_), sem, 16)
                    lq_state[0] = i + 1

                def lq_flush(s):
                    i = lq_state[0] - 1
                    if i >= 0 and i not in lq_done:
                        psem = "sLA" if i % 2 == 0 else "sLB"
                        s.wait_ge(ctx.sems[psem], lq_vals[i])
                        s.sem_inc(ctx.sems["sLD"], 1)
                        ctx.cnt["sLD"] += 1
                        lq_done[i] = ctx.cnt["sLD"]

                def w1mm(t, fi, e):
                    ctx.w(t, "sLD", lq_done[lq_tick[("w1", e, fi)]])
                    ctx.w(t, "sACT", act_hist.get(fi - 2, act_base))
                    if fi < 2:
                        ctx.w(t, "sDVE", dve_base)
                    b0 = (fi % 2) * 2
                    for m in range(8):
                        for half in range(2):
                            ins = t.matmul(
                                PS[:, (b0 + half) * 512:(b0 + half + 1) * 512],
                                wi_ring[fi % 2][:, m * 128:(m + 1) * 128],
                                xnbf[:, m * CT + half * TC:
                                     m * CT + (half + 1) * TC],
                                start=(m == 0), stop=(m == 7))
                    pe_hist[fi] = ctx.inc(ins, "sPE")

                def w1ge(a, fi):
                    ctx.w(a, "sPE", pe_hist[fi])
                    ctx.w(a, "sDVE", dve_hist.get(fi - 1, dve_base))
                    b0 = (fi % 2) * 2
                    a.activation(glt[:, 0:TC], bank(b0), AF.Gelu)
                    a.activation(glt[:, TC:CT], bank(b0 + 1), AF.Gelu)
                    act_hist[fi] = ctx.cnt["sACT"]

                def w1cb(v, fi):
                    ctx.w(v, "sACT", act_hist[fi])
                    if fi == 0:
                        ctx.w(v, "sPE")
                    v.tensor_mul(h_e[:, fi * CT:(fi + 1) * CT],
                                 glt[:, 0:CT], bank(6, 2))
                    dve_hist[fi] = ctx.cnt["sDVE"]

                for fi in range(33):
                    if fi < 32:
                        def w1ld(s, e=e, fi=fi):
                            ctx.w(s, "sW12", w12_done[0])
                            ctx.w(s, "sPE", pe_hist.get(fi - 2, pe_base))
                            lq_tick[("w1", e, fi)] = lq_state[0]
                            lq_issue(s, wi_ring[fi % 2][:, 0:1024].rearrange(
                                         "p (m f) -> p m f", m=8),
                                     w1b[e * D:(e + 1) * D,
                                         fi * 128:(fi + 1) * 128].rearrange(
                                         "(m p) f -> p m f", p=128))
                        block.sync(w1ld)
                    else:
                        block.sync(lambda s: lq_flush(s))
                    if fi >= 1:
                        block.tensor(lambda t, fj=fi - 1, e=e: w1mm(t, fj, e))
                        BS(lambda a, fj=fi - 1: w1ge(a, fj))
                        BV(lambda v, fj=fi - 1: w1cb(v, fj))

                pe2_hist = {}
                dve2_hist = {}
                pe2_base = ctx.cnt["sPE"]
                dve2_base = ctx.cnt["sDVE"]

                def w2mm(t, om, hh, e):
                    ctx.w(t, "sLD", lq_done[lq_tick[("w2", e, om, hh)]])
                    if hh == 0:
                        ctx.w(t, "sDVE", dve2_hist.get(om - 2, dve2_base))
                    b0 = 2 + (om % 2) * 2
                    for fh in range(16):
                        fg = hh * 16 + fh
                        for half in range(2):
                            ins = t.matmul(
                                PS[:, (b0 + half) * 512:(b0 + half + 1) * 512],
                                w2r[:, hh * 2048 + fh * 128:
                                    hh * 2048 + (fh + 1) * 128],
                                h_e[:, fg * CT + half * TC:
                                    fg * CT + (half + 1) * TC],
                                start=(fg == 0), stop=(fg == 31))
                    pe2_hist[(om, hh)] = ctx.inc(ins, "sPE")

                def w2add(v, om, ci):
                    ctx.w(v, "sPE", pe2_hist[(om, 1)])
                    b0 = 2 + (om % 2) * 2
                    rv = res[:, col(ci, om, 0):col(ci, om, 0) + CT]
                    v.tensor_add(rv, rv, PS[:, b0 * 512:(b0 + 2) * 512])
                    dve2_hist[om] = ctx.cnt["sDVE"]

                jobs = [(om, hh) for om in range(8) for hh in range(2)]
                for j in range(17):
                    if j < 16:
                        om, hh = jobs[j]

                        def w2ld(s, e=e, om=om, hh=hh):
                            ctx.w(s, "sW12", w12_done[0])
                            ctx.w(s, "sPE", pe2_hist.get((om, hh - 2),
                                  pe2_hist.get((om - 1, hh), pe2_base)))
                            lq_tick[("w2", e, om, hh)] = lq_state[0]
                            lq_issue(s, w2r[:, hh * 2048:(hh + 1) * 2048].rearrange(
                                         "p (fh o) -> p fh o", fh=16),
                                     w2b[e * FF + hh * 2048:
                                         e * FF + (hh + 1) * 2048,
                                         om * 128:(om + 1) * 128].rearrange(
                                         "(fh p) o -> p fh o", p=128))
                        block.sync(w2ld)
                    else:
                        block.sync(lambda s: lq_flush(s))
                    if j >= 1:
                        pom, phh = jobs[j - 1]
                        block.tensor(lambda t, pom=pom, phh=phh, e=e:
                                     w2mm(t, pom, phh, e))
                        if phh == 1:
                            BV(lambda v, pom=pom, ci=ci: w2add(v, pom, ci))
            AR.pop()

        for ci in range(NCH):
            emit_ffn(ci)
        ctx.barrier()
        if upto == "ffn1":
            emit_out()
            return nc

        kvdb = d_bf["w_kvd"]
        kupb_d = d_bf["w_kup"]
        vupb_d = d_bf["w_vup"]
        wqb_d = d_bf["w_q"]
        wob_d = d_bf["w_o"]
        wqrb_d = d_bf["wqr_ext"]
        wkrb_d = d_bf["wkr_ext"]

        AR.push()
        sqf = AR.alloc(128, 8 * TC, F32)
        osc = AR.alloc(128, TC, F32)
        kvd_sb = AR.alloc(128, 1024, BF16)
        ckv_own = AR.alloc(128, NCH * CT, BF16)

        def kvd_ld(s):
            ctx.inc(s.dma_start(
                out=kvd_sb[:, :].rearrange("p (m f) -> p m f", m=8),
                in_=kvdb.rearrange("(m p) f -> p m f", p=128)), "sDW", 16)
        block.sync(kvd_ld)

        for ci in range(NCH):
            emit_ln1(ci, sqf, osc)

            def ckv_mm(t, ci=ci):
                ctx.w(t, "sDW")
                ctx.w(t, "sDVE")
                for m in range(8):
                    for half in range(2):
                        ins = t.matmul(PS[:, half * 512:(half + 1) * 512],
                                       kvd_sb[:, m * 128:(m + 1) * 128],
                                       xnbf[:, m * CT + half * TC:
                                            m * CT + (half + 1) * TC],
                                       start=(m == 0), stop=(m == 7))
                ctx.inc(ins, "sPE")
            block.tensor(ckv_mm)

            def ckv_ev(v, ci=ci):
                ctx.w(v, "sPE")
                v.tensor_copy(ckv_own[:, ci * CT:(ci + 1) * CT], bank(0, 2))
            BV(ckv_ev)

            def ckv_st(s, ci=ci):
                ctx.w(s, "sDVE")
                ctx.inc(s.dma_start(out=ag2_in[:, ci * CT:(ci + 1) * CT],
                                    in_=ckv_own[:, ci * CT:(ci + 1) * CT]),
                        "sDX", 16)
            block.sync(ckv_st)

        def ag2(g):
            ctx.w(g, "sDX")
            ctx.inc(g.collective_compute(
                "AllGather", ALU.bypass, replica_groups=[list(range(8))],
                ins=[ag2_in.ap()], outs=[ag2_out.ap()]), "sCC", 1)
            g.wait_ge(ctx.sems["sCC"], ctx.cnt["sCC"])
        block.gpsimd(ag2)
        AR.pop()
        ctx.barrier()

        AR.push()
        kupb = AR.alloc(128, 1024, BF16)
        vupb = AR.alloc(128, 1024, BF16)
        wkrb = AR.alloc(128, 64, BF16)
        ckv_own2 = AR.alloc(128, NCH * CT, BF16)
        k97 = AR.alloc(128, T, BF16)
        ckv_kv = AR.alloc(128, T, BF16)
        ropesK_b = [AR.alloc(64, TC, BF16) for _ in range(2)]
        ropeown = AR.alloc(128, CT, BF16)
        masks_sb = AR.alloc(128, 2048, BF16)
        v_tok = AR.alloc(128, 64 * 65, BF16)
        q97 = AR.alloc(128, CT, BF16)
        kdiag = AR.alloc(128, CT, BF16)
        PT = [AR.alloc(128, CT, BF16) for _ in range(2)]
        vdiag = AR.alloc(128, 8 * 65, BF16)
        attnp = AR.alloc(128, CT, BF16)
        attn_tmp = AR.alloc(64, TC, F32)
        qsq = AR.alloc(128, TC, F32)
        T1 = AR.alloc(32, TC, F32)
        T2 = AR.alloc(32, TC, F32)
        wq_r = wring_t.ap()[:, 0:512]
        wqrb = wring_t.ap()[:, 512:1024]
        kvd_sb2 = wring_t.ap()[:, 1024:2048]

        def attn_ld0(s):
            ctx.w(s, "sPE")
            ctx.w(s, "sDVE")
            ctx.w(s, "sACT")
            ctx.inc(s.dma_start(out=kupb[:, :], in_=kupb_d[:, :]), "sDW", 16)
            ctx.inc(s.dma_start(out=vupb[:, :], in_=vupb_d[:, :]), "sDW", 16)
            ctx.inc(s.dma_start(out=wkrb[:, :], in_=wkrb_d[:, :]), "sDW", 16)
            ctx.inc(s.dma_start(
                out=wqrb.rearrange("p (m f) -> p m f", m=8),
                in_=wqrb_d.rearrange("(m p) f -> p m f", p=128)), "sDW", 16)
            ctx.inc(s.dma_start(out=masks_sb[:, :], in_=d_in["masksbf"]), "sDW", 16)
            ctx.inc(s.dma_start(
                out=kvd_sb2.rearrange("p (m f) -> p m f", m=8),
                in_=kvdb.rearrange("(m p) f -> p m f", p=128)), "sDW", 16)
        block.sync(attn_ld0)

        sqa = k97.bitcast(F32)
        osca = ckv_kv[:, 0:1024].bitcast(F32)

        for ci in range(NCH):
            emit_ln1(ci, sqa, osca)

            def ckv_mm2(t, ci=ci):
                ctx.w(t, "sDW")
                ctx.w(t, "sDVE")
                for m in range(8):
                    for half in range(2):
                        ins = t.matmul(PS[:, half * 512:(half + 1) * 512],
                                       kvd_sb2[:, m * 128:(m + 1) * 128],
                                       xnbf[:, m * CT + half * TC:
                                            m * CT + (half + 1) * TC],
                                       start=(m == 0), stop=(m == 7))
                ctx.inc(ins, "sPE")
            block.tensor(ckv_mm2)

            def ckv_ev2(v, ci=ci):
                ctx.w(v, "sPE")
                v.tensor_copy(ckv_own2[:, ci * CT:(ci + 1) * CT], bank(0, 2))
            BV(ckv_ev2)

        for ci in range(NCH):
            emit_ln1(ci, sqa, osca)

            def kv_loads(g, ci=ci):
                ctx.w(g, "sACT")
                ctx.w(g, "sPE")
                ctx.w(g, "sDVE")
                for kb in range(8):
                    rank = kb if ci == 0 else 7 - kb
                    ctx.inc(g.dma_start(
                        out=ckv_kv[:, kb * CT:(kb + 1) * CT],
                        in_=ag2_out[rank * 128:(rank + 1) * 128,
                                    ci * CT:(ci + 1) * CT]), "sDG", 16)
            block.gpsimd(kv_loads)

            def ro_ld(s, ci=ci):
                ctx.w(s, "sPE")
                ctx.w(s, "sACT")
                ctx.inc(s.dma_start(
                    out=ropeown[:, :],
                    in_=d_in["ropesbf_own"][:, ci * CT:(ci + 1) * CT]),
                    "sDX", 16)
            block.sync(ro_ld)

            def kq_init(v, ci=ci):
                ctx.w(v, "sDG")
                ctx.w(v, "sDX")
                ctx.w(v, "sPE")
                v.memset(k97[0:32, :], 0.0)
                v.memset(q97[0:32, :], 0.0)
                v.memset(kdiag[0:32, :], 0.0)
                v.memset(kdiag[0:1, :], 1.0)
                v.memset(v_tok[:, :], 1.0)
                v.memset(vdiag[:, :], 1.0)
            BV(kq_init)

            def aug_ld(g, ci=ci):
                ctx.w(g, "sDVE")
                ctx.inc(g.dma_start(out=k97[0:4, :],
                                    in_=d_in["kaug"][ci * 4:(ci + 1) * 4, :]),
                        "sDG", 16)
                ctx.inc(g.dma_start(out=q97[0:4, :],
                                    in_=d_in["qaug"][:, :]), "sDG", 16)
            block.gpsimd(aug_ld)

            for s_ in range(16):
                def kr_ld(s, s_=s_):
                    ctx.w(s, "sDVE")
                    ctx.inc(s.dma_start(
                        out=ropesK_b[s_ % 2][:, :],
                        in_=d_in["ropesbf"][64:128, s_ * 512:(s_ + 1) * 512]),
                        "sDX", 16)
                block.sync(kr_ld)

                def kr_mm(t, s_=s_):
                    ctx.w(t, "sDVE")
                    ctx.w(t, "sDG")
                    ctx.inc(t.matmul(
                        PS[0:64, (s_ % 2) * 1024:(s_ % 2) * 1024 + 512],
                        wkrb[:, 0:64],
                        ckv_kv[:, s_ * 512:(s_ + 1) * 512],
                        start=True, stop=True), "sPE")
                block.tensor(kr_mm)

                def kr_rot(v, s_=s_):
                    ctx.w(v, "sPE")
                    ctx.w(v, "sDX")
                    b0 = (s_ % 2) * 1024
                    v.tensor_mul(T1[:, 0:TC], PS[0:32, b0:b0 + 512],
                                 ropesK_b[s_ % 2][0:32, :])
                    v.tensor_mul(T2[:, 0:TC], PS[32:64, b0:b0 + 512],
                                 ropesK_b[s_ % 2][32:64, :])
                    v.tensor_add(k97[32:64, s_ * 512:(s_ + 1) * 512],
                                 T1[:, 0:TC], T2[:, 0:TC])
                BV(kr_rot)

            for half in range(2):
                def kdr_mm(t, half=half, ci=ci):
                    ctx.w(t, "sDVE")
                    ctx.inc(t.matmul(
                        PS[0:64, half * 1024:half * 1024 + 512],
                        wkrb[:, 0:64],
                        ckv_own2[:, ci * CT + half * TC:
                                 ci * CT + (half + 1) * TC],
                        start=True, stop=True), "sPE")
                block.tensor(kdr_mm)

                def kdr_rot(v, half=half):
                    ctx.w(v, "sPE")
                    b0 = half * 1024
                    v.tensor_mul(T1[:, 0:TC], PS[0:32, b0:b0 + 512],
                                 ropeown[64:96, half * TC:(half + 1) * TC])
                    v.tensor_mul(T2[:, 0:TC], PS[32:64, b0:b0 + 512],
                                 ropeown[96:128, half * TC:(half + 1) * TC])
                    v.tensor_add(kdiag[32:64, half * TC:(half + 1) * TC],
                                 T1[:, 0:TC], T2[:, 0:TC])
                BV(kdr_rot)

            for half in range(2):
                def qr_mm(t, half=half):
                    ctx.w(t, "sDVE")
                    ctx.w(t, "sDW")
                    for m in range(8):
                        ins = t.matmul(
                            PS[0:64, half * 1024:half * 1024 + 512],
                            wqrb[:, m * 64:(m + 1) * 64],
                            xnbf[:, m * CT + half * TC:
                                 m * CT + (half + 1) * TC],
                            start=(m == 0), stop=(m == 7))
                    ctx.inc(ins, "sPE")
                block.tensor(qr_mm)

                def qr_rot(v, half=half):
                    ctx.w(v, "sPE")
                    b0 = half * 1024
                    v.tensor_mul(T1[:, 0:TC], PS[0:32, b0:b0 + 512],
                                 ropeown[0:32, half * TC:(half + 1) * TC])
                    v.tensor_mul(T2[:, 0:TC], PS[32:64, b0:b0 + 512],
                                 ropeown[32:64, half * TC:(half + 1) * TC])
                    v.tensor_add(q97[32:64, half * TC:(half + 1) * TC],
                                 T1[:, 0:TC], T2[:, 0:TC])
                BV(qr_rot)

            q2r = rows2[0:1, 0:1024]
            kn2 = rows2[0:1, 1024:2048]
            kmaxc = rows2[0:1, 2047:2048]

            for h in range(NH):
                def q_ld(s, h=h):
                    ctx.w(s, "sPE")
                    ctx.inc(s.dma_start(
                        out=wq_r.rearrange("p (m f) -> p m f", m=8),
                        in_=wqb_d[:, h * 64:(h + 1) * 64].rearrange(
                            "(m p) f -> p m f", p=128)), "sDW", 16)
                block.sync(q_ld)

                def q_mm(t, h=h):
                    ctx.w(t, "sDW")
                    ctx.w(t, "sACT")
                    ctx.w(t, "sDVE")
                    for m in range(8):
                        for half in range(2):
                            ins = t.matmul(
                                PS[64:128, half * 512:half * 512 + 512],
                                wq_r[:, m * 64:(m + 1) * 64],
                                xnbf[:, m * CT + half * TC:
                                     m * CT + (half + 1) * TC],
                                start=(m == 0), stop=(m == 7))
                    ctx.inc(ins, "sPE")
                block.tensor(q_mm)

                def q_ev(a, h=h):
                    ctx.w(a, "sPE")
                    a.activation(q97[64:128, :], PS[64:128, 0:1024], AF.Copy,
                                 scale=float(S96))
                BS(q_ev)

                for half in range(2):
                    def q_sq(a, half=half):
                        ctx.w(a, "sDVE")
                        ctx.w(a, "sPE")
                        if half == 0:
                            ctx.w(a, "sACT")
                        a.activation(qsq[64:128, :],
                                     q97[64:128, half * TC:(half + 1) * TC],
                                     AF.Square)
                        a.activation(qsq[32:64, :],
                                     q97[32:64, half * TC:(half + 1) * TC],
                                     AF.Square)
                    BS(q_sq)

                    def q_n(t, half=half):
                        ctx.w(t, "sACT")
                        t.matmul(PS[0:1, 4 * 512 + half * 512:
                                     4 * 512 + half * 512 + 512],
                                 ones_col[64:128, 0:1], qsq[64:128, :],
                                 start=True, stop=False)
                        ins = t.matmul(PS[0:1, 4 * 512 + half * 512:
                                          4 * 512 + half * 512 + 512],
                                       ones_col[32:64, 0:1], qsq[32:64, :],
                                       start=False, stop=True)
                        ctx.inc(ins, "sPE")
                    block.tensor(q_n)

                def q_n2(a, h=h):
                    ctx.w(a, "sPE")
                    a.activation(q2r, PS[0:1, 4 * 512:4 * 512 + 1024], AF.Sqrt)
                BS(q_n2)

                kev_h = {}
                ksq_h = {}
                knm_h = {}
                knx_h = {}
                kev_base = ctx.cnt["sACT"]
                knx_base = ctx.cnt["sDVE"]
                knm_base = ctx.cnt["sPE"]
                for s_ in range(16):
                    def k_mm(t, h=h, s_=s_):
                        ctx.w(t, "sACT", kev_h.get(s_ - 2, kev_base))
                        if s_ == 0:
                            ctx.w(t, "sACT")
                            ctx.w(t, "sDVE")
                        ctx.inc(t.matmul(
                            PS[64:128, (s_ % 2) * 512:(s_ % 2) * 512 + 512],
                            kupb[:, h * 64:(h + 1) * 64],
                            ckv_kv[:, s_ * 512:(s_ + 1) * 512],
                            start=True, stop=True), "sPE")
                        knm_h[("k", s_)] = ctx.cnt["sPE"]
                    block.tensor(k_mm)

                    def k_ev(a, s_=s_):
                        ctx.w(a, "sPE", knm_h[("k", s_)])
                        a.activation(k97[64:128, s_ * 512:(s_ + 1) * 512],
                                     PS[64:128, (s_ % 2) * 512:
                                        (s_ % 2) * 512 + 512], AF.Copy)
                        a.activation(qsq[64:128, (s_ % 2) * 0:512] if False
                                     else qsq[64:128, 0:512],
                                     k97[64:128, s_ * 512:(s_ + 1) * 512],
                                     AF.Square)
                        a.activation(qsq[32:64, 0:512],
                                     k97[32:64, s_ * 512:(s_ + 1) * 512],
                                     AF.Square)
                        kev_h[s_] = ctx.cnt["sACT"]
                    BS(k_ev)

                    def kn_mm(t, s_=s_):
                        ctx.w(t, "sACT", kev_h[s_])
                        bnk = 4 if s_ % 2 == 0 else 5
                        ctx.w(t, "sDVE", knx_h.get(s_ - 2, knx_base))
                        t.matmul(PS[0:1, bnk * 512:bnk * 512 + 512],
                                 ones_col[64:128, 0:1], qsq[64:128, :],
                                 start=True, stop=False)
                        ins = t.matmul(PS[0:1, bnk * 512:bnk * 512 + 512],
                                       ones_col[32:64, 0:1], qsq[32:64, :],
                                       start=False, stop=True)
                        knm_h[s_] = ctx.inc(ins, "sPE")
                    block.tensor(kn_mm)

                    def kn_max(v, s_=s_):
                        ctx.w(v, "sPE", knm_h[s_])
                        bnk = 4 if s_ % 2 == 0 else 5
                        if s_ == 0:
                            v.tensor_copy(kn2[:, 0:512],
                                          PS[0:1, bnk * 512:bnk * 512 + 512])
                        else:
                            v.tensor_max(kn2[:, 0:512], kn2[:, 0:512],
                                         PS[0:1, bnk * 512:bnk * 512 + 512])
                        knx_h[s_] = ctx.cnt["sDVE"]
                    BV(kn_max)

                def kmax_fin(v):
                    v.reduce_max(kmaxc, kn2[:, 0:512], AX.X)
                BV(kmax_fin)

                def kmax_sq(a):
                    ctx.w(a, "sDVE")
                    a.activation(kmaxc, kmaxc, AF.Sqrt)
                BS(kmax_sq)

                def mrow(v, h=h):
                    ctx.w(v, "sACT")
                    v.tensor_scalar_mul(kmaxc, kmaxc, -1.0)
                    v.tensor_scalar_mul(q2r, q2r, kmaxc[0:1, 0:1])
                    v.tensor_copy(q97[0:1, :], q2r)
                BV(mrow)

                def kd_mm(t, h=h, ci=ci):
                    ctx.w(t, "sDVE")
                    ctx.w(t, "sACT")
                    for half in range(2):
                        ins = t.matmul(
                            PS[64:128, half * 512:half * 512 + 512],
                            kupb[:, h * 64:(h + 1) * 64],
                            ckv_own2[:, ci * CT + half * TC:
                                     ci * CT + (half + 1) * TC],
                            start=True, stop=True)
                    ctx.inc(ins, "sPE")
                block.tensor(kd_mm)

                def kd_ev(a, h=h):
                    ctx.w(a, "sPE")
                    a.activation(kdiag[64:128, :], PS[64:128, 0:1024], AF.Copy)
                BS(kd_ev)

                for tb4 in range(2):
                    def vd_mm(t, h=h, ci=ci, tb4=tb4):
                        ctx.w(t, "sDVE")
                        ctx.w(t, "sACT")
                        for j in range(4):
                            tb = tb4 * 4 + j
                            ins = t.matmul(PS[:, 5 * 512 + j * 64:
                                              5 * 512 + (j + 1) * 64],
                                           ckv_own2[:, ci * CT + tb * 128:
                                                    ci * CT + (tb + 1) * 128],
                                           vupb[:, h * 64:(h + 1) * 64],
                                           start=True, stop=True)
                        ctx.inc(ins, "sPE")
                    block.tensor(vd_mm)

                    def vd_ev(v, tb4=tb4):
                        ctx.w(v, "sPE")
                        v.tensor_copy(
                            vdiag[:, tb4 * 4 * 65:tb4 * 4 * 65 + 260].rearrange(
                                "p (b c) -> p b c", b=4)[:, :, 0:64],
                            PS[:, 5 * 512:5 * 512 + 256].rearrange(
                                "p (b c) -> p b c", b=4))
                    BV(vd_ev)

                for tb4 in range(16):
                    def v_mm(t, tb4=tb4, h=h):
                        ctx.w(t, "sDVE")
                        for j in range(4):
                            tb = tb4 * 4 + j
                            ins = t.matmul(PS[:, 5 * 512 + j * 64:
                                              5 * 512 + (j + 1) * 64],
                                           ckv_kv[:, tb * 128:(tb + 1) * 128],
                                           vupb[:, h * 64:(h + 1) * 64],
                                           start=True, stop=True)
                        ctx.inc(ins, "sPE")
                    block.tensor(v_mm)

                    def v_ev(v, tb4=tb4):
                        ctx.w(v, "sPE")
                        v.tensor_copy(
                            v_tok[:, tb4 * 4 * 65:tb4 * 4 * 65 + 260].rearrange(
                                "p (b c) -> p b c", b=4)[:, :, 0:64],
                            PS[:, 5 * 512:5 * 512 + 256].rearrange(
                                "p (b c) -> p b c", b=4))
                    BV(v_ev)

                smm_h = {}
                exp_h = {}
                omm_h = {}
                smm_base = ctx.cnt["sACT"]
                omm_base = ctx.cnt["sPE"]
                for kvb in range(64):
                    def s_mm(t, kvb=kvb):
                        ctx.w(t, "sACT", exp_h.get(kvb - 2, smm_base))
                        if kvb == 0:
                            ctx.w(t, "sACT")
                            ctx.w(t, "sDVE")
                            ctx.w(t, "sDG")
                        b0 = (kvb % 2) * 2
                        for qt in range(2):
                            ins = t.matmul(PS[:, (b0 + qt) * 512:
                                              (b0 + qt + 1) * 512],
                                           k97[:, kvb * 128:(kvb + 1) * 128],
                                           q97[:, qt * 512:(qt + 1) * 512],
                                           start=True, stop=True)
                        smm_h[kvb] = ctx.inc(ins, "sPE")
                    block.tensor(s_mm)

                    def s_exp(a, kvb=kvb):
                        ctx.w(a, "sPE", max(smm_h[kvb],
                                            omm_h.get(kvb - 2, omm_base)))
                        b0 = (kvb % 2) * 2
                        a.activation(PT[kvb % 2][:, :],
                                     PS[:, b0 * 512:(b0 + 2) * 512], AF.Exp)
                        exp_h[kvb] = ctx.cnt["sACT"]
                    BS(s_exp)

                    def o_mm(t, kvb=kvb):
                        ctx.w(t, "sACT", exp_h[kvb])
                        for qt in range(2):
                            ins = t.matmul(
                                PS[0:65, (6 + qt) * 512:(6 + qt) * 512 + 512],
                                v_tok[:, kvb * 65:(kvb + 1) * 65],
                                PT[kvb % 2][:, qt * 512:(qt + 1) * 512],
                                start=(kvb == 0), stop=False)
                        omm_h[kvb] = ctx.inc(ins, "sPE")
                    block.tensor(o_mm)

                for qt in range(2):
                    for b4 in range(4):
                        def d_mm(t, qt=qt, b4=b4):
                            ctx.w(t, "sACT")
                            ctx.w(t, "sDVE")
                            ctx.inc(t.matmul(PS[:, 0:512],
                                             kdiag[:, qt * 512 + b4 * 128:
                                                   qt * 512 + (b4 + 1) * 128],
                                             q97[:, qt * 512:(qt + 1) * 512],
                                             start=True, stop=True), "sPE")
                        block.tensor(d_mm)

                        def d_msk(v, b4=b4):
                            ctx.w(v, "sPE")
                            v.tensor_add(PS[:, 0:512], PS[:, 0:512],
                                         masks_sb[:, b4 * 512:(b4 + 1) * 512])
                        BV(d_msk)

                        def d_exp(a, qt=qt, b4=b4):
                            ctx.w(a, "sDVE")
                            a.activation(PT[b4 % 2][:, 0:512], PS[:, 0:512],
                                         AF.Exp)
                        BS(d_exp)

                        def d_omm(t, qt=qt, b4=b4):
                            ctx.w(t, "sACT")
                            kb = qt * 4 + b4
                            ctx.inc(t.matmul(
                                PS[0:65, (6 + qt) * 512:(6 + qt) * 512 + 512],
                                vdiag[:, kb * 65:(kb + 1) * 65],
                                PT[b4 % 2][:, 0:512],
                                start=False, stop=(b4 == 3)), "sPE")
                        block.tensor(d_omm)

                for qt in range(2):
                    rno = rows2[64:65, qt * 512:(qt + 1) * 512]

                    def n_rc(v, qt=qt):
                        ctx.w(v, "sPE")
                        v.reciprocal(rno, PS[64:65, (6 + qt) * 512:
                                             (6 + qt) * 512 + 512])
                    BV(n_rc)

                    def n_bc(t, qt=qt):
                        ctx.w(t, "sDVE")
                        ctx.inc(t.matmul(PS[0:64, 2 * 512:2 * 512 + 512],
                                         ones3[64:65, 0:64], rno,
                                         start=True, stop=True), "sPE")
                    block.tensor(n_bc)

                    def n_ev(a, qt=qt):
                        ctx.w(a, "sPE")
                        a.activation(attn_tmp[:, 0:512],
                                     PS[0:64, (6 + qt) * 512:(6 + qt) * 512 + 512],
                                     AF.Copy)
                    BS(n_ev)

                    def n_mul(v, qt=qt, h=h):
                        ctx.w(v, "sACT")
                        ctx.w(v, "sPE")
                        v.tensor_mul(
                            attnp[(h % 2) * 64:(h % 2) * 64 + 64,
                                  qt * 512:(qt + 1) * 512],
                            attn_tmp[:, 0:512], PS[0:64, 2 * 512:2 * 512 + 512])
                    BV(n_mul)

                if h % 2 == 1:
                    hp = h // 2

                    def wo_ld(s, hp=hp):
                        ctx.w(s, "sPE")
                        ctx.w(s, "sDVE")
                        ctx.inc(s.dma_start(
                            out=wo_ring[hp % 2][:, 0:1024],
                            in_=wob_d[hp * 128:(hp + 1) * 128, :]), "sDW", 16)
                    block.sync(wo_ld)

                    for om in range(8):
                        def wo_mm(t, om=om, hp=hp):
                            ctx.w(t, "sDW")
                            ctx.w(t, "sDVE")
                            for half in range(2):
                                ins = t.matmul(
                                    PS[:, (2 + half) * 512:(3 + half) * 512],
                                    wo_ring[hp % 2][:, om * 128:(om + 1) * 128],
                                    attnp[:, half * TC:(half + 1) * TC],
                                    start=True, stop=True)
                            ctx.inc(ins, "sPE")
                        block.tensor(wo_mm)

                        def wo_add(v, om=om, ci=ci):
                            ctx.w(v, "sPE")
                            rv = res[:, col(ci, om, 0):col(ci, om, 0) + CT]
                            v.tensor_add(rv, rv, bank(2, 2))
                        BV(wo_add)
        AR.pop()
        ctx.barrier()
        if upto == "attn":
            emit_out()
            return nc

        for ci in range(NCH):
            emit_ffn(ci)
        ctx.barrier()

        emit_out()
    return nc


def kernel(**inputs):
    per_core = make_host_inputs(inputs)
    nc = build_nc()
    r = run_bass_kernel_spmd(nc, per_core, core_ids=list(range(8)))
    return assemble_output(r.results)


# version 04
# speedup vs baseline: 1.2839x; 1.0094x over previous
import numpy as np
import ml_dtypes
import concourse.bass as bass
import concourse.mybir as mybir
from concourse.bass_utils import run_bass_kernel_spmd

F32 = mybir.dt.float32
BF16 = mybir.dt.bfloat16
AF = mybir.ActivationFunctionType
ALU = mybir.AluOpType
AX = mybir.AxisListType

D = 1024
T = 8192
B = 2
DI = 2048
NST = 16
E = 8
FF = 4096
NH = 16
HD = 64
DR = 32
DC = 128
CT = 1024
NCH = 2
TCORE = NCH * CT
TC = 512
DGRP = 4
LN_EPS = 1e-5
MASKVAL = -30000.0
S96 = float(1.0 / np.sqrt(np.float64(HD + DR)))

UPTO = "all"


def _chunks_of_core(k):
    return [(0, k), (1, 7 - k)]


def _fm(xslice):
    t, d = xslice.shape
    return np.ascontiguousarray(
        xslice.reshape(t, d // 128, 128).transpose(2, 1, 0).reshape(128, -1))


def make_host_inputs(inputs):
    x = np.asarray(inputs["L0_attn_a_x"], np.float32)
    inv_freq = (1.0 / 10000.0 ** (np.arange(0, DR, 2, dtype=np.float32) / DR)).astype(np.float32)
    fr = np.outer(np.arange(T, dtype=np.float32), inv_freq)
    cos = np.cos(fr.astype(np.float64)).astype(np.float32).T
    sin = np.sin(fr.astype(np.float64)).astype(np.float32).T
    ropes = np.zeros((128, T), np.float32)
    ropes[0:16] = cos * np.float32(S96)
    ropes[16:32] = cos * np.float32(S96)
    ropes[32:48] = -sin * np.float32(S96)
    ropes[48:64] = sin * np.float32(S96)
    ropes[64:80] = cos
    ropes[80:96] = cos
    ropes[96:112] = -sin
    ropes[112:128] = sin
    masks = np.zeros((128, 4 * 512), np.float32)
    ii = np.arange(128)[:, None]
    jj = np.arange(512)[None, :]
    for di, delta in enumerate((0, 128, 256, 384)):
        masks[:, di * 512:(di + 1) * 512] = np.where(ii + delta > jj, MASKVAL, 0.0)
    onehots = np.zeros((48, 16 * 128), np.float32)
    for n in range(16):
        onehots[n, n * 128:(n + 1) * 128] = 1.0
        onehots[32 + n, n * 128:(n + 1) * 128] = 1.0
    conv_w = np.asarray(inputs["L0_attn_a_conv_w"], np.float32)
    conv_w_pd = np.ascontiguousarray(
        conv_w[:, 0, :].reshape(16, 128, 4).transpose(1, 0, 2).reshape(128, 64))
    conv_b_pd = np.ascontiguousarray(
        np.asarray(inputs["L0_attn_a_conv_b"], np.float32).reshape(16, 128).T)
    A_pd = np.ascontiguousarray(
        np.asarray(inputs["L0_attn_a_A_log"], np.float32).reshape(16, 128, 16)
        .transpose(1, 0, 2).reshape(128, 256))
    D_pd = np.ascontiguousarray(
        np.asarray(inputs["L0_attn_a_D_skip"], np.float32).reshape(16, 128).T)
    gamma_pd = np.ascontiguousarray(
        np.asarray(inputs["L0_attn_a_gamma"], np.float32).reshape(8, 128).T)
    beta_pd = np.ascontiguousarray(
        np.asarray(inputs["L0_attn_a_beta"], np.float32).reshape(8, 128).T)
    wx = np.asarray(inputs["L0_attn_a_W_x"], np.float32)
    wx_ext = np.zeros((DI, 65), np.float32)
    wx_ext[:, 0:16] = wx[:, 0:16]
    wx_ext[:, 32:48] = wx[:, 16:32]
    wx_ext[:, 64] = wx[:, 32]
    wqr = np.asarray(inputs["L0_attn_b_W_q_rope"], np.float32)
    wqr_ext = np.ascontiguousarray(np.concatenate(
        [wqr[:, 0:16], wqr[:, 16:32], wqr[:, 16:32], wqr[:, 0:16]], 1))
    wkr = np.asarray(inputs["L0_attn_b_W_k_rope"], np.float32)
    wkr_ext = np.ascontiguousarray(np.concatenate(
        [wkr[:, 0:16], wkr[:, 16:32], wkr[:, 16:32], wkr[:, 0:16]], 1))

    shared = {
        "masksbf": masks.astype(ml_dtypes.bfloat16),
        "ropesbf": ropes.astype(ml_dtypes.bfloat16),
        "onehots": onehots,
        "conv_w_pd": conv_w_pd, "conv_b_pd": conv_b_pd,
        "A_pd": A_pd, "D_pd": D_pd, "gamma_pd": gamma_pd, "beta_pd": beta_pd,
        "wx_ext": wx_ext, "wqr_ext": wqr_ext, "wkr_ext": wkr_ext,
        "neg1": np.full((128, 8), -1.0, np.float32),
        "ident": np.eye(128, dtype=np.float32),
        "w_in": np.asarray(inputs["L0_attn_a_W_in"], np.float32),
        "w_out": np.asarray(inputs["L0_attn_a_W_out"], np.float32),
        "gate_w": np.asarray(inputs["L0_ffn_gate_w"], np.float32),
        "w1": np.asarray(inputs["L0_ffn_w1"], np.float32).reshape(E * D, FF),
        "w2": np.asarray(inputs["L0_ffn_w2"], np.float32).reshape(E * FF, D),
        "w_kvd": np.asarray(inputs["L0_attn_b_W_kv_down"], np.float32),
        "w_kup": np.asarray(inputs["L0_attn_b_W_k_up"], np.float32),
        "w_vup": np.asarray(inputs["L0_attn_b_W_v_up"], np.float32),
        "w_q": np.asarray(inputs["L0_attn_b_W_q"], np.float32),
        "w_o": np.asarray(inputs["L0_attn_b_W_o"], np.float32),
    }
    qaug = np.zeros((4, 1024), np.float32)
    qaug[1, 0:512] = 1.0
    qaug[2, 512:1024] = 1.0
    shared["qaug"] = qaug
    per_core = []
    for k in range(8):
        m = dict(shared)
        xfm = np.empty((128, 8, NCH, CT), np.float32)
        halo = np.zeros((128, NCH, 8, 3), np.float32)
        hmask = np.zeros((128, NCH), np.float32)
        agmask = np.zeros((128, NCH * 7), np.float32)
        for ci, (b, p) in enumerate(_chunks_of_core(k)):
            f = _fm(x[b, p * CT:(p + 1) * CT, :]).reshape(128, 8, CT)
            xfm[:, :, ci, :] = f
            if p > 0:
                halo[:, ci] = _fm(x[b, p * CT - 3:p * CT, :]).reshape(128, 8, 3)
                hmask[:, ci] = 1.0
            for s in range(7):
                if s < p:
                    agmask[:, ci * 7 + s] = 1.0
        kaug = np.zeros((NCH * 4, T), np.float32)
        ropes_own = np.zeros((128, NCH * CT), np.float32)
        for ci, (b, p) in enumerate(_chunks_of_core(k)):
            kaug[ci * 4 + 0, :] = 1.0
            kaug[ci * 4 + 1, p * CT:] = MASKVAL
            kaug[ci * 4 + 2, p * CT + 512:] = MASKVAL
            sl = slice(p * CT, (p + 1) * CT)
            ropes_own[0:64, ci * CT:(ci + 1) * CT] = ropes[0:64, sl]
            ropes_own[64:128, ci * CT:(ci + 1) * CT] = ropes[64:128, sl]
        m["kaug"] = kaug
        m["ropesbf_own"] = ropes_own.astype(ml_dtypes.bfloat16)
        m["xfm"] = np.ascontiguousarray(xfm.reshape(128, -1))
        m["halo"] = np.ascontiguousarray(halo.reshape(128, -1))
        m["hmask"] = hmask
        m["agmask"] = agmask
        per_core.append(m)
    return per_core


def assemble_output(results):
    out = np.empty((B, T, D), np.float32)
    for k in range(8):
        y = results[k]["yfm"].reshape(128, 8, NCH, CT)
        for ci, (b, p) in enumerate(_chunks_of_core(k)):
            out[b, p * CT:(p + 1) * CT, :] = y[:, :, ci].transpose(2, 1, 0).reshape(CT, D)
    return out


def col(ci, g, t0):
    return g * (NCH * CT) + ci * CT + t0



_COMPUTE_OPS = {
    "tensor_copy", "tensor_tensor", "tensor_add", "tensor_sub", "tensor_mul",
    "tensor_max", "tensor_scalar", "tensor_single_scalar", "tensor_scalar_add",
    "tensor_scalar_sub", "tensor_scalar_mul", "tensor_scalar_max",
    "tensor_scalar_min", "tensor_relu", "tensor_reduce", "reduce_max",
    "reduce_sum", "scalar_tensor_tensor", "tensor_tensor_scan", "memset",
    "memzero", "activation", "mul", "add", "copy", "square", "sqrt", "sign",
    "reciprocal", "select", "copy_predicated", "transpose", "bn_stats",
    "bn_aggr", "max", "max_index", "match_replace", "stream_shuffle", "iota",
    "engine_nop", "pool", "pool_avg", "pool_max",
}


class Eng:
    def __init__(self, eng, ctx, semname):
        self._e, self._c, self._s = eng, ctx, semname

    def __getattr__(self, name):
        f = getattr(self._e, name)
        if name in _COMPUTE_OPS:
            def wrap(*a, **k):
                self._c.w(self._e, self._s)
                ins = f(*a, **k)
                ins.then_inc(self._c.sems[self._s], 1)
                self._c.cnt[self._s] += 1
                return ins
            return wrap
        return f

class Ctx:
    def __init__(self, nc, block, sems):
        self.nc, self.block, self.sems = nc, block, sems
        self.cnt = {k: 0 for k in sems}

    def inc(self, instr, name, k=1):
        if name in ("sDVE", "sACT", "sPOOL"):
            return self.cnt[name]
        instr.then_inc(self.sems[name], k)
        self.cnt[name] += k
        return self.cnt[name]

    def w(self, eng, name, val=None):
        v = self.cnt[name] if val is None else val
        if v > 0:
            eng.wait_ge(self.sems[name], v)

    BARRIER_SEMS = ("sPE", "sACT", "sDVE", "sPOOL", "sCC", "sW12", "sOUT")

    def barrier(self):
        snap = {k: self.cnt[k] for k in self.BARRIER_SEMS if k in self.cnt}
        for en in ("sync", "tensor", "vector", "scalar", "gpsimd"):
            def f(e, snap=snap):
                for nm, v in snap.items():
                    if v > 0:
                        e.wait_ge(self.sems[nm], v)
            getattr(self.block, en)(f)


class Arena:
    def __init__(self, t, nbytes):
        self.t, self.nbytes, self.off, self.stack = t, nbytes, 0, []
        self.peak = 0

    def push(self):
        self.stack.append(self.off)

    def pop(self):
        self.off = self.stack.pop()

    def alloc(self, rows, cols, dt):
        esz = 2 if dt == BF16 else 4
        self.off = (self.off + 3) // 4 * 4
        base = self.off
        self.off += cols * esz
        self.peak = max(self.peak, self.off)
        assert self.off <= self.nbytes, f"arena overflow {self.off}>{self.nbytes}"
        v = self.t.ap()[0:rows, base // 2: base // 2 + cols * (esz // 2)]
        if dt == F32:
            v = v.bitcast(F32)
        return v


def build_nc(upto=None):
    upto = upto or UPTO
    nc = bass.Bass("TRN2", target_bir_lowering=False)
    d_in = {}
    for nm, shape in [
        ("xfm", [128, 8 * NCH * CT]), ("halo", [128, NCH * 8 * 3]),
        ("hmask", [128, NCH]), ("agmask", [128, NCH * 7]),
        ("onehots", [48, 2048]),
        ("neg1", [128, 8]), ("ident", [128, 128]), ("qaug", [4, 1024]), ("kaug", [NCH * 4, T]),
        ("conv_w_pd", [128, 64]), ("conv_b_pd", [128, 16]), ("A_pd", [128, 256]),
        ("D_pd", [128, 16]), ("gamma_pd", [128, 8]), ("beta_pd", [128, 8]),
        ("wx_ext", [DI, 65]), ("wqr_ext", [D, 64]), ("wkr_ext", [DC, 64]),
        ("w_in", [D, 2 * DI]), ("w_out", [DI, D]), ("gate_w", [D, E]),
        ("w1", [E * D, FF]), ("w2", [E * FF, D]),
        ("w_kvd", [D, DC]), ("w_kup", [DC, D]), ("w_vup", [DC, D]),
        ("w_q", [D, D]), ("w_o", [D, D]),
    ]:
        d_in[nm] = nc.dram_tensor(nm, shape, F32, kind="ExternalInput").ap()
    for nm, shape in [("masksbf", [128, 2048]), ("ropesbf", [128, T]),
                      ("ropesbf_own", [128, NCH * CT])]:
        d_in[nm] = nc.dram_tensor(nm, shape, BF16, kind="ExternalInput").ap()
    yfm = nc.dram_tensor("yfm", [128, 8 * NCH * CT], F32, kind="ExternalOutput").ap()

    d_bf = {}
    for nm, shape in [
        ("w_in", [D, 2 * DI]), ("w_out", [DI, D]), ("wx_ext", [DI, 65]),
        ("gate_w", [D, E]), ("w1", [E * D, FF]), ("w2", [E * FF, D]),
        ("w_kvd", [D, DC]), ("w_kup", [DC, D]), ("w_vup", [DC, D]),
        ("w_q", [D, D]), ("w_o", [D, D]), ("wqr_ext", [D, 64]),
        ("wkr_ext", [DC, 64]),
    ]:
        d_bf[nm] = nc.dram_tensor(nm + "_bf", shape, BF16).ap()
    ag1_in = nc.dram_tensor("ag1_in", [128, NCH * 257], F32)
    ag1_out = nc.dram_tensor("ag1_out", [8 * 128, NCH * 257], F32)
    ag2_in = nc.dram_tensor("ag2_in", [128, TCORE], BF16)
    ag2_out = nc.dram_tensor("ag2_out", [8 * 128, TCORE], BF16)

    from contextlib import ExitStack
    with ExitStack() as _es:
        res = _es.enter_context(nc.sbuf_tensor("res", [128, 8 * NCH * CT], F32))
        arena_t = _es.enter_context(nc.sbuf_tensor("arena", [128, 40960], BF16))
        xnbf = _es.enter_context(nc.sbuf_tensor("xnbf", [128, 8 * CT], BF16))
        rows1 = _es.enter_context(nc.sbuf_tensor("rows1", [66, NCH * CT], F32))
        rows2 = _es.enter_context(nc.sbuf_tensor("rows2", [66, NCH * CT], F32))
        rows3 = _es.enter_context(nc.sbuf_tensor("rows3", [66, NCH * CT], F32))
        small = _es.enter_context(nc.sbuf_tensor("small", [128, 700], F32))
        oneh = _es.enter_context(nc.sbuf_tensor("oneh", [48, 2048], F32))
        ones3 = _es.enter_context(nc.sbuf_tensor("ones3", [66, 128], F32))
        hstart = _es.enter_context(nc.sbuf_tensor("hstart", [128, 256], F32))
        pub = _es.enter_context(nc.sbuf_tensor("pub", [128, NCH * 257], F32))
        pred = _es.enter_context(nc.sbuf_tensor("pred", [128, 257], F32))
        wring_t = _es.enter_context(nc.sbuf_tensor("wring", [128, 4096], BF16))
        PS = _es.enter_context(nc.psum_tensor("PS", [128, 4096], F32))
        _sems = [_es.enter_context(nc.semaphore("s%d" % i)) for i in range(13)]
        _sems13 = _sems
        _1, _2, _3, _4, _5, _6, _7, _8, _9, _10 = _sems[:10]
        _11, _12, _13 = _sems[10:]
        block = _es.enter_context(nc.Block())
        def BV(fn):
            block.vector(lambda v: fn(Eng(v, ctx, "sDVE")))

        def BS(fn):
            block.scalar(lambda a: fn(Eng(a, ctx, "sACT")))

        def BG(fn):
            block.gpsimd(lambda g: fn(Eng(g, ctx, "sPOOL")))

        ctx = Ctx(nc, block, {"sDW": _1, "sDX": _2, "sPE": _3, "sACT": _4,
                              "sDVE": _5, "sPOOL": _6, "sOUT": _7, "sCC": _8, "sDG": _9, "sW12": _10,
                              "sLA": _11, "sLB": _12, "sLD": _13})
        AR = Arena(arena_t, 81920)
        WR = Arena(wring_t, 8192)

        def bank(i, n=1):
            return PS[:, i * 512:(i + n) * 512]

        sm = small.ap()
        convw_sb = sm[:, 0:64]
        convb_sb = sm[:, 64:80]
        A_sb = sm[:, 80:336]
        D_sb = sm[:, 336:352]
        gam_sb = sm[:, 352:360]
        bet_sb = sm[:, 360:368]
        ones_col = sm[:, 368:369]
        eps_col = sm[:, 369:370]
        hmask_sb = sm[:, 370:372]
        agm_sb = sm[:, 372:386]
        sdt_sb = sm[:, 386:388]
        neg1_sb = sm[:, 388:396]
        tiny = sm[:, 400:700]

        def p0_sync(s):
            for dst, src in [
                (res[:, :], d_in["xfm"]), (oneh[:, :], d_in["onehots"]),
                (convw_sb, d_in["conv_w_pd"]), (convb_sb, d_in["conv_b_pd"]),
                (A_sb, d_in["A_pd"]), (D_sb, d_in["D_pd"]),
                (gam_sb, d_in["gamma_pd"]), (bet_sb, d_in["beta_pd"]),
                (hmask_sb, d_in["hmask"]), (agm_sb, d_in["agmask"]),
                (neg1_sb, d_in["neg1"]),
            ]:
                ctx.inc(s.dma_start(out=dst, in_=src), "sDX", 16)
        block.sync(p0_sync)

        def p0_vec(v):
            ctx.w(v, "sDX")
            v.memset(ones_col, 1.0)
            v.memset(eps_col, float(LN_EPS))
            v.memset(ones3[:, :], 1.0)
            ctx.inc(v.memset(hstart[:, :], 0.0), "sDVE")
        BV(p0_vec)

        def p0_scalar(sc):
            ctx.w(sc, "sDVE")
            sc.activation(A_sb, A_sb, AF.Exp)
            ctx.inc(sc.mul(A_sb, A_sb, -1.0), "sACT")
        BS(p0_scalar)

        def p0_preconv(g):
            for nm in ["w_in", "w_out", "wx_ext", "gate_w", "w_kvd", "w_kup",
                       "w_vup", "w_q", "w_o", "wqr_ext", "wkr_ext"]:
                srct, dst = d_in[nm], d_bf[nm]
                ctx.inc(g.dma_start(out=dst[:, :], in_=srct[:, :]), "sDG", 16)
        BG(p0_preconv)
        ctx.barrier()

        w12_done = [0]
        if upto in ("ffn1", "attn", "all"):
            def p0_preconv2(g):
                for nm in ("w1", "w2"):
                    srct, dst = d_in[nm], d_bf[nm]
                    rows = srct.shape[0]
                    for r0 in range(0, rows, rows // 4):
                        last = (nm == "w2" and r0 + rows // 4 >= rows)
                        ctx.inc(g.dma_start(out=dst[r0:r0 + rows // 4, :],
                                            in_=srct[r0:r0 + rows // 4, :]),
                                "sW12" if last else "sDG", 16)
                w12_done[0] = ctx.cnt["sW12"]
            BG(p0_preconv2)

        winb = d_bf["w_in"]
        woutb = d_bf["w_out"]
        wxb = d_bf["wx_ext"]

        wx_sb = AR.alloc(128, 16 * 65, BF16)
        ident_sb = AR.alloc(128, 128, F32)
        wi_ring = [WR.alloc(128, 1024, BF16) for _ in range(2)]
        wo_ring = [WR.alloc(128, 1024, BF16) for _ in range(2)]

        preconv_done = ctx.cnt["sDG"]

        def p_load_wx(s):
            ctx.w(s, "sDG", preconv_done)
            for di in range(16):
                ctx.inc(s.dma_start(out=wx_sb[:, di * 65:(di + 1) * 65],
                                    in_=wxb[di * 128:(di + 1) * 128, :]), "sDW", 16)
            ctx.inc(s.dma_start(out=ident_sb, in_=d_in["ident"]), "sDW", 16)
        block.sync(p_load_wx)

        def bcast(t, dst, src_row, tn):
            bp = src_row.base_partition()
            onr = ones3[bp:bp + 1, 0:128]
            ins = None
            for i in range((tn + 511) // 512):
                c0, c1 = i * 512, min(tn, (i + 1) * 512)
                ins = t.matmul(dst[:, c0:c1], onr, src_row[:, c0:c1],
                               start=True, stop=True)
            return ins

        def win_slice(di, zhalf=False):
            c0 = (DI if zhalf else 0) + di * 128
            return winb[:, c0:c0 + 128].rearrange("(m p) f -> p m f", p=128)

        def emit_ln2(srcv, tn, mu_r, rs_r, mui_r, rsi_r, dstfn, sq, lnt,
                     mask_col=None):
            def on_sq(a):
                ctx.w(a, "sDVE")
                for i in range(8):
                    ins = a.activation(sq[:, i * tn:(i + 1) * tn], srcv[i], AF.Square)
                ctx.inc(ins, "sACT")
            BS(on_sq)

            def on_pe(t):
                ctx.w(t, "sACT")
                ctx.w(t, "sDVE")
                for i in range(8):
                    t.matmul(PS[0:1, 0:tn], ones_col, srcv[i],
                             start=(i == 0), stop=(i == 7))
                for i in range(8):
                    ins = t.matmul(PS[32:33, 0:tn], ones_col,
                                   sq[:, i * tn:(i + 1) * tn],
                                   start=(i == 0), stop=(i == 7))
                ctx.inc(ins, "sPE")
            block.tensor(on_pe)

            def on_rows(v):
                ctx.w(v, "sPE")
                v.tensor_scalar_mul(mu_r, PS[0:1, 0:tn], 1.0 / D)
                v.tensor_scalar_mul(rs_r, PS[32:33, 0:tn], 1.0 / D)
                tmp = PS[32:33, 3 * 512:3 * 512 + tn]
                v.tensor_mul(tmp, mu_r, mu_r)
                ctx.inc(v.tensor_sub(rs_r, rs_r, tmp), "sDVE")
            BV(on_rows)

            def on_rstd(a):
                ctx.w(a, "sDVE")
                a.activation(rs_r, rs_r, AF.Ln, bias=eps_col[0:1, 0:1])
                ctx.inc(a.activation(rs_r, rs_r, AF.Exp, scale=-0.5), "sACT")
            BS(on_rstd)

            def on_bc(t):
                ctx.w(t, "sACT")
                bcast(t, bank(1), mu_r, tn)
                ctx.inc(bcast(t, bank(2), rs_r, tn), "sPE")
            block.tensor(on_bc)

            def on_ln0(v):
                ctx.w(v, "sPE")
                for i in range(8):
                    o = lnt[:, i * tn:(i + 1) * tn]
                    v.tensor_sub(o, srcv[i], bank(1)[:, 0:tn])
                    ins = v.tensor_mul(o, o, bank(2)[:, 0:tn])
                ctx.inc(ins, "sDVE")
            BV(on_ln0)

            def on_sq2(a):
                ctx.w(a, "sDVE")
                for i in range(8):
                    ins = a.activation(sq[:, i * tn:(i + 1) * tn],
                                       lnt[:, i * tn:(i + 1) * tn], AF.Square)
                ctx.inc(ins, "sACT")
            BS(on_sq2)

            def on_pe2(t):
                ctx.w(t, "sACT")
                for i in range(8):
                    t.matmul(PS[0:1, 0:tn], ones_col,
                             lnt[:, i * tn:(i + 1) * tn],
                             start=(i == 0), stop=(i == 7))
                for i in range(8):
                    ins = t.matmul(PS[32:33, 0:tn], ones_col,
                                   sq[:, i * tn:(i + 1) * tn],
                                   start=(i == 0), stop=(i == 7))
                ctx.inc(ins, "sPE")
            block.tensor(on_pe2)

            def on_rows2(v):
                ctx.w(v, "sPE")
                v.tensor_scalar_mul(mui_r, PS[0:1, 0:tn], 1.0 / D)
                v.tensor_scalar_mul(rsi_r, PS[32:33, 0:tn], 1.0 / D)
                tmp = PS[32:33, 3 * 512:3 * 512 + tn]
                v.tensor_mul(tmp, mui_r, mui_r)
                ctx.inc(v.tensor_sub(rsi_r, rsi_r, tmp), "sDVE")
            BV(on_rows2)

            def on_rstd2(a):
                ctx.w(a, "sDVE")
                a.activation(rsi_r, rsi_r, AF.Ln, bias=eps_col[0:1, 0:1])
                ctx.inc(a.activation(rsi_r, rsi_r, AF.Exp, scale=-0.5), "sACT")
            BS(on_rstd2)

            def on_bc2(t):
                ctx.w(t, "sACT")
                bcast(t, bank(1), mui_r, tn)
                ctx.inc(bcast(t, bank(2), rsi_r, tn), "sPE")
            block.tensor(on_bc2)

            def on_xn(v):
                ctx.w(v, "sPE")
                for i in range(8):
                    o = lnt[:, i * tn:(i + 1) * tn]
                    v.tensor_sub(o, o, bank(1)[:, 0:tn])
                    v.tensor_mul(o, o, bank(2)[:, 0:tn])
                    if mask_col is None:
                        ins = v.tensor_scalar(dstfn(i), o, gam_sb[:, i:i + 1],
                                              bet_sb[:, i:i + 1], ALU.mult, ALU.add)
                    else:
                        v.tensor_scalar(o, o, gam_sb[:, i:i + 1],
                                        bet_sb[:, i:i + 1], ALU.mult, ALU.add)
                        ins = v.tensor_scalar_mul(dstfn(i), o, mask_col)
                ctx.inc(ins, "sDVE")
            BV(on_xn)

        def emit_xinner_conv(ci, t0, xconv, xi_sb, stash, conv_acc):
            for di in range(16):
                ring = wi_ring[di % 2]

                def ld(s, di=di, ring=ring):
                    ctx.w(s, "sPE")
                    ctx.inc(s.dma_start(out=ring[:, 0:1024].rearrange("p (m f) -> p m f", m=8), in_=win_slice(di)),
                            "sDW", 16)
                block.sync(ld)

                def mm(t, di=di, ring=ring, t0=t0):
                    ctx.w(t, "sDW")
                    ctx.w(t, "sDVE")
                    for m in range(8):
                        ins = t.matmul(bank(3)[:, 0:TC],
                                       ring[:, m * 128:(m + 1) * 128],
                                       xnbf[:, m * CT + t0:m * CT + t0 + TC],
                                       start=(m == 0), stop=(m == 7))
                    ctx.inc(ins, "sPE")
                block.tensor(mm)

                def conv(v, di=di):
                    ctx.w(v, "sPE")
                    ctx.w(v, "sACT")
                    v.tensor_copy(xi_sb[:, 0:3], stash[:, di * 3:(di + 1) * 3])
                    v.tensor_copy(xi_sb[:, 3:3 + TC], bank(3)[:, 0:TC])
                    v.tensor_copy(stash[:, di * 3:(di + 1) * 3],
                                  bank(3)[:, TC - 3:TC])
                    v.tensor_scalar_mul(conv_acc[:, 0:TC], xi_sb[:, 0:TC],
                                        convw_sb[:, di * 4:di * 4 + 1])
                    for j in range(1, 4):
                        ins = v.scalar_tensor_tensor(
                            conv_acc[:, 0:TC], xi_sb[:, j:j + TC],
                            convw_sb[:, di * 4 + j:di * 4 + j + 1],
                            conv_acc[:, 0:TC], ALU.mult, ALU.add)
                    ctx.inc(ins, "sDVE")
                BV(conv)

                def silu(a, di=di, t0=t0):
                    ctx.w(a, "sDVE")
                    ctx.inc(a.activation(
                        xconv[:, di * CT + t0:di * CT + t0 + TC],
                        conv_acc[:, 0:TC], AF.Silu,
                        bias=convb_sb[:, di:di + 1]), "sACT")
                BS(silu)

        def emit_halo(ci, halo_f, halo_xn, stash, sq, lnt):
            def ld(s, ci=ci):
                ctx.w(s, "sACT")
                ctx.w(s, "sDVE")
                ctx.w(s, "sPOOL")
                ctx.w(s, "sPE")
                ctx.inc(s.dma_start(out=halo_f[:, :],
                                    in_=d_in["halo"][:, ci * 24:(ci + 1) * 24]),
                        "sDX", 16)
            block.sync(ld)

            def wt(v):
                ctx.w(v, "sDX")
                ctx.inc(v.engine_nop(), "sDVE")
            BV(wt)
            hv = [halo_f[:, i * 3:(i + 1) * 3] for i in range(8)]
            emit_ln2(hv, 3, tiny[0:1, 8:11], tiny[0:1, 12:15],
                     tiny[0:1, 16:19], tiny[0:1, 20:23],
                     lambda i: halo_xn[:, i * 3:(i + 1) * 3],
                     sq, lnt, mask_col=hmask_sb[:, ci:ci + 1])
            for di in range(16):
                ring = wi_ring[di % 2]

                def ld2(s, di=di, ring=ring):
                    ctx.w(s, "sPE")
                    ctx.inc(s.dma_start(out=ring[:, 0:1024].rearrange("p (m f) -> p m f", m=8), in_=win_slice(di)),
                            "sDW", 16)
                block.sync(ld2)

                def mm(t, di=di, ring=ring):
                    ctx.w(t, "sDW")
                    ctx.w(t, "sDVE")
                    for m in range(8):
                        ins = t.matmul(bank(3)[:, 0:3],
                                       ring[:, m * 128:(m + 1) * 128],
                                       halo_xn[:, m * 3:(m + 1) * 3],
                                       start=(m == 0), stop=(m == 7))
                    ctx.inc(ins, "sPE")
                block.tensor(mm)

                def ev(v, di=di):
                    ctx.w(v, "sPE")
                    ctx.inc(v.tensor_copy(stash[:, di * 3:(di + 1) * 3],
                                          bank(3)[:, 0:3]), "sDVE")
                BV(ev)

        for ci in (range(NCH) if upto != "p0" else []):
            AR.push()
            xconv = AR.alloc(128, 16 * CT, BF16)
            AR.push()
            sq_t = AR.alloc(128, 8 * TC, F32)
            ln_t = AR.alloc(128, 8 * TC, F32)
            xi_sb = AR.alloc(128, TC + 3, F32)
            stash = AR.alloc(128, 16 * 3, F32)
            halo_xn = AR.alloc(128, 8 * 3, BF16)
            halo_f = AR.alloc(128, 8 * 3, F32)
            conv_acc = AR.alloc(128, TC, F32)

            emit_halo(ci, halo_f, halo_xn, stash, sq_t, ln_t)
            for tc in range(CT // TC):
                t0 = tc * TC
                srcv = [res[:, col(ci, dh, t0):col(ci, dh, t0) + TC] for dh in range(8)]
                emit_ln2(srcv, TC,
                         rows2[0:1, ci * CT + t0:ci * CT + t0 + TC],
                         rows2[32:33, ci * CT + t0:ci * CT + t0 + TC],
                         rows3[0:1, ci * CT + t0:ci * CT + t0 + TC],
                         rows3[32:33, ci * CT + t0:ci * CT + t0 + TC],
                         lambda i, t0=t0: xnbf[:, i * CT + t0:i * CT + t0 + TC],
                         sq_t, ln_t)
                emit_xinner_conv(ci, t0, xconv, xi_sb, stash, conv_acc)

                def wx_mm(t, ci=ci, t0=t0, xconv=xconv):
                    ctx.w(t, "sACT")
                    ctx.w(t, "sDVE")
                    for di in range(16):
                        ins = t.matmul(PS[0:65, 4 * 512:4 * 512 + TC],
                                       wx_sb[:, di * 65:(di + 1) * 65],
                                       xconv[:, di * CT + t0:di * CT + t0 + TC],
                                       start=(di == 0), stop=(di == 15))
                    ctx.inc(ins, "sPE")
                block.tensor(wx_mm)

                dt_r = rows1[64:65, ci * CT + t0:ci * CT + t0 + TC]
                b_r = rows1[0:16, ci * CT + t0:ci * CT + t0 + TC]
                c_r = rows1[32:48, ci * CT + t0:ci * CT + t0 + TC]
                scr = rows2[64:65, ci * CT + t0:ci * CT + t0 + TC]

                def dtrow1(v, dt_r=dt_r, scr=scr):
                    ctx.w(v, "sPE")
                    ctx.inc(v.tensor_scalar_max(
                        scr, PS[64:65, 4 * 512:4 * 512 + TC], 0.0), "sDVE")
                BV(dtrow1)

                def dtrow2(a, dt_r=dt_r):
                    ctx.w(a, "sDVE")
                    a.activation(dt_r, PS[64:65, 4 * 512:4 * 512 + TC], AF.Abs)
                    a.activation(dt_r, dt_r, AF.Exp, scale=-1.0)
                    ctx.inc(a.activation(dt_r, dt_r, AF.Ln, bias=ones_col[0:1, 0:1]), "sACT")
                BS(dtrow2)

                def dtrow3(v, dt_r=dt_r, scr=scr, b_r=b_r, c_r=c_r):
                    ctx.w(v, "sACT")
                    v.tensor_add(dt_r, dt_r, scr)
                    v.tensor_copy(c_r, PS[32:48, 4 * 512:4 * 512 + TC])
                    ctx.inc(v.tensor_copy(b_r, PS[0:16, 4 * 512:4 * 512 + TC]),
                            "sDVE")
                BV(dtrow3)

                def dtb16(t, dt_r=dt_r):
                    ctx.w(t, "sDVE")
                    ctx.inc(t.matmul(PS[0:16, 5 * 512:5 * 512 + TC],
                                     ones3[64:65, 0:16], dt_r,
                                     start=True, stop=True), "sPE")
                block.tensor(dtb16)

                def wBmul(v, b_r=b_r):
                    ctx.w(v, "sPE")
                    ctx.inc(v.tensor_mul(b_r, b_r, PS[0:16, 5 * 512:5 * 512 + TC]),
                            "sDVE")
                BV(wBmul)
            AR.pop()
            sc_dA = [AR.alloc(128, CT, F32) for _ in range(2)]
            sc_ub = [AR.alloc(128, CT, F32) for _ in range(2)]
            sc_h = [AR.alloc(128, CT, F32) for _ in range(2)]

            def sdt_red(v, ci=ci):
                v.tensor_reduce(sm[64:65, 428:429],
                                rows1[64:65, ci * CT:(ci + 1) * CT], AX.X, ALU.add)
                ctx.inc(v.engine_nop(), "sDVE")
            BV(sdt_red)

            def sdt_bc(t, ci=ci):
                ctx.w(t, "sDVE")
                t.matmul(bank(3)[:, 0:1], ones3[64:65, 0:128], sm[64:65, 428:429],
                         start=True, stop=True)
                ctx.inc(bcast(t, bank(0, 2),
                              rows1[64:65, ci * CT:(ci + 1) * CT], CT), "sPE")
            block.tensor(sdt_bc)

            def sdt_cp(v, ci=ci):
                ctx.w(v, "sPE")
                v.tensor_copy(sdt_sb[:, ci:ci + 1], bank(3)[:, 0:1])
                ctx.inc(v.tensor_copy(pub[:, ci * 257 + 256:ci * 257 + 257],
                                      bank(3)[:, 0:1]), "sDVE")
            BV(sdt_cp)

            for n in range(16):
                def nb(t, n=n, ci=ci):
                    ctx.w(t, "sDVE")
                    ctx.w(t, "sPOOL")
                    for i in range(2):
                        ins = t.matmul(PS[:, (2 + i) * 512:(3 + i) * 512],
                                       oneh[0:16, n * 128:(n + 1) * 128],
                                       rows1[0:16, ci * CT + i * 512:
                                             ci * CT + (i + 1) * 512],
                                       start=True, stop=True)
                    ctx.inc(ins, "sPE")
                block.tensor(nb)

                for db in range(16):
                    buf = (n * 16 + db) % 2
                    dA, ub, h = sc_dA[buf], sc_ub[buf], sc_h[buf]

                    def mk_dA(a, n=n, db=db, dA=dA):
                        ctx.w(a, "sPE")
                        ctx.w(a, "sDVE")
                        ctx.inc(a.activation(
                            dA[:, 0:CT], bank(0, 2), AF.Exp,
                            scale=A_sb[:, db * 16 + n:db * 16 + n + 1]), "sACT")
                    BS(mk_dA)

                    def mk_ub(v, db=db, ub=ub, xconv=xconv):
                        ctx.w(v, "sPE")
                        ctx.inc(v.tensor_mul(ub[:, 0:CT],
                                             xconv[:, db * CT:(db + 1) * CT],
                                             bank(2, 2)), "sDVE")
                    BV(mk_ub)

                    def do_scan(v, n=n, db=db, dA=dA, ub=ub, h=h, ci=ci):
                        ctx.w(v, "sACT")
                        ctx.w(v, "sPOOL")
                        ctx.w(v, "sDVE")
                        ctx.inc(v.tensor_tensor_scan(h[:, 0:CT], dA[:, 0:CT],
                                                     ub[:, 0:CT], 0.0,
                                                     ALU.mult, ALU.add), "sDVE")
                        ctx.w(v, "sDVE")
                        ctx.inc(v.tensor_copy(
                            pub[:, ci * 257 + db * 16 + n:ci * 257 + db * 16 + n + 1],
                            h[:, CT - 1:CT]), "sDVE")
                    BV(do_scan)

            def pubfence(v):
                ctx.w(v, "sPOOL")
                ctx.w(v, "sDVE")
                ctx.inc(v.engine_nop(), "sDVE")
            BV(pubfence)
            AR.pop()

        if upto in ("p0", "p1"):
            ctx.barrier()

            def out_dma0(s):
                ctx.inc(s.dma_start(out=yfm, in_=res[:, :]), "sOUT", 16)
                s.wait_ge(ctx.sems["sOUT"], ctx.cnt["sOUT"])
            block.sync(out_dma0)
            return nc

        def ag1(g):
            ctx.w(g, "sDVE")
            ctx.inc(g.dma_start(out=ag1_in[:, :], in_=pub[:, :]), "sDG", 16)
            g.wait_ge(ctx.sems["sDG"], ctx.cnt["sDG"])
            ctx.inc(g.collective_compute(
                "AllGather", ALU.bypass, replica_groups=[list(range(8))],
                ins=[ag1_in.ap()], outs=[ag1_out.ap()]), "sCC", 1)
            g.wait_ge(ctx.sems["sCC"], ctx.cnt["sCC"])
        BG(ag1)
        ctx.barrier()

        if upto == "ag":
            def out_dma1(s):
                ctx.inc(s.dma_start(out=yfm, in_=res[:, :]), "sOUT", 16)
                s.wait_ge(ctx.sems["sOUT"], ctx.cnt["sOUT"])
            block.sync(out_dma1)
            return nc

        for ci in range(NCH):
            AR.push()
            xconv3 = AR.alloc(128, 16 * CT, BF16)
            y_acc = AR.alloc(128, DGRP * CT, F32)
            xsc = AR.alloc(128, TC, F32)
            P_t = AR.alloc(128, 256, F32)

            def h0(v):
                ctx.inc(v.memset(hstart[:, :], 0.0), "sDVE")
            BV(h0)
            for s_ in range(7):
                rank = s_ if ci == 0 else 7 - s_

                def pld(g, rank=rank, ci=ci):
                    ctx.w(g, "sDVE")
                    ctx.inc(g.dma_start(
                        out=pred[:, :],
                        in_=ag1_out[rank * 128:(rank + 1) * 128,
                                    ci * 257:(ci + 1) * 257]), "sDG", 16)
                BG(pld)

                def hv1(v):
                    ctx.w(v, "sCC")
                    ctx.w(v, "sDG")
                    ctx.inc(v.tensor_scalar_mul(P_t, A_sb, pred[:, 256:257]),
                            "sDVE")
                BV(hv1)

                def ha(a):
                    ctx.w(a, "sDVE")
                    ctx.inc(a.activation(P_t, P_t, AF.Exp), "sACT")
                BS(ha)

                def hv2(v, s_=s_, ci=ci):
                    ctx.w(v, "sACT")
                    m = agm_sb[:, ci * 7 + s_:ci * 7 + s_ + 1]
                    v.tensor_scalar_add(P_t, P_t, -1.0)
                    v.tensor_scalar_mul(P_t, P_t, m)
                    v.tensor_scalar_add(P_t, P_t, 1.0)
                    v.tensor_mul(hstart[:, :], hstart[:, :], P_t)
                    ctx.inc(v.scalar_tensor_tensor(hstart[:, :], pred[:, 0:256], m,
                                                   hstart[:, :], ALU.mult, ALU.add),
                            "sDVE")
                BV(hv2)

            mu0 = rows2[0:1, ci * CT:(ci + 1) * CT]
            rs0 = rows2[32:33, ci * CT:(ci + 1) * CT]
            mui = rows3[0:1, ci * CT:(ci + 1) * CT]
            rsi = rows3[32:33, ci * CT:(ci + 1) * CT]

            for tc in range(2):
                t0 = tc * TC

                def xb1(t, t0=t0):
                    ctx.w(t, "sDVE")
                    bcast(t, bank(1), mu0[:, t0:t0 + TC], TC)
                    bcast(t, bank(2), rs0[:, t0:t0 + TC], TC)
                    bcast(t, bank(5), mui[:, t0:t0 + TC], TC)
                    ctx.inc(bcast(t, bank(6), rsi[:, t0:t0 + TC], TC), "sPE")
                block.tensor(xb1)

                def xv1(v, t0=t0, ci=ci):
                    ctx.w(v, "sPE")
                    for i in range(8):
                        o = xsc[:, 0:TC]
                        rv = res[:, col(ci, i, t0):col(ci, i, t0) + TC]
                        xno = xnbf[:, i * CT + t0:i * CT + t0 + TC]
                        v.tensor_sub(o, rv, bank(1)[:, 0:TC])
                        v.tensor_mul(o, o, bank(2)[:, 0:TC])
                        v.tensor_add(rv, rv, o)
                        v.tensor_sub(o, o, bank(5)[:, 0:TC])
                        v.tensor_mul(o, o, bank(6)[:, 0:TC])
                        ins = v.tensor_scalar(xno, o, gam_sb[:, i:i + 1],
                                              bet_sb[:, i:i + 1], ALU.mult, ALU.add)
                    ctx.inc(ins, "sDVE")
                BV(xv1)

            AR.push()
            sq3 = AR.alloc(128, 8 * 3, F32)
            lh3 = AR.alloc(128, 8 * 3, F32)
            xi3 = AR.alloc(128, TC + 3, F32)
            st3 = AR.alloc(128, 16 * 3, F32)
            hx3 = AR.alloc(128, 8 * 3, BF16)
            hf3 = AR.alloc(128, 8 * 3, F32)
            ca3 = AR.alloc(128, TC, F32)
            emit_halo(ci, hf3, hx3, st3, sq3, lh3)
            for tc in range(2):
                emit_xinner_conv(ci, tc * TC, xconv3, xi3, st3, ca3)
            AR.pop()

            AR.push()
            y3bf = AR.alloc(128, CT, BF16)
            s3_dA = [AR.alloc(128, CT, F32) for _ in range(2)]
            s3_ub = [AR.alloc(128, CT, F32) for _ in range(2)]
            s3_h = [AR.alloc(128, CT, F32) for _ in range(2)]
            zs_t = s3_dA[0]
            t2_t = s3_dA[1]

            for g0 in range(0, 16, DGRP):
                def dtb(t, ci=ci):
                    ctx.w(t, "sACT")
                    ctx.w(t, "sDVE")
                    ctx.inc(bcast(t, bank(0, 2),
                                  rows1[64:65, ci * CT:(ci + 1) * CT], CT), "sPE")
                block.tensor(dtb)

                scn3_h = {}
                dA3_h = {}
                ya_h = {}
                scn3_base = ctx.cnt["sDVE"]
                ya_base = ctx.cnt["sPOOL"]
                for n in range(16):
                    def nb(t, n=n, ci=ci):
                        ctx.w(t, "sDVE")
                        ctx.w(t, "sPOOL")
                        for i in range(2):
                            t.matmul(PS[:, (2 + i) * 512:(3 + i) * 512],
                                     oneh[0:16, n * 128:(n + 1) * 128],
                                     rows1[0:16, ci * CT + i * 512:
                                           ci * CT + (i + 1) * 512],
                                     start=True, stop=True)
                        for i in range(2):
                            ins = t.matmul(PS[:, (4 + i) * 512:(5 + i) * 512],
                                           oneh[32:48, n * 128:(n + 1) * 128],
                                           rows1[32:48, ci * CT + i * 512:
                                                 ci * CT + (i + 1) * 512],
                                           start=True, stop=True)
                        ctx.inc(ins, "sPE")
                    block.tensor(nb)
                    nb_pe3 = ctx.cnt["sPE"]

                    for db in range(g0, g0 + DGRP):
                        idx = n * DGRP + (db - g0)
                        buf = idx % 2
                        dA, ub, h = s3_dA[buf], s3_ub[buf], s3_h[buf]

                        def mk_dA(a, n=n, db=db, dA=dA, idx=idx, nb_pe3=nb_pe3):
                            ctx.w(a, "sPE", nb_pe3)
                            ctx.w(a, "sDVE", scn3_h.get(idx - 2, scn3_base))
                            ctx.inc(a.activation(
                                dA[:, 0:CT], bank(0, 2), AF.Exp,
                                scale=A_sb[:, db * 16 + n:db * 16 + n + 1]), "sACT")
                            dA3_h[idx] = ctx.cnt["sACT"]
                        BS(mk_dA)

                        def mk_ub(v, db=db, ub=ub, idx=idx, nb_pe3=nb_pe3):
                            ctx.w(v, "sPE", nb_pe3)
                            ctx.w(v, "sPOOL", ya_h.get(idx - 2, ya_base))
                            ctx.inc(v.tensor_mul(ub[:, 0:CT],
                                                 xconv3[:, db * CT:(db + 1) * CT],
                                                 bank(2, 2)), "sDVE")
                        BV(mk_ub)

                        def do_scan(v, n=n, db=db, dA=dA, ub=ub, h=h, idx=idx):
                            ctx.w(v, "sACT", dA3_h[idx])
                            v.tensor_tensor_scan(
                                h[:, 0:CT], dA[:, 0:CT], ub[:, 0:CT],
                                hstart[:, db * 16 + n:db * 16 + n + 1],
                                ALU.mult, ALU.add)
                            v.tensor_mul(ub[:, 0:CT], h[:, 0:CT],
                                         bank(4, 2))
                            scn3_h[idx] = ctx.cnt["sDVE"]
                        BV(do_scan)

                        def y_add(g_, n=n, db=db, ub=ub, g0=g0, idx=idx):
                            ctx.w(g_, "sDVE", scn3_h[idx])
                            ya = y_acc[:, (db - g0) * CT:(db - g0 + 1) * CT]
                            if n == 0:
                                ins = g_.tensor_copy(ya, ub[:, 0:CT])
                            else:
                                ins = g_.tensor_add(ya, ya, ub[:, 0:CT])
                            ya_h[idx] = ctx.inc(ins, "sPOOL")
                        BG(y_add)

                for db in range(g0, g0 + DGRP):
                    ring = wi_ring[db % 2]

                    def zld(s, db=db, ring=ring):
                        ctx.w(s, "sPE")
                        ctx.inc(s.dma_start(out=ring[:, 0:1024].rearrange("p (m f) -> p m f", m=8),
                                            in_=win_slice(db, zhalf=True)),
                                "sDW", 16)
                    block.sync(zld)

                    def zmm(t, db=db, ring=ring):
                        ctx.w(t, "sDW")
                        ctx.w(t, "sACT")
                        ctx.w(t, "sPOOL")
                        for m in range(8):
                            for half in range(2):
                                ins = t.matmul(
                                    PS[:, half * 512:(half + 1) * 512],
                                    ring[:, m * 128:(m + 1) * 128],
                                    xnbf[:, m * CT + half * TC:
                                         m * CT + (half + 1) * TC],
                                    start=(m == 0), stop=(m == 7))
                        ctx.inc(ins, "sPE")
                    block.tensor(zmm)

                    def zsilu(a):
                        ctx.w(a, "sPE")
                        a.activation(zs_t[:, 0:TC], bank(0), AF.Silu)
                        ctx.inc(a.activation(zs_t[:, TC:CT], bank(1), AF.Silu),
                                "sACT")
                    BS(zsilu)

                    def gate(v, db=db, g0=g0):
                        ctx.w(v, "sACT")
                        ctx.w(v, "sPOOL")
                        ctx.w(v, "sPE")
                        ya = y_acc[:, (db - g0) * CT:(db - g0 + 1) * CT]
                        xc = xconv3[:, db * CT:(db + 1) * CT]
                        v.scalar_tensor_tensor(t2_t[:, 0:CT], xc,
                                               D_sb[:, db:db + 1], ya,
                                               ALU.mult, ALU.add)
                        ctx.inc(v.tensor_mul(y3bf[:, 0:CT], t2_t[:, 0:CT],
                                             zs_t[:, 0:CT]), "sDVE")
                    BV(gate)

                    def wld(s, db=db):
                        ctx.w(s, "sPE")
                        ctx.inc(s.dma_start(
                            out=wo_ring[db % 2][:, 0:1024],
                            in_=woutb[db * 128:(db + 1) * 128, :]), "sDW", 16)
                    block.sync(wld)

                    for om in range(8):
                        def wmm(t, om=om, db=db):
                            ctx.w(t, "sDW")
                            ctx.w(t, "sDVE")
                            b0 = 2 + (om % 2) * 2
                            for half in range(2):
                                ins = t.matmul(
                                    PS[:, (b0 + half) * 512:(b0 + half + 1) * 512],
                                    wo_ring[db % 2][:, om * 128:(om + 1) * 128],
                                    y3bf[:, half * TC:(half + 1) * TC],
                                    start=True, stop=True)
                            ctx.inc(ins, "sPE")
                        block.tensor(wmm)

                        def wadd(v, om=om, ci=ci):
                            ctx.w(v, "sPE")
                            b0 = 2 + (om % 2) * 2
                            rv = res[:, col(ci, om, 0):col(ci, om, 0) + CT]
                            ctx.inc(v.tensor_add(
                                rv, rv, PS[:, b0 * 512:(b0 + 2) * 512]), "sDVE")
                        BV(wadd)
            AR.pop()
            AR.pop()
        ctx.barrier()

        def emit_out():
            def out_dma(s):
                ctx.inc(s.dma_start(out=yfm, in_=res[:, :]), "sOUT", 16)
                s.wait_ge(ctx.sems["sOUT"], ctx.cnt["sOUT"])
            block.sync(out_dma)

        if upto == "mamba":
            emit_out()
            return nc

        def emit_ln1(ci, sqf, osc):
            for tc in range(2):
                t0 = tc * TC
                srcv = [res[:, col(ci, dh, t0):col(ci, dh, t0) + TC]
                        for dh in range(8)]
                mu_r = rows2[0:1, ci * CT + t0:ci * CT + t0 + TC]
                rs_r = rows2[32:33, ci * CT + t0:ci * CT + t0 + TC]

                def s_sq(a, srcv=srcv):
                    ctx.w(a, "sDVE")
                    for i in range(8):
                        a.activation(sqf[:, i * TC:(i + 1) * TC], srcv[i], AF.Square)
                BS(s_sq)

                def s_pe(t, srcv=srcv):
                    ctx.w(t, "sACT")
                    ctx.w(t, "sDVE")
                    for i in range(8):
                        t.matmul(PS[0:1, 0:TC], ones_col, srcv[i],
                                 start=(i == 0), stop=(i == 7))
                    for i in range(8):
                        ins = t.matmul(PS[32:33, 0:TC], ones_col,
                                       sqf[:, i * TC:(i + 1) * TC],
                                       start=(i == 0), stop=(i == 7))
                    ctx.inc(ins, "sPE")
                block.tensor(s_pe)

                def s_rows(v, mu_r=mu_r, rs_r=rs_r):
                    ctx.w(v, "sPE")
                    v.tensor_scalar_mul(mu_r, PS[0:1, 0:TC], 1.0 / D)
                    v.tensor_scalar_mul(rs_r, PS[32:33, 0:TC], 1.0 / D)
                    tmp = PS[32:33, 3 * 512:3 * 512 + TC]
                    v.tensor_mul(tmp, mu_r, mu_r)
                    v.tensor_sub(rs_r, rs_r, tmp)
                BV(s_rows)

                def s_rstd(a, rs_r=rs_r):
                    ctx.w(a, "sDVE")
                    a.activation(rs_r, rs_r, AF.Ln, bias=eps_col[0:1, 0:1])
                    a.activation(rs_r, rs_r, AF.Exp, scale=-0.5)
                BS(s_rstd)

                def s_bc(t, mu_r=mu_r, rs_r=rs_r):
                    ctx.w(t, "sACT")
                    bcast(t, bank(1), mu_r, TC)
                    ctx.inc(bcast(t, bank(2), rs_r, TC), "sPE")
                block.tensor(s_bc)

                def s_fin(v, srcv=srcv, t0=t0, ci=ci):
                    ctx.w(v, "sPE")
                    for i in range(8):
                        v.tensor_sub(osc[:, 0:TC], srcv[i], bank(1)[:, 0:TC])
                        v.tensor_mul(xnbf[:, i * CT + t0:i * CT + t0 + TC],
                                     osc[:, 0:TC], bank(2)[:, 0:TC])
                BV(s_fin)

        gate_sb = AR.alloc(128, 64, BF16)
        gateb = d_bf["gate_w"]
        w1b = d_bf["w1"]
        w2b = d_bf["w2"]

        def p_load_gate(s):
            ctx.inc(s.dma_start(
                out=gate_sb[:, :].rearrange("p (m e) -> p m e", m=8),
                in_=gateb.rearrange("(m p) e -> p m e", p=128)), "sDW", 16)
        block.sync(p_load_gate)

        lq_state = [0]
        lq_vals = {}
        lq_done = {}
        lq_tick = {}

        def emit_ffn(ci):
            AR.push()
            sqf = AR.alloc(128, 8 * TC, F32)
            osc = AR.alloc(128, TC, F32)
            emit_ln1(ci, sqf, osc)
            AR.pop()
            AR.push()
            glt = AR.alloc(128, CT, F32)
            h_e = AR.alloc(128, 32 * CT, BF16)
            w2r = AR.alloc(128, 32 * 128, BF16)

            lg = tiny[:, 0:8]
            m8 = tiny[:, 8:16]
            mv = tiny[:, 16:24]
            pr = tiny[:, 24:32]
            cmb = tiny[:, 32:40]
            ngm = tiny[:, 40:41]
            ssum = tiny[:, 41:42]
            den = tiny[:, 42:43]
            for tt in range(8):
                def g_pe(t, tt=tt, ci=ci):
                    ctx.w(t, "sDVE")
                    ctx.w(t, "sDW")
                    for m in range(8):
                        ins = t.matmul(
                            PS[:, 5 * 512:5 * 512 + 8],
                            xnbf[:, m * CT + tt * 128:m * CT + (tt + 1) * 128],
                            gate_sb[:, m * 8:(m + 1) * 8],
                            start=(m == 0), stop=(m == 7))
                    ctx.inc(ins, "sPE")
                block.tensor(g_pe)

                def g_v1(v):
                    ctx.w(v, "sPE")
                    v.tensor_copy(lg, PS[:, 5 * 512:5 * 512 + 8])
                    v.reduce_max(ngm, lg, AX.X, negate=True)
                    v.tensor_scalar_add(lg, lg, ngm[:, 0:1])
                BV(g_v1)

                def g_a1(a):
                    ctx.w(a, "sDVE")
                    a.activation(lg, lg, AF.Exp)
                BS(g_a1)

                def g_v2(v):
                    ctx.w(v, "sACT")
                    v.reduce_sum(ssum, lg, AX.X)
                    v.reciprocal(ssum, ssum)
                    v.tensor_scalar_mul(lg, lg, ssum[:, 0:1])
                    v.max(out=m8, in_=lg)
                    v.tensor_copy(mv[:, 0:2], m8[:, 0:2])
                    v.tensor_copy(mv[:, 2:8], neg1_sb[:, 2:8])
                    v.match_replace(pr, mv, lg, -1.0)
                    v.tensor_sub(pr, lg, pr)
                    v.tensor_scalar_min(pr, pr, 1.0)
                    v.tensor_add(den, m8[:, 0:1], m8[:, 1:2])
                    v.reciprocal(den, den)
                    v.tensor_mul(cmb, pr, lg)
                    v.tensor_scalar_mul(cmb, cmb, den[:, 0:1])
                BV(g_v2)

                def g_tr(t, tt=tt):
                    ctx.w(t, "sDVE")
                    ctx.inc(t.transpose(PS[0:8, 4 * 512:4 * 512 + 128], cmb,
                                        ident_sb), "sPE")
                block.tensor(g_tr)

                def g_cp(v, tt=tt, ci=ci):
                    ctx.w(v, "sPE")
                    v.tensor_copy(rows1[0:8, ci * CT + tt * 128:
                                        ci * CT + (tt + 1) * 128],
                                  PS[0:8, 4 * 512:4 * 512 + 128])
                BV(g_cp)

            for e in range(E):
                def c_bc(t, e=e, ci=ci):
                    ctx.w(t, "sDVE")
                    for i in range(2):
                        ins = t.matmul(PS[:, (6 + i) * 512:(7 + i) * 512],
                                       oneh[0:16, e * 128:(e + 1) * 128],
                                       rows1[0:16, ci * CT + i * 512:
                                             ci * CT + (i + 1) * 512],
                                       start=True, stop=True)
                    ctx.inc(ins, "sPE")
                block.tensor(c_bc)

                pe_hist = {}
                act_hist = {}
                dve_hist = {}
                pe_base = ctx.cnt["sPE"]
                act_base = ctx.cnt["sACT"]
                dve_base = ctx.cnt["sDVE"]

                def lq_issue(s, out, in_):
                    i = lq_state[0]
                    sem = "sLA" if i % 2 == 0 else "sLB"
                    if i >= 1:
                        psem = "sLA" if (i - 1) % 2 == 0 else "sLB"
                        s.wait_ge(ctx.sems[psem], lq_vals[i - 1])
                        s.sem_inc(ctx.sems["sLD"], 1)
                        ctx.cnt["sLD"] += 1
                        lq_done[i - 1] = ctx.cnt["sLD"]
                    lq_vals[i] = ctx.inc(s.dma_start(out=out, in_=in_), sem, 16)
                    lq_state[0] = i + 1

                def lq_flush(s):
                    i = lq_state[0] - 1
                    if i >= 0 and i not in lq_done:
                        psem = "sLA" if i % 2 == 0 else "sLB"
                        s.wait_ge(ctx.sems[psem], lq_vals[i])
                        s.sem_inc(ctx.sems["sLD"], 1)
                        ctx.cnt["sLD"] += 1
                        lq_done[i] = ctx.cnt["sLD"]

                def w1mm(t, fi, e):
                    ctx.w(t, "sLD", lq_done[lq_tick[("w1", e, fi)]])
                    ctx.w(t, "sACT", act_hist.get(fi - 2, act_base))
                    if fi < 2:
                        ctx.w(t, "sDVE", dve_base)
                    b0 = (fi % 2) * 2
                    for m in range(8):
                        for half in range(2):
                            ins = t.matmul(
                                PS[:, (b0 + half) * 512:(b0 + half + 1) * 512],
                                wi_ring[fi % 2][:, m * 128:(m + 1) * 128],
                                xnbf[:, m * CT + half * TC:
                                     m * CT + (half + 1) * TC],
                                start=(m == 0), stop=(m == 7))
                    pe_hist[fi] = ctx.inc(ins, "sPE")

                def w1ge(a, fi):
                    ctx.w(a, "sPE", pe_hist[fi])
                    ctx.w(a, "sDVE", dve_hist.get(fi - 1, dve_base))
                    b0 = (fi % 2) * 2
                    a.activation(glt[:, 0:TC], bank(b0), AF.Gelu)
                    a.activation(glt[:, TC:CT], bank(b0 + 1), AF.Gelu)
                    act_hist[fi] = ctx.cnt["sACT"]

                def w1cb(v, fi):
                    ctx.w(v, "sACT", act_hist[fi])
                    if fi == 0:
                        ctx.w(v, "sPE")
                    v.tensor_mul(h_e[:, fi * CT:(fi + 1) * CT],
                                 glt[:, 0:CT], bank(6, 2))
                    dve_hist[fi] = ctx.cnt["sDVE"]

                for fi in range(33):
                    if fi < 32:
                        def w1ld(s, e=e, fi=fi):
                            ctx.w(s, "sW12", w12_done[0])
                            ctx.w(s, "sPE", pe_hist.get(fi - 2, pe_base))
                            lq_tick[("w1", e, fi)] = lq_state[0]
                            lq_issue(s, wi_ring[fi % 2][:, 0:1024].rearrange(
                                         "p (m f) -> p m f", m=8),
                                     w1b[e * D:(e + 1) * D,
                                         fi * 128:(fi + 1) * 128].rearrange(
                                         "(m p) f -> p m f", p=128))
                        block.sync(w1ld)
                    else:
                        block.sync(lambda s: lq_flush(s))
                    if fi >= 1:
                        block.tensor(lambda t, fj=fi - 1, e=e: w1mm(t, fj, e))
                        BS(lambda a, fj=fi - 1: w1ge(a, fj))
                        BV(lambda v, fj=fi - 1: w1cb(v, fj))

                pe2_hist = {}
                dve2_hist = {}
                pe2_base = ctx.cnt["sPE"]
                dve2_base = ctx.cnt["sDVE"]

                def w2mm(t, om, hh, e):
                    ctx.w(t, "sLD", lq_done[lq_tick[("w2", e, om, hh)]])
                    if hh == 0:
                        ctx.w(t, "sDVE", dve2_hist.get(om - 2, dve2_base))
                    b0 = 2 + (om % 2) * 2
                    for fh in range(16):
                        fg = hh * 16 + fh
                        for half in range(2):
                            ins = t.matmul(
                                PS[:, (b0 + half) * 512:(b0 + half + 1) * 512],
                                w2r[:, hh * 2048 + fh * 128:
                                    hh * 2048 + (fh + 1) * 128],
                                h_e[:, fg * CT + half * TC:
                                    fg * CT + (half + 1) * TC],
                                start=(fg == 0), stop=(fg == 31))
                    pe2_hist[(om, hh)] = ctx.inc(ins, "sPE")

                def w2add(v, om, ci):
                    ctx.w(v, "sPE", pe2_hist[(om, 1)])
                    b0 = 2 + (om % 2) * 2
                    rv = res[:, col(ci, om, 0):col(ci, om, 0) + CT]
                    v.tensor_add(rv, rv, PS[:, b0 * 512:(b0 + 2) * 512])
                    dve2_hist[om] = ctx.cnt["sDVE"]

                jobs = [(om, hh) for om in range(8) for hh in range(2)]
                for j in range(17):
                    if j < 16:
                        om, hh = jobs[j]

                        def w2ld(s, e=e, om=om, hh=hh):
                            ctx.w(s, "sW12", w12_done[0])
                            ctx.w(s, "sPE", pe2_hist.get((om, hh - 2),
                                  pe2_hist.get((om - 1, hh), pe2_base)))
                            lq_tick[("w2", e, om, hh)] = lq_state[0]
                            lq_issue(s, w2r[:, hh * 2048:(hh + 1) * 2048].rearrange(
                                         "p (fh o) -> p fh o", fh=16),
                                     w2b[e * FF + hh * 2048:
                                         e * FF + (hh + 1) * 2048,
                                         om * 128:(om + 1) * 128].rearrange(
                                         "(fh p) o -> p fh o", p=128))
                        block.sync(w2ld)
                    else:
                        block.sync(lambda s: lq_flush(s))
                    if j >= 1:
                        pom, phh = jobs[j - 1]
                        block.tensor(lambda t, pom=pom, phh=phh, e=e:
                                     w2mm(t, pom, phh, e))
                        if phh == 1:
                            BV(lambda v, pom=pom, ci=ci: w2add(v, pom, ci))
            AR.pop()

        for ci in range(NCH):
            emit_ffn(ci)
        ctx.barrier()
        if upto == "ffn1":
            emit_out()
            return nc

        kvdb = d_bf["w_kvd"]
        kupb_d = d_bf["w_kup"]
        vupb_d = d_bf["w_vup"]
        wqb_d = d_bf["w_q"]
        wob_d = d_bf["w_o"]
        wqrb_d = d_bf["wqr_ext"]
        wkrb_d = d_bf["wkr_ext"]

        AR.push()
        sqf = AR.alloc(128, 8 * TC, F32)
        osc = AR.alloc(128, TC, F32)
        kvd_sb = AR.alloc(128, 1024, BF16)
        ckv_own = AR.alloc(128, NCH * CT, BF16)

        def kvd_ld(s):
            ctx.inc(s.dma_start(
                out=kvd_sb[:, :].rearrange("p (m f) -> p m f", m=8),
                in_=kvdb.rearrange("(m p) f -> p m f", p=128)), "sDW", 16)
        block.sync(kvd_ld)

        for ci in range(NCH):
            emit_ln1(ci, sqf, osc)

            def ckv_mm(t, ci=ci):
                ctx.w(t, "sDW")
                ctx.w(t, "sDVE")
                for m in range(8):
                    for half in range(2):
                        ins = t.matmul(PS[:, half * 512:(half + 1) * 512],
                                       kvd_sb[:, m * 128:(m + 1) * 128],
                                       xnbf[:, m * CT + half * TC:
                                            m * CT + (half + 1) * TC],
                                       start=(m == 0), stop=(m == 7))
                ctx.inc(ins, "sPE")
            block.tensor(ckv_mm)

            def ckv_ev(v, ci=ci):
                ctx.w(v, "sPE")
                v.tensor_copy(ckv_own[:, ci * CT:(ci + 1) * CT], bank(0, 2))
            BV(ckv_ev)

            def ckv_st(s, ci=ci):
                ctx.w(s, "sDVE")
                ctx.inc(s.dma_start(out=ag2_in[:, ci * CT:(ci + 1) * CT],
                                    in_=ckv_own[:, ci * CT:(ci + 1) * CT]),
                        "sDX", 16)
            block.sync(ckv_st)

        def ag2(g):
            ctx.w(g, "sDX")
            ctx.inc(g.collective_compute(
                "AllGather", ALU.bypass, replica_groups=[list(range(8))],
                ins=[ag2_in.ap()], outs=[ag2_out.ap()]), "sCC", 1)
            g.wait_ge(ctx.sems["sCC"], ctx.cnt["sCC"])
        block.gpsimd(ag2)
        AR.pop()
        ctx.barrier()

        AR.push()
        kupb = AR.alloc(128, 1024, BF16)
        vupb = AR.alloc(128, 1024, BF16)
        wkrb = AR.alloc(128, 64, BF16)
        ckv_own2 = AR.alloc(128, NCH * CT, BF16)
        k97 = AR.alloc(128, T, BF16)
        ckv_kv = AR.alloc(128, T, BF16)
        ropesK_b = [AR.alloc(64, TC, BF16) for _ in range(2)]
        ropeown = AR.alloc(128, CT, BF16)
        masks_sb = AR.alloc(128, 2048, BF16)
        v_tok = AR.alloc(128, 64 * 65, BF16)
        q97 = AR.alloc(128, CT, BF16)
        kdiag = AR.alloc(128, CT, BF16)
        PT = [AR.alloc(128, CT, BF16) for _ in range(2)]
        vdiag = AR.alloc(128, 8 * 65, BF16)
        attnp = AR.alloc(128, CT, BF16)
        attn_tmp = AR.alloc(64, TC, F32)
        qsq = AR.alloc(128, TC, F32)
        qsq2 = AR.alloc(128, TC, F32)
        T1 = AR.alloc(32, TC, F32)
        T2 = AR.alloc(32, TC, F32)
        wq_r = wring_t.ap()[:, 0:512]
        wqrb = wring_t.ap()[:, 512:1024]
        kvd_sb2 = wring_t.ap()[:, 1024:2048]

        def attn_ld0(s):
            ctx.w(s, "sPE")
            ctx.w(s, "sDVE")
            ctx.w(s, "sACT")
            ctx.inc(s.dma_start(out=kupb[:, :], in_=kupb_d[:, :]), "sDW", 16)
            ctx.inc(s.dma_start(out=vupb[:, :], in_=vupb_d[:, :]), "sDW", 16)
            ctx.inc(s.dma_start(out=wkrb[:, :], in_=wkrb_d[:, :]), "sDW", 16)
            ctx.inc(s.dma_start(
                out=wqrb.rearrange("p (m f) -> p m f", m=8),
                in_=wqrb_d.rearrange("(m p) f -> p m f", p=128)), "sDW", 16)
            ctx.inc(s.dma_start(out=masks_sb[:, :], in_=d_in["masksbf"]), "sDW", 16)
            ctx.inc(s.dma_start(
                out=kvd_sb2.rearrange("p (m f) -> p m f", m=8),
                in_=kvdb.rearrange("(m p) f -> p m f", p=128)), "sDW", 16)
        block.sync(attn_ld0)

        sqa = k97.bitcast(F32)
        osca = ckv_kv[:, 0:1024].bitcast(F32)

        for ci in range(NCH):
            emit_ln1(ci, sqa, osca)

            def ckv_mm2(t, ci=ci):
                ctx.w(t, "sDW")
                ctx.w(t, "sDVE")
                for m in range(8):
                    for half in range(2):
                        ins = t.matmul(PS[:, half * 512:(half + 1) * 512],
                                       kvd_sb2[:, m * 128:(m + 1) * 128],
                                       xnbf[:, m * CT + half * TC:
                                            m * CT + (half + 1) * TC],
                                       start=(m == 0), stop=(m == 7))
                ctx.inc(ins, "sPE")
            block.tensor(ckv_mm2)

            def ckv_ev2(v, ci=ci):
                ctx.w(v, "sPE")
                v.tensor_copy(ckv_own2[:, ci * CT:(ci + 1) * CT], bank(0, 2))
            BV(ckv_ev2)

        for ci in range(NCH):
            emit_ln1(ci, sqa, osca)

            def kv_loads(g, ci=ci):
                ctx.w(g, "sACT")
                ctx.w(g, "sPE")
                ctx.w(g, "sDVE")
                for kb in range(8):
                    rank = kb if ci == 0 else 7 - kb
                    ctx.inc(g.dma_start(
                        out=ckv_kv[:, kb * CT:(kb + 1) * CT],
                        in_=ag2_out[rank * 128:(rank + 1) * 128,
                                    ci * CT:(ci + 1) * CT]), "sDG", 16)
            block.gpsimd(kv_loads)

            def ro_ld(s, ci=ci):
                ctx.w(s, "sPE")
                ctx.w(s, "sACT")
                ctx.inc(s.dma_start(
                    out=ropeown[:, :],
                    in_=d_in["ropesbf_own"][:, ci * CT:(ci + 1) * CT]),
                    "sDX", 16)
            block.sync(ro_ld)

            def kq_init(v, ci=ci):
                ctx.w(v, "sDG")
                ctx.w(v, "sDX")
                ctx.w(v, "sPE")
                v.memset(k97[0:32, :], 0.0)
                v.memset(q97[0:32, :], 0.0)
                v.memset(kdiag[0:32, :], 0.0)
                v.memset(kdiag[0:1, :], 1.0)
                v.memset(v_tok[:, :], 1.0)
                v.memset(vdiag[:, :], 1.0)
            BV(kq_init)

            def aug_ld(g, ci=ci):
                ctx.w(g, "sDVE")
                ctx.inc(g.dma_start(out=k97[0:4, :],
                                    in_=d_in["kaug"][ci * 4:(ci + 1) * 4, :]),
                        "sDG", 16)
                ctx.inc(g.dma_start(out=q97[0:4, :],
                                    in_=d_in["qaug"][:, :]), "sDG", 16)
            block.gpsimd(aug_ld)

            for s_ in range(16):
                def kr_ld(s, s_=s_):
                    ctx.w(s, "sDVE")
                    ctx.inc(s.dma_start(
                        out=ropesK_b[s_ % 2][:, :],
                        in_=d_in["ropesbf"][64:128, s_ * 512:(s_ + 1) * 512]),
                        "sDX", 16)
                block.sync(kr_ld)

                def kr_mm(t, s_=s_):
                    ctx.w(t, "sDVE")
                    ctx.w(t, "sDG")
                    ctx.inc(t.matmul(
                        PS[0:64, (s_ % 2) * 1024:(s_ % 2) * 1024 + 512],
                        wkrb[:, 0:64],
                        ckv_kv[:, s_ * 512:(s_ + 1) * 512],
                        start=True, stop=True), "sPE")
                block.tensor(kr_mm)

                def kr_rot(v, s_=s_):
                    ctx.w(v, "sPE")
                    ctx.w(v, "sDX")
                    b0 = (s_ % 2) * 1024
                    v.tensor_mul(T1[:, 0:TC], PS[0:32, b0:b0 + 512],
                                 ropesK_b[s_ % 2][0:32, :])
                    v.tensor_mul(T2[:, 0:TC], PS[32:64, b0:b0 + 512],
                                 ropesK_b[s_ % 2][32:64, :])
                    v.tensor_add(k97[32:64, s_ * 512:(s_ + 1) * 512],
                                 T1[:, 0:TC], T2[:, 0:TC])
                BV(kr_rot)

            for half in range(2):
                def kdr_mm(t, half=half, ci=ci):
                    ctx.w(t, "sDVE")
                    ctx.inc(t.matmul(
                        PS[0:64, half * 1024:half * 1024 + 512],
                        wkrb[:, 0:64],
                        ckv_own2[:, ci * CT + half * TC:
                                 ci * CT + (half + 1) * TC],
                        start=True, stop=True), "sPE")
                block.tensor(kdr_mm)

                def kdr_rot(v, half=half):
                    ctx.w(v, "sPE")
                    b0 = half * 1024
                    v.tensor_mul(T1[:, 0:TC], PS[0:32, b0:b0 + 512],
                                 ropeown[64:96, half * TC:(half + 1) * TC])
                    v.tensor_mul(T2[:, 0:TC], PS[32:64, b0:b0 + 512],
                                 ropeown[96:128, half * TC:(half + 1) * TC])
                    v.tensor_add(kdiag[32:64, half * TC:(half + 1) * TC],
                                 T1[:, 0:TC], T2[:, 0:TC])
                BV(kdr_rot)

            for half in range(2):
                def qr_mm(t, half=half):
                    ctx.w(t, "sDVE")
                    ctx.w(t, "sDW")
                    for m in range(8):
                        ins = t.matmul(
                            PS[0:64, half * 1024:half * 1024 + 512],
                            wqrb[:, m * 64:(m + 1) * 64],
                            xnbf[:, m * CT + half * TC:
                                 m * CT + (half + 1) * TC],
                            start=(m == 0), stop=(m == 7))
                    ctx.inc(ins, "sPE")
                block.tensor(qr_mm)

                def qr_rot(v, half=half):
                    ctx.w(v, "sPE")
                    b0 = half * 1024
                    v.tensor_mul(T1[:, 0:TC], PS[0:32, b0:b0 + 512],
                                 ropeown[0:32, half * TC:(half + 1) * TC])
                    v.tensor_mul(T2[:, 0:TC], PS[32:64, b0:b0 + 512],
                                 ropeown[32:64, half * TC:(half + 1) * TC])
                    v.tensor_add(q97[32:64, half * TC:(half + 1) * TC],
                                 T1[:, 0:TC], T2[:, 0:TC])
                BV(qr_rot)

            q2r = rows2[0:1, 0:1024]
            kn2 = rows2[0:1, 1024:2048]
            kmaxc = rows2[0:1, 2047:2048]

            for h in range(NH):
                def q_ld(s, h=h):
                    ctx.w(s, "sPE")
                    ctx.inc(s.dma_start(
                        out=wq_r.rearrange("p (m f) -> p m f", m=8),
                        in_=wqb_d[:, h * 64:(h + 1) * 64].rearrange(
                            "(m p) f -> p m f", p=128)), "sDW", 16)
                block.sync(q_ld)

                def q_mm(t, h=h):
                    ctx.w(t, "sDW")
                    ctx.w(t, "sACT")
                    ctx.w(t, "sDVE")
                    for m in range(8):
                        for half in range(2):
                            ins = t.matmul(
                                PS[64:128, half * 512:half * 512 + 512],
                                wq_r[:, m * 64:(m + 1) * 64],
                                xnbf[:, m * CT + half * TC:
                                     m * CT + (half + 1) * TC],
                                start=(m == 0), stop=(m == 7))
                    ctx.inc(ins, "sPE")
                block.tensor(q_mm)

                def q_ev(v, h=h):
                    ctx.w(v, "sPE")
                    v.tensor_scalar_mul(q97[64:128, :], PS[64:128, 0:1024],
                                        float(S96))
                BV(q_ev)

                for half in range(2):
                    def q_sq(v, half=half):
                        ctx.w(v, "sPE")
                        qa = q97[64:128, half * TC:(half + 1) * TC]
                        qb = q97[32:64, half * TC:(half + 1) * TC]
                        v.tensor_mul(qsq[64:128, :], qa, qa)
                        v.tensor_mul(qsq[32:64, :], qb, qb)
                    BV(q_sq)

                    def q_n(t, half=half):
                        ctx.w(t, "sDVE")
                        t.matmul(PS[0:1, 4 * 512 + half * 512:
                                     4 * 512 + half * 512 + 512],
                                 ones_col[64:128, 0:1], qsq[64:128, :],
                                 start=True, stop=False)
                        ins = t.matmul(PS[0:1, 4 * 512 + half * 512:
                                          4 * 512 + half * 512 + 512],
                                       ones_col[32:64, 0:1], qsq[32:64, :],
                                       start=False, stop=True)
                        ctx.inc(ins, "sPE")
                    block.tensor(q_n)

                def q_n2(v, h=h):
                    ctx.w(v, "sPE")
                    v.tensor_copy(q2r, PS[0:1, 4 * 512:4 * 512 + 1024])
                BV(q_n2)

                kev_h = {}
                knm_h = {}
                knx_h = {}
                kev_base = ctx.cnt["sDVE"]
                for s_ in range(16):
                    def k_mm(t, h=h, s_=s_):
                        ctx.w(t, "sDVE", kev_h.get(s_ - 2, kev_base))
                        if s_ == 0:
                            ctx.w(t, "sACT")
                            ctx.w(t, "sDVE")
                        ctx.inc(t.matmul(
                            PS[64:128, (s_ % 2) * 512:(s_ % 2) * 512 + 512],
                            kupb[:, h * 64:(h + 1) * 64],
                            ckv_kv[:, s_ * 512:(s_ + 1) * 512],
                            start=True, stop=True), "sPE")
                        knm_h[("k", s_)] = ctx.cnt["sPE"]
                    block.tensor(k_mm)

                    def k_ev(v, s_=s_):
                        ctx.w(v, "sPE", knm_h[("k", s_)])
                        sl = slice(s_ * 512, (s_ + 1) * 512)
                        qs = qsq if s_ % 2 == 0 else qsq2
                        v.tensor_copy(k97[64:128, sl],
                                      PS[64:128, (s_ % 2) * 512:
                                         (s_ % 2) * 512 + 512])
                        v.tensor_mul(qs[64:128, 0:512],
                                     k97[64:128, sl], k97[64:128, sl])
                        v.tensor_mul(qs[32:64, 0:512],
                                     k97[32:64, sl], k97[32:64, sl])
                        kev_h[s_] = ctx.cnt["sDVE"]
                    BV(k_ev)

                    def kn_mm(t, s_=s_):
                        ctx.w(t, "sDVE", kev_h[s_])
                        bnk = 4 if s_ % 2 == 0 else 5
                        qs = qsq if s_ % 2 == 0 else qsq2
                        t.matmul(PS[0:1, bnk * 512:bnk * 512 + 512],
                                 ones_col[64:128, 0:1], qs[64:128, :],
                                 start=True, stop=False)
                        ins = t.matmul(PS[0:1, bnk * 512:bnk * 512 + 512],
                                       ones_col[32:64, 0:1], qs[32:64, :],
                                       start=False, stop=True)
                        knm_h[s_] = ctx.inc(ins, "sPE")
                    block.tensor(kn_mm)

                    def kn_max(v, s_=s_):
                        ctx.w(v, "sPE", knm_h[s_])
                        bnk = 4 if s_ % 2 == 0 else 5
                        if s_ == 0:
                            v.tensor_copy(kn2[:, 0:512],
                                          PS[0:1, bnk * 512:bnk * 512 + 512])
                        else:
                            v.tensor_max(kn2[:, 0:512], kn2[:, 0:512],
                                         PS[0:1, bnk * 512:bnk * 512 + 512])
                        knx_h[s_] = ctx.cnt["sDVE"]
                    BV(kn_max)

                def kmax_fin(v):
                    v.reduce_max(kmaxc, kn2[:, 0:512], AX.X)
                    v.tensor_scalar_mul(q2r, q2r, kmaxc[0:1, 0:1])
                BV(kmax_fin)

                def mrow_s(a):
                    ctx.w(a, "sDVE")
                    a.activation(q2r, q2r, AF.Sqrt)
                BS(mrow_s)

                def mrow(v, h=h):
                    ctx.w(v, "sACT")
                    v.tensor_scalar_mul(q97[0:1, :], q2r, -1.0)
                BV(mrow)

                def kd_mm(t, h=h, ci=ci):
                    ctx.w(t, "sDVE")
                    ctx.w(t, "sACT")
                    for half in range(2):
                        ins = t.matmul(
                            PS[64:128, half * 512:half * 512 + 512],
                            kupb[:, h * 64:(h + 1) * 64],
                            ckv_own2[:, ci * CT + half * TC:
                                     ci * CT + (half + 1) * TC],
                            start=True, stop=True)
                    ctx.inc(ins, "sPE")
                block.tensor(kd_mm)

                def kd_ev(v, h=h):
                    ctx.w(v, "sPE")
                    v.tensor_copy(kdiag[64:128, :], PS[64:128, 0:1024])
                BV(kd_ev)

                for tb4 in range(2):
                    def vd_mm(t, h=h, ci=ci, tb4=tb4):
                        ctx.w(t, "sDVE")
                        ctx.w(t, "sACT")
                        for j in range(4):
                            tb = tb4 * 4 + j
                            ins = t.matmul(PS[:, 5 * 512 + j * 64:
                                              5 * 512 + (j + 1) * 64],
                                           ckv_own2[:, ci * CT + tb * 128:
                                                    ci * CT + (tb + 1) * 128],
                                           vupb[:, h * 64:(h + 1) * 64],
                                           start=True, stop=True)
                        ctx.inc(ins, "sPE")
                    block.tensor(vd_mm)

                    def vd_ev(v, tb4=tb4):
                        ctx.w(v, "sPE")
                        v.tensor_copy(
                            vdiag[:, tb4 * 4 * 65:tb4 * 4 * 65 + 260].rearrange(
                                "p (b c) -> p b c", b=4)[:, :, 0:64],
                            PS[:, 5 * 512:5 * 512 + 256].rearrange(
                                "p (b c) -> p b c", b=4))
                    BV(vd_ev)

                for tb4 in range(16):
                    def v_mm(t, tb4=tb4, h=h):
                        ctx.w(t, "sDVE")
                        for j in range(4):
                            tb = tb4 * 4 + j
                            ins = t.matmul(PS[:, 5 * 512 + j * 64:
                                              5 * 512 + (j + 1) * 64],
                                           ckv_kv[:, tb * 128:(tb + 1) * 128],
                                           vupb[:, h * 64:(h + 1) * 64],
                                           start=True, stop=True)
                        ctx.inc(ins, "sPE")
                    block.tensor(v_mm)

                    def v_ev(v, tb4=tb4):
                        ctx.w(v, "sPE")
                        v.tensor_copy(
                            v_tok[:, tb4 * 4 * 65:tb4 * 4 * 65 + 260].rearrange(
                                "p (b c) -> p b c", b=4)[:, :, 0:64],
                            PS[:, 5 * 512:5 * 512 + 256].rearrange(
                                "p (b c) -> p b c", b=4))
                    BV(v_ev)

                smm_h = {}
                exp_h = {}
                omm_h = {}
                smm_base = ctx.cnt["sACT"]
                omm_base = ctx.cnt["sPE"]
                for kvb in range(64):
                    def s_mm(t, kvb=kvb):
                        ctx.w(t, "sACT", exp_h.get(kvb - 2, smm_base))
                        if kvb == 0:
                            ctx.w(t, "sACT")
                            ctx.w(t, "sDVE")
                            ctx.w(t, "sDG")
                        b0 = (kvb % 2) * 2
                        for qt in range(2):
                            ins = t.matmul(PS[:, (b0 + qt) * 512:
                                              (b0 + qt + 1) * 512],
                                           k97[:, kvb * 128:(kvb + 1) * 128],
                                           q97[:, qt * 512:(qt + 1) * 512],
                                           start=True, stop=True)
                        smm_h[kvb] = ctx.inc(ins, "sPE")
                    block.tensor(s_mm)

                    def s_exp(a, kvb=kvb):
                        ctx.w(a, "sPE", max(smm_h[kvb],
                                            omm_h.get(kvb - 2, omm_base)))
                        b0 = (kvb % 2) * 2
                        a.activation(PT[kvb % 2][:, :],
                                     PS[:, b0 * 512:(b0 + 2) * 512], AF.Exp)
                        exp_h[kvb] = ctx.cnt["sACT"]
                    BS(s_exp)

                    def o_mm(t, kvb=kvb):
                        ctx.w(t, "sACT", exp_h[kvb])
                        for qt in range(2):
                            ins = t.matmul(
                                PS[0:65, (6 + qt) * 512:(6 + qt) * 512 + 512],
                                v_tok[:, kvb * 65:(kvb + 1) * 65],
                                PT[kvb % 2][:, qt * 512:(qt + 1) * 512],
                                start=(kvb == 0), stop=False)
                        omm_h[kvb] = ctx.inc(ins, "sPE")
                    block.tensor(o_mm)

                dmm_h = {}
                dmsk_h = {}
                dexp_h = {}
                domm_h = {}
                for qt in range(2):
                    for b4 in range(4):
                        idx = qt * 4 + b4

                        def d_mm(t, qt=qt, b4=b4, idx=idx):
                            if idx < 2:
                                ctx.w(t, "sACT")
                                ctx.w(t, "sDVE")
                            else:
                                ctx.w(t, "sACT", dexp_h[idx - 2])
                            ctx.inc(t.matmul(
                                PS[:, (idx % 2) * 512:(idx % 2 + 1) * 512],
                                kdiag[:, qt * 512 + b4 * 128:
                                      qt * 512 + (b4 + 1) * 128],
                                q97[:, qt * 512:(qt + 1) * 512],
                                start=True, stop=True), "sPE")
                            dmm_h[idx] = ctx.cnt["sPE"]
                        block.tensor(d_mm)

                        def d_msk(v, b4=b4, idx=idx):
                            ctx.w(v, "sPE", dmm_h[idx])
                            sl = slice((idx % 2) * 512, (idx % 2 + 1) * 512)
                            v.tensor_add(PS[:, sl], PS[:, sl],
                                         masks_sb[:, b4 * 512:(b4 + 1) * 512])
                            dmsk_h[idx] = ctx.cnt["sDVE"]
                        BV(d_msk)

                        def d_exp(a, qt=qt, b4=b4, idx=idx):
                            ctx.w(a, "sDVE", dmsk_h[idx])
                            if idx < 2:
                                ctx.w(a, "sPE")
                            else:
                                ctx.w(a, "sPE", domm_h[idx - 2])
                            a.activation(PT[idx % 2][:, 0:512],
                                         PS[:, (idx % 2) * 512:
                                            (idx % 2 + 1) * 512], AF.Exp)
                            dexp_h[idx] = ctx.cnt["sACT"]
                        BS(d_exp)

                        def d_omm(t, qt=qt, b4=b4, idx=idx):
                            ctx.w(t, "sACT", dexp_h[idx])
                            kb = qt * 4 + b4
                            ctx.inc(t.matmul(
                                PS[0:65, (6 + qt) * 512:(6 + qt) * 512 + 512],
                                vdiag[:, kb * 65:(kb + 1) * 65],
                                PT[idx % 2][:, 0:512],
                                start=False, stop=(b4 == 3)), "sPE")
                            domm_h[idx] = ctx.cnt["sPE"]
                        block.tensor(d_omm)

                for qt in range(2):
                    rno = rows2[64:65, qt * 512:(qt + 1) * 512]

                    def n_rc(v, qt=qt):
                        ctx.w(v, "sPE")
                        v.reciprocal(rno, PS[64:65, (6 + qt) * 512:
                                             (6 + qt) * 512 + 512])
                    BV(n_rc)

                    def n_bc(t, qt=qt):
                        ctx.w(t, "sDVE")
                        ctx.inc(t.matmul(PS[0:64, 2 * 512:2 * 512 + 512],
                                         ones3[64:65, 0:64], rno,
                                         start=True, stop=True), "sPE")
                    block.tensor(n_bc)

                    def n_ev(v, qt=qt):
                        ctx.w(v, "sPE")
                        v.tensor_copy(attn_tmp[:, 0:512],
                                      PS[0:64, (6 + qt) * 512:
                                         (6 + qt) * 512 + 512])
                    BV(n_ev)

                    def n_mul(v, qt=qt, h=h):
                        ctx.w(v, "sPE")
                        v.tensor_mul(
                            attnp[(h % 2) * 64:(h % 2) * 64 + 64,
                                  qt * 512:(qt + 1) * 512],
                            attn_tmp[:, 0:512], PS[0:64, 2 * 512:2 * 512 + 512])
                    BV(n_mul)

                if h % 2 == 1:
                    hp = h // 2

                    def wo_ld(s, hp=hp):
                        ctx.w(s, "sPE")
                        ctx.w(s, "sDVE")
                        ctx.inc(s.dma_start(
                            out=wo_ring[hp % 2][:, 0:1024],
                            in_=wob_d[hp * 128:(hp + 1) * 128, :]), "sDW", 16)
                    block.sync(wo_ld)

                    for om in range(8):
                        def wo_mm(t, om=om, hp=hp):
                            ctx.w(t, "sDW")
                            ctx.w(t, "sDVE")
                            for half in range(2):
                                ins = t.matmul(
                                    PS[:, (2 + half) * 512:(3 + half) * 512],
                                    wo_ring[hp % 2][:, om * 128:(om + 1) * 128],
                                    attnp[:, half * TC:(half + 1) * TC],
                                    start=True, stop=True)
                            ctx.inc(ins, "sPE")
                        block.tensor(wo_mm)

                        def wo_add(v, om=om, ci=ci):
                            ctx.w(v, "sPE")
                            rv = res[:, col(ci, om, 0):col(ci, om, 0) + CT]
                            v.tensor_add(rv, rv, bank(2, 2))
                        BV(wo_add)
        AR.pop()
        ctx.barrier()
        if upto == "attn":
            emit_out()
            return nc

        for ci in range(NCH):
            emit_ffn(ci)
        ctx.barrier()

        emit_out()
    return nc


def kernel(**inputs):
    per_core = make_host_inputs(inputs)
    nc = build_nc()
    r = run_bass_kernel_spmd(nc, per_core, core_ids=list(range(8)))
    return assemble_output(r.results)


# version 06 (final)
# speedup vs baseline: 1.4754x; 1.1492x over previous
import numpy as np
import ml_dtypes
import concourse.bass as bass
import concourse.mybir as mybir
from concourse.bass_utils import run_bass_kernel_spmd

F32 = mybir.dt.float32
BF16 = mybir.dt.bfloat16
AF = mybir.ActivationFunctionType
ALU = mybir.AluOpType
AX = mybir.AxisListType

D = 1024
T = 8192
B = 2
DI = 2048
NST = 16
E = 8
FF = 4096
NH = 16
HD = 64
DR = 32
DC = 128
CT = 1024
NCH = 2
TCORE = NCH * CT
TC = 512
DGRP = 4
LN_EPS = 1e-5
MASKVAL = -30000.0
S96 = float(1.0 / np.sqrt(np.float64(HD + DR)))

UPTO = "all"


def _chunks_of_core(k):
    return [(0, k), (1, 7 - k)]


def _fm(xslice):
    t, d = xslice.shape
    return np.ascontiguousarray(
        xslice.reshape(t, d // 128, 128).transpose(2, 1, 0).reshape(128, -1))


def make_host_inputs(inputs):
    x = np.asarray(inputs["L0_attn_a_x"], np.float32)
    inv_freq = (1.0 / 10000.0 ** (np.arange(0, DR, 2, dtype=np.float32) / DR)).astype(np.float32)
    fr = np.outer(np.arange(T, dtype=np.float32), inv_freq)
    cos = np.cos(fr.astype(np.float64)).astype(np.float32).T
    sin = np.sin(fr.astype(np.float64)).astype(np.float32).T
    ropes = np.zeros((128, T), np.float32)
    ropes[0:16] = cos * np.float32(S96)
    ropes[16:32] = cos * np.float32(S96)
    ropes[32:48] = -sin * np.float32(S96)
    ropes[48:64] = sin * np.float32(S96)
    ropes[64:80] = cos
    ropes[80:96] = cos
    ropes[96:112] = -sin
    ropes[112:128] = sin
    masks = np.zeros((128, 4 * 512), np.float32)
    ii = np.arange(128)[:, None]
    jj = np.arange(512)[None, :]
    for di, delta in enumerate((0, 128, 256, 384)):
        masks[:, di * 512:(di + 1) * 512] = np.where(ii + delta > jj, MASKVAL, 0.0)
    onehots = np.zeros((48, 16 * 128), np.float32)
    for n in range(16):
        onehots[n, n * 128:(n + 1) * 128] = 1.0
        onehots[32 + n, n * 128:(n + 1) * 128] = 1.0
    conv_w = np.asarray(inputs["L0_attn_a_conv_w"], np.float32)
    conv_w_pd = np.ascontiguousarray(
        conv_w[:, 0, :].reshape(16, 128, 4).transpose(1, 0, 2).reshape(128, 64))
    conv_b_pd = np.ascontiguousarray(
        np.asarray(inputs["L0_attn_a_conv_b"], np.float32).reshape(16, 128).T)
    A_pd = np.ascontiguousarray(
        np.asarray(inputs["L0_attn_a_A_log"], np.float32).reshape(16, 128, 16)
        .transpose(1, 0, 2).reshape(128, 256))
    D_pd = np.ascontiguousarray(
        np.asarray(inputs["L0_attn_a_D_skip"], np.float32).reshape(16, 128).T)
    gamma_pd = np.ascontiguousarray(
        np.asarray(inputs["L0_attn_a_gamma"], np.float32).reshape(8, 128).T)
    beta_pd = np.ascontiguousarray(
        np.asarray(inputs["L0_attn_a_beta"], np.float32).reshape(8, 128).T)
    wx = np.asarray(inputs["L0_attn_a_W_x"], np.float32)
    wx_ext = np.zeros((DI, 65), np.float32)
    wx_ext[:, 0:16] = wx[:, 0:16]
    wx_ext[:, 32:48] = wx[:, 16:32]
    wx_ext[:, 64] = wx[:, 32]
    wqr = np.asarray(inputs["L0_attn_b_W_q_rope"], np.float32)
    wqr_ext = np.ascontiguousarray(np.concatenate(
        [wqr[:, 0:16], wqr[:, 16:32], wqr[:, 16:32], wqr[:, 0:16]], 1))
    wkr = np.asarray(inputs["L0_attn_b_W_k_rope"], np.float32)
    wkr_ext = np.ascontiguousarray(np.concatenate(
        [wkr[:, 0:16], wkr[:, 16:32], wkr[:, 16:32], wkr[:, 0:16]], 1))

    shared = {
        "masksbf": masks.astype(ml_dtypes.bfloat16),
        "ropesbf": ropes.astype(ml_dtypes.bfloat16),
        "onehots": onehots,
        "conv_w_pd": conv_w_pd, "conv_b_pd": conv_b_pd,
        "A_pd": A_pd, "D_pd": D_pd, "gamma_pd": gamma_pd, "beta_pd": beta_pd,
        "wx_ext": wx_ext, "wqr_ext": wqr_ext, "wkr_ext": wkr_ext,
        "neg1": np.full((128, 8), -1.0, np.float32),
        "ident": np.eye(128, dtype=np.float32),
        "w_in": np.asarray(inputs["L0_attn_a_W_in"], np.float32),
        "w_out": np.asarray(inputs["L0_attn_a_W_out"], np.float32),
        "gate_w": np.asarray(inputs["L0_ffn_gate_w"], np.float32),
        "w1": np.asarray(inputs["L0_ffn_w1"], np.float32).reshape(E * D, FF),
        "w2": np.asarray(inputs["L0_ffn_w2"], np.float32).reshape(E * FF, D),
        "w_kvd": np.asarray(inputs["L0_attn_b_W_kv_down"], np.float32),
        "w_kup": np.asarray(inputs["L0_attn_b_W_k_up"], np.float32),
        "w_vup": np.asarray(inputs["L0_attn_b_W_v_up"], np.float32),
        "w_q": np.asarray(inputs["L0_attn_b_W_q"], np.float32),
        "w_o": np.asarray(inputs["L0_attn_b_W_o"], np.float32),
    }
    qaug = np.zeros((4, 1024), np.float32)
    qaug[1, 0:512] = 1.0
    qaug[2, 512:1024] = 1.0
    shared["qaug"] = qaug
    per_core = []
    for k in range(8):
        m = dict(shared)
        xfm = np.empty((128, 8, NCH, CT), np.float32)
        halo = np.zeros((128, NCH, 8, 3), np.float32)
        hmask = np.zeros((128, NCH), np.float32)
        agmask = np.zeros((128, NCH * 7), np.float32)
        for ci, (b, p) in enumerate(_chunks_of_core(k)):
            f = _fm(x[b, p * CT:(p + 1) * CT, :]).reshape(128, 8, CT)
            xfm[:, :, ci, :] = f
            if p > 0:
                halo[:, ci] = _fm(x[b, p * CT - 3:p * CT, :]).reshape(128, 8, 3)
                hmask[:, ci] = 1.0
            for s in range(7):
                if s < p:
                    agmask[:, ci * 7 + s] = 1.0
        kaug = np.zeros((NCH * 4, T), np.float32)
        ropes_own = np.zeros((128, NCH * CT), np.float32)
        for ci, (b, p) in enumerate(_chunks_of_core(k)):
            kaug[ci * 4 + 0, :] = 1.0
            kaug[ci * 4 + 1, p * CT:] = MASKVAL
            kaug[ci * 4 + 2, p * CT + 512:] = MASKVAL
            sl = slice(p * CT, (p + 1) * CT)
            ropes_own[0:64, ci * CT:(ci + 1) * CT] = ropes[0:64, sl]
            ropes_own[64:128, ci * CT:(ci + 1) * CT] = ropes[64:128, sl]
        m["kaug"] = kaug
        m["ropesbf_own"] = ropes_own.astype(ml_dtypes.bfloat16)
        m["xfm"] = np.ascontiguousarray(xfm.reshape(128, -1))
        m["halo"] = np.ascontiguousarray(halo.reshape(128, -1))
        m["hmask"] = hmask
        m["agmask"] = agmask
        per_core.append(m)
    return per_core


def assemble_output(results):
    out = np.empty((B, T, D), np.float32)
    for k in range(8):
        y = results[k]["yfm"].reshape(128, 8, NCH, CT)
        for ci, (b, p) in enumerate(_chunks_of_core(k)):
            out[b, p * CT:(p + 1) * CT, :] = y[:, :, ci].transpose(2, 1, 0).reshape(CT, D)
    return out


def col(ci, g, t0):
    return g * (NCH * CT) + ci * CT + t0



_COMPUTE_OPS = {
    "tensor_copy", "tensor_tensor", "tensor_add", "tensor_sub", "tensor_mul",
    "tensor_max", "tensor_scalar", "tensor_single_scalar", "tensor_scalar_add",
    "tensor_scalar_sub", "tensor_scalar_mul", "tensor_scalar_max",
    "tensor_scalar_min", "tensor_relu", "tensor_reduce", "reduce_max",
    "reduce_sum", "scalar_tensor_tensor", "tensor_tensor_scan", "memset",
    "memzero", "activation", "mul", "add", "copy", "square", "sqrt", "sign",
    "reciprocal", "select", "copy_predicated", "transpose", "bn_stats",
    "bn_aggr", "max", "max_index", "match_replace", "stream_shuffle", "iota",
    "engine_nop", "pool", "pool_avg", "pool_max",
}


class Eng:
    def __init__(self, eng, ctx, semname):
        self._e, self._c, self._s = eng, ctx, semname

    def __getattr__(self, name):
        f = getattr(self._e, name)
        if name in _COMPUTE_OPS:
            def wrap(*a, **k):
                self._c.w(self._e, self._s)
                ins = f(*a, **k)
                ins.then_inc(self._c.sems[self._s], 1)
                self._c.cnt[self._s] += 1
                return ins
            return wrap
        return f

class Ctx:
    def __init__(self, nc, block, sems):
        self.nc, self.block, self.sems = nc, block, sems
        self.cnt = {k: 0 for k in sems}

    def inc(self, instr, name, k=1):
        if name in ("sDVE", "sACT", "sPOOL"):
            return self.cnt[name]
        instr.then_inc(self.sems[name], k)
        self.cnt[name] += k
        return self.cnt[name]

    def w(self, eng, name, val=None):
        v = self.cnt[name] if val is None else val
        if v > 0:
            eng.wait_ge(self.sems[name], v)

    BARRIER_SEMS = ("sPE", "sACT", "sDVE", "sPOOL", "sCC", "sW12", "sOUT")

    def barrier(self):
        snap = {k: self.cnt[k] for k in self.BARRIER_SEMS if k in self.cnt}
        for en in ("sync", "tensor", "vector", "scalar", "gpsimd"):
            def f(e, snap=snap):
                for nm, v in snap.items():
                    if v > 0:
                        e.wait_ge(self.sems[nm], v)
            getattr(self.block, en)(f)


class Arena:
    def __init__(self, t, nbytes):
        self.t, self.nbytes, self.off, self.stack = t, nbytes, 0, []
        self.peak = 0

    def push(self):
        self.stack.append(self.off)

    def pop(self):
        self.off = self.stack.pop()

    def alloc(self, rows, cols, dt):
        esz = 2 if dt == BF16 else 4
        self.off = (self.off + 3) // 4 * 4
        base = self.off
        self.off += cols * esz
        self.peak = max(self.peak, self.off)
        assert self.off <= self.nbytes, f"arena overflow {self.off}>{self.nbytes}"
        v = self.t.ap()[0:rows, base // 2: base // 2 + cols * (esz // 2)]
        if dt == F32:
            v = v.bitcast(F32)
        return v


def build_nc(upto=None):
    upto = upto or UPTO
    nc = bass.Bass("TRN2", target_bir_lowering=False)
    d_in = {}
    for nm, shape in [
        ("xfm", [128, 8 * NCH * CT]), ("halo", [128, NCH * 8 * 3]),
        ("hmask", [128, NCH]), ("agmask", [128, NCH * 7]),
        ("onehots", [48, 2048]),
        ("neg1", [128, 8]), ("ident", [128, 128]), ("qaug", [4, 1024]), ("kaug", [NCH * 4, T]),
        ("conv_w_pd", [128, 64]), ("conv_b_pd", [128, 16]), ("A_pd", [128, 256]),
        ("D_pd", [128, 16]), ("gamma_pd", [128, 8]), ("beta_pd", [128, 8]),
        ("wx_ext", [DI, 65]), ("wqr_ext", [D, 64]), ("wkr_ext", [DC, 64]),
        ("w_in", [D, 2 * DI]), ("w_out", [DI, D]), ("gate_w", [D, E]),
        ("w1", [E * D, FF]), ("w2", [E * FF, D]),
        ("w_kvd", [D, DC]), ("w_kup", [DC, D]), ("w_vup", [DC, D]),
        ("w_q", [D, D]), ("w_o", [D, D]),
    ]:
        d_in[nm] = nc.dram_tensor(nm, shape, F32, kind="ExternalInput").ap()
    for nm, shape in [("masksbf", [128, 2048]), ("ropesbf", [128, T]),
                      ("ropesbf_own", [128, NCH * CT])]:
        d_in[nm] = nc.dram_tensor(nm, shape, BF16, kind="ExternalInput").ap()
    yfm = nc.dram_tensor("yfm", [128, 8 * NCH * CT], F32, kind="ExternalOutput").ap()

    d_bf = {}
    for nm, shape in [
        ("w_in", [D, 2 * DI]), ("w_out", [DI, D]), ("wx_ext", [DI, 65]),
        ("gate_w", [D, E]), ("w1", [E * D, FF]), ("w2", [E * FF, D]),
        ("w_kvd", [D, DC]), ("w_kup", [DC, D]), ("w_vup", [DC, D]),
        ("w_q", [D, D]), ("w_o", [D, D]), ("wqr_ext", [D, 64]),
        ("wkr_ext", [DC, 64]),
    ]:
        d_bf[nm] = nc.dram_tensor(nm + "_bf", shape, BF16).ap()
    ag1_in = nc.dram_tensor("ag1_in", [128, NCH * 257], F32)
    ag1_out = nc.dram_tensor("ag1_out", [8 * 128, NCH * 257], F32)
    ag2_in = nc.dram_tensor("ag2_in", [128, TCORE], BF16)
    ag2_out = nc.dram_tensor("ag2_out", [8 * 128, TCORE], BF16)

    from contextlib import ExitStack
    with ExitStack() as _es:
        res = _es.enter_context(nc.sbuf_tensor("res", [128, 8 * NCH * CT], F32))
        arena_t = _es.enter_context(nc.sbuf_tensor("arena", [128, 40960], BF16))
        xnbf = _es.enter_context(nc.sbuf_tensor("xnbf", [128, 8 * CT], BF16))
        rows1 = _es.enter_context(nc.sbuf_tensor("rows1", [66, NCH * CT], F32))
        rows2 = _es.enter_context(nc.sbuf_tensor("rows2", [66, NCH * CT], F32))
        rows3 = _es.enter_context(nc.sbuf_tensor("rows3", [66, NCH * CT], F32))
        small = _es.enter_context(nc.sbuf_tensor("small", [128, 700], F32))
        oneh = _es.enter_context(nc.sbuf_tensor("oneh", [48, 2048], F32))
        ones3 = _es.enter_context(nc.sbuf_tensor("ones3", [66, 128], F32))
        hstart = _es.enter_context(nc.sbuf_tensor("hstart", [128, 256], F32))
        pub = _es.enter_context(nc.sbuf_tensor("pub", [128, NCH * 257], F32))
        pred = _es.enter_context(nc.sbuf_tensor("pred", [128, 257], F32))
        wring_t = _es.enter_context(nc.sbuf_tensor("wring", [128, 4096], BF16))
        PS = _es.enter_context(nc.psum_tensor("PS", [128, 4096], F32))
        _sems = [_es.enter_context(nc.semaphore("s%d" % i)) for i in range(13)]
        _sems13 = _sems
        _1, _2, _3, _4, _5, _6, _7, _8, _9, _10 = _sems[:10]
        _11, _12, _13 = _sems[10:]
        block = _es.enter_context(nc.Block())
        def BV(fn):
            block.vector(lambda v: fn(Eng(v, ctx, "sDVE")))

        def BS(fn):
            block.scalar(lambda a: fn(Eng(a, ctx, "sACT")))

        def BG(fn):
            block.gpsimd(lambda g: fn(Eng(g, ctx, "sPOOL")))

        ctx = Ctx(nc, block, {"sDW": _1, "sDX": _2, "sPE": _3, "sACT": _4,
                              "sDVE": _5, "sPOOL": _6, "sOUT": _7, "sCC": _8, "sDG": _9, "sW12": _10,
                              "sLA": _11, "sLB": _12, "sLD": _13})
        AR = Arena(arena_t, 81920)
        WR = Arena(wring_t, 8192)

        def bank(i, n=1):
            return PS[:, i * 512:(i + n) * 512]

        sm = small.ap()
        convw_sb = sm[:, 0:64]
        convb_sb = sm[:, 64:80]
        A_sb = sm[:, 80:336]
        D_sb = sm[:, 336:352]
        gam_sb = sm[:, 352:360]
        bet_sb = sm[:, 360:368]
        ones_col = sm[:, 368:369]
        eps_col = sm[:, 369:370]
        hmask_sb = sm[:, 370:372]
        agm_sb = sm[:, 372:386]
        sdt_sb = sm[:, 386:388]
        neg1_sb = sm[:, 388:396]
        tiny = sm[:, 400:700]

        def p0_sync(s):
            for dst, src in [
                (res[:, :], d_in["xfm"]), (oneh[:, :], d_in["onehots"]),
                (convw_sb, d_in["conv_w_pd"]), (convb_sb, d_in["conv_b_pd"]),
                (A_sb, d_in["A_pd"]), (D_sb, d_in["D_pd"]),
                (gam_sb, d_in["gamma_pd"]), (bet_sb, d_in["beta_pd"]),
                (hmask_sb, d_in["hmask"]), (agm_sb, d_in["agmask"]),
                (neg1_sb, d_in["neg1"]),
            ]:
                ctx.inc(s.dma_start(out=dst, in_=src), "sDX", 16)
        block.sync(p0_sync)

        def p0_vec(v):
            ctx.w(v, "sDX")
            v.memset(ones_col, 1.0)
            v.memset(eps_col, float(LN_EPS))
            v.memset(ones3[:, :], 1.0)
            ctx.inc(v.memset(hstart[:, :], 0.0), "sDVE")
        BV(p0_vec)

        def p0_scalar(sc):
            ctx.w(sc, "sDVE")
            sc.activation(A_sb, A_sb, AF.Exp)
            ctx.inc(sc.mul(A_sb, A_sb, -1.0), "sACT")
        BS(p0_scalar)

        def p0_preconv(g):
            for nm in ["w_in", "w_out", "wx_ext", "gate_w", "w_kvd", "w_kup",
                       "w_vup", "w_q", "w_o", "wqr_ext", "wkr_ext"]:
                srct, dst = d_in[nm], d_bf[nm]
                ctx.inc(g.dma_start(out=dst[:, :], in_=srct[:, :]), "sDG", 16)
        BG(p0_preconv)
        ctx.barrier()

        w12_done = [0]
        if upto in ("ffn1", "attn", "all"):
            def p0_preconv2(g):
                for nm in ("w1", "w2"):
                    srct, dst = d_in[nm], d_bf[nm]
                    rows = srct.shape[0]
                    for r0 in range(0, rows, rows // 4):
                        last = (nm == "w2" and r0 + rows // 4 >= rows)
                        ctx.inc(g.dma_start(out=dst[r0:r0 + rows // 4, :],
                                            in_=srct[r0:r0 + rows // 4, :]),
                                "sW12" if last else "sDG", 16)
                w12_done[0] = ctx.cnt["sW12"]
            BG(p0_preconv2)

        winb = d_bf["w_in"]
        woutb = d_bf["w_out"]
        wxb = d_bf["wx_ext"]

        wx_sb = AR.alloc(128, 16 * 65, BF16)
        ident_sb = AR.alloc(128, 128, F32)
        wi_ring = [WR.alloc(128, 1024, BF16) for _ in range(2)]
        wo_ring = [WR.alloc(128, 1024, BF16) for _ in range(2)]

        preconv_done = ctx.cnt["sDG"]

        def p_load_wx(s):
            ctx.w(s, "sDG", preconv_done)
            for di in range(16):
                ctx.inc(s.dma_start(out=wx_sb[:, di * 65:(di + 1) * 65],
                                    in_=wxb[di * 128:(di + 1) * 128, :]), "sDW", 16)
            ctx.inc(s.dma_start(out=ident_sb, in_=d_in["ident"]), "sDW", 16)
        block.sync(p_load_wx)

        def bcast(t, dst, src_row, tn):
            bp = src_row.base_partition()
            onr = ones3[bp:bp + 1, 0:128]
            ins = None
            for i in range((tn + 511) // 512):
                c0, c1 = i * 512, min(tn, (i + 1) * 512)
                ins = t.matmul(dst[:, c0:c1], onr, src_row[:, c0:c1],
                               start=True, stop=True)
            return ins

        def win_slice(di, zhalf=False):
            c0 = (DI if zhalf else 0) + di * 128
            return winb[:, c0:c0 + 128].rearrange("(m p) f -> p m f", p=128)

        def emit_ln2(srcv, tn, mu_r, rs_r, mui_r, rsi_r, dstfn, sq, lnt,
                     mask_col=None):
            def on_sq(a):
                ctx.w(a, "sDVE")
                for i in range(8):
                    ins = a.activation(sq[:, i * tn:(i + 1) * tn], srcv[i], AF.Square)
                ctx.inc(ins, "sACT")
            BS(on_sq)

            def on_pe(t):
                ctx.w(t, "sACT")
                ctx.w(t, "sDVE")
                for i in range(8):
                    t.matmul(PS[0:1, 0:tn], ones_col, srcv[i],
                             start=(i == 0), stop=(i == 7))
                for i in range(8):
                    ins = t.matmul(PS[32:33, 0:tn], ones_col,
                                   sq[:, i * tn:(i + 1) * tn],
                                   start=(i == 0), stop=(i == 7))
                ctx.inc(ins, "sPE")
            block.tensor(on_pe)

            def on_rows(v):
                ctx.w(v, "sPE")
                v.tensor_scalar_mul(mu_r, PS[0:1, 0:tn], 1.0 / D)
                v.tensor_scalar_mul(rs_r, PS[32:33, 0:tn], 1.0 / D)
                tmp = PS[32:33, 3 * 512:3 * 512 + tn]
                v.tensor_mul(tmp, mu_r, mu_r)
                ctx.inc(v.tensor_sub(rs_r, rs_r, tmp), "sDVE")
            BV(on_rows)

            def on_rstd(a):
                ctx.w(a, "sDVE")
                a.activation(rs_r, rs_r, AF.Ln, bias=eps_col[0:1, 0:1])
                ctx.inc(a.activation(rs_r, rs_r, AF.Exp, scale=-0.5), "sACT")
            BS(on_rstd)

            def on_bc(t):
                ctx.w(t, "sACT")
                bcast(t, bank(1), mu_r, tn)
                ctx.inc(bcast(t, bank(2), rs_r, tn), "sPE")
            block.tensor(on_bc)

            def on_ln0(v):
                ctx.w(v, "sPE")
                for i in range(8):
                    o = lnt[:, i * tn:(i + 1) * tn]
                    v.tensor_sub(o, srcv[i], bank(1)[:, 0:tn])
                    ins = v.tensor_mul(o, o, bank(2)[:, 0:tn])
                ctx.inc(ins, "sDVE")
            BV(on_ln0)

            def on_sq2(a):
                ctx.w(a, "sDVE")
                for i in range(8):
                    ins = a.activation(sq[:, i * tn:(i + 1) * tn],
                                       lnt[:, i * tn:(i + 1) * tn], AF.Square)
                ctx.inc(ins, "sACT")
            BS(on_sq2)

            def on_pe2(t):
                ctx.w(t, "sACT")
                for i in range(8):
                    t.matmul(PS[0:1, 0:tn], ones_col,
                             lnt[:, i * tn:(i + 1) * tn],
                             start=(i == 0), stop=(i == 7))
                for i in range(8):
                    ins = t.matmul(PS[32:33, 0:tn], ones_col,
                                   sq[:, i * tn:(i + 1) * tn],
                                   start=(i == 0), stop=(i == 7))
                ctx.inc(ins, "sPE")
            block.tensor(on_pe2)

            def on_rows2(v):
                ctx.w(v, "sPE")
                v.tensor_scalar_mul(mui_r, PS[0:1, 0:tn], 1.0 / D)
                v.tensor_scalar_mul(rsi_r, PS[32:33, 0:tn], 1.0 / D)
                tmp = PS[32:33, 3 * 512:3 * 512 + tn]
                v.tensor_mul(tmp, mui_r, mui_r)
                ctx.inc(v.tensor_sub(rsi_r, rsi_r, tmp), "sDVE")
            BV(on_rows2)

            def on_rstd2(a):
                ctx.w(a, "sDVE")
                a.activation(rsi_r, rsi_r, AF.Ln, bias=eps_col[0:1, 0:1])
                ctx.inc(a.activation(rsi_r, rsi_r, AF.Exp, scale=-0.5), "sACT")
            BS(on_rstd2)

            def on_bc2(t):
                ctx.w(t, "sACT")
                bcast(t, bank(1), mui_r, tn)
                ctx.inc(bcast(t, bank(2), rsi_r, tn), "sPE")
            block.tensor(on_bc2)

            def on_xn(v):
                ctx.w(v, "sPE")
                for i in range(8):
                    o = lnt[:, i * tn:(i + 1) * tn]
                    v.tensor_sub(o, o, bank(1)[:, 0:tn])
                    v.tensor_mul(o, o, bank(2)[:, 0:tn])
                    if mask_col is None:
                        ins = v.tensor_scalar(dstfn(i), o, gam_sb[:, i:i + 1],
                                              bet_sb[:, i:i + 1], ALU.mult, ALU.add)
                    else:
                        v.tensor_scalar(o, o, gam_sb[:, i:i + 1],
                                        bet_sb[:, i:i + 1], ALU.mult, ALU.add)
                        ins = v.tensor_scalar_mul(dstfn(i), o, mask_col)
                ctx.inc(ins, "sDVE")
            BV(on_xn)

        def emit_xinner_conv(ci, t0, xconv, xi_sb, stash, conv_acc):
            for di in range(16):
                ring = wi_ring[di % 2]

                def ld(s, di=di, ring=ring):
                    ctx.w(s, "sPE")
                    ctx.inc(s.dma_start(out=ring[:, 0:1024].rearrange("p (m f) -> p m f", m=8), in_=win_slice(di)),
                            "sDW", 16)
                block.sync(ld)

                def mm(t, di=di, ring=ring, t0=t0):
                    ctx.w(t, "sDW")
                    ctx.w(t, "sDVE")
                    for m in range(8):
                        ins = t.matmul(bank(3)[:, 0:TC],
                                       ring[:, m * 128:(m + 1) * 128],
                                       xnbf[:, m * CT + t0:m * CT + t0 + TC],
                                       start=(m == 0), stop=(m == 7))
                    ctx.inc(ins, "sPE")
                block.tensor(mm)

                def conv(v, di=di):
                    ctx.w(v, "sPE")
                    ctx.w(v, "sACT")
                    v.tensor_copy(xi_sb[:, 0:3], stash[:, di * 3:(di + 1) * 3])
                    v.tensor_copy(xi_sb[:, 3:3 + TC], bank(3)[:, 0:TC])
                    v.tensor_copy(stash[:, di * 3:(di + 1) * 3],
                                  bank(3)[:, TC - 3:TC])
                    v.tensor_scalar_mul(conv_acc[:, 0:TC], xi_sb[:, 0:TC],
                                        convw_sb[:, di * 4:di * 4 + 1])
                    for j in range(1, 4):
                        ins = v.scalar_tensor_tensor(
                            conv_acc[:, 0:TC], xi_sb[:, j:j + TC],
                            convw_sb[:, di * 4 + j:di * 4 + j + 1],
                            conv_acc[:, 0:TC], ALU.mult, ALU.add)
                    ctx.inc(ins, "sDVE")
                BV(conv)

                def silu(a, di=di, t0=t0):
                    ctx.w(a, "sDVE")
                    ctx.inc(a.activation(
                        xconv[:, di * CT + t0:di * CT + t0 + TC],
                        conv_acc[:, 0:TC], AF.Silu,
                        bias=convb_sb[:, di:di + 1]), "sACT")
                BS(silu)

        def emit_halo(ci, halo_f, halo_xn, stash, sq, lnt):
            def ld(s, ci=ci):
                ctx.w(s, "sACT")
                ctx.w(s, "sDVE")
                ctx.w(s, "sPOOL")
                ctx.w(s, "sPE")
                ctx.inc(s.dma_start(out=halo_f[:, :],
                                    in_=d_in["halo"][:, ci * 24:(ci + 1) * 24]),
                        "sDX", 16)
            block.sync(ld)

            def wt(v):
                ctx.w(v, "sDX")
                ctx.inc(v.engine_nop(), "sDVE")
            BV(wt)
            hv = [halo_f[:, i * 3:(i + 1) * 3] for i in range(8)]
            emit_ln2(hv, 3, tiny[0:1, 8:11], tiny[0:1, 12:15],
                     tiny[0:1, 16:19], tiny[0:1, 20:23],
                     lambda i: halo_xn[:, i * 3:(i + 1) * 3],
                     sq, lnt, mask_col=hmask_sb[:, ci:ci + 1])
            for di in range(16):
                ring = wi_ring[di % 2]

                def ld2(s, di=di, ring=ring):
                    ctx.w(s, "sPE")
                    ctx.inc(s.dma_start(out=ring[:, 0:1024].rearrange("p (m f) -> p m f", m=8), in_=win_slice(di)),
                            "sDW", 16)
                block.sync(ld2)

                def mm(t, di=di, ring=ring):
                    ctx.w(t, "sDW")
                    ctx.w(t, "sDVE")
                    for m in range(8):
                        ins = t.matmul(bank(3)[:, 0:3],
                                       ring[:, m * 128:(m + 1) * 128],
                                       halo_xn[:, m * 3:(m + 1) * 3],
                                       start=(m == 0), stop=(m == 7))
                    ctx.inc(ins, "sPE")
                block.tensor(mm)

                def ev(v, di=di):
                    ctx.w(v, "sPE")
                    ctx.inc(v.tensor_copy(stash[:, di * 3:(di + 1) * 3],
                                          bank(3)[:, 0:3]), "sDVE")
                BV(ev)

        for ci in (range(NCH) if upto != "p0" else []):
            AR.push()
            xconv = AR.alloc(128, 16 * CT, BF16)
            AR.push()
            sq_t = AR.alloc(128, 8 * TC, F32)
            ln_t = AR.alloc(128, 8 * TC, F32)
            xi_sb = AR.alloc(128, TC + 3, F32)
            stash = AR.alloc(128, 16 * 3, F32)
            halo_xn = AR.alloc(128, 8 * 3, BF16)
            halo_f = AR.alloc(128, 8 * 3, F32)
            conv_acc = AR.alloc(128, TC, F32)

            emit_halo(ci, halo_f, halo_xn, stash, sq_t, ln_t)
            for tc in range(CT // TC):
                t0 = tc * TC
                srcv = [res[:, col(ci, dh, t0):col(ci, dh, t0) + TC] for dh in range(8)]
                emit_ln2(srcv, TC,
                         rows2[0:1, ci * CT + t0:ci * CT + t0 + TC],
                         rows2[32:33, ci * CT + t0:ci * CT + t0 + TC],
                         rows3[0:1, ci * CT + t0:ci * CT + t0 + TC],
                         rows3[32:33, ci * CT + t0:ci * CT + t0 + TC],
                         lambda i, t0=t0: xnbf[:, i * CT + t0:i * CT + t0 + TC],
                         sq_t, ln_t)
                emit_xinner_conv(ci, t0, xconv, xi_sb, stash, conv_acc)

                def wx_mm(t, ci=ci, t0=t0, xconv=xconv):
                    ctx.w(t, "sACT")
                    ctx.w(t, "sDVE")
                    for di in range(16):
                        ins = t.matmul(PS[0:65, 4 * 512:4 * 512 + TC],
                                       wx_sb[:, di * 65:(di + 1) * 65],
                                       xconv[:, di * CT + t0:di * CT + t0 + TC],
                                       start=(di == 0), stop=(di == 15))
                    ctx.inc(ins, "sPE")
                block.tensor(wx_mm)

                dt_r = rows1[64:65, ci * CT + t0:ci * CT + t0 + TC]
                b_r = rows1[0:16, ci * CT + t0:ci * CT + t0 + TC]
                c_r = rows1[32:48, ci * CT + t0:ci * CT + t0 + TC]
                scr = rows2[64:65, ci * CT + t0:ci * CT + t0 + TC]

                def dtrow1(v, dt_r=dt_r, scr=scr):
                    ctx.w(v, "sPE")
                    ctx.inc(v.tensor_scalar_max(
                        scr, PS[64:65, 4 * 512:4 * 512 + TC], 0.0), "sDVE")
                BV(dtrow1)

                def dtrow2(a, dt_r=dt_r):
                    ctx.w(a, "sDVE")
                    a.activation(dt_r, PS[64:65, 4 * 512:4 * 512 + TC], AF.Abs)
                    a.activation(dt_r, dt_r, AF.Exp, scale=-1.0)
                    ctx.inc(a.activation(dt_r, dt_r, AF.Ln, bias=ones_col[0:1, 0:1]), "sACT")
                BS(dtrow2)

                def dtrow3(v, dt_r=dt_r, scr=scr, b_r=b_r, c_r=c_r):
                    ctx.w(v, "sACT")
                    v.tensor_add(dt_r, dt_r, scr)
                    v.tensor_copy(c_r, PS[32:48, 4 * 512:4 * 512 + TC])
                    ctx.inc(v.tensor_copy(b_r, PS[0:16, 4 * 512:4 * 512 + TC]),
                            "sDVE")
                BV(dtrow3)

                def dtb16(t, dt_r=dt_r):
                    ctx.w(t, "sDVE")
                    ctx.inc(t.matmul(PS[0:16, 5 * 512:5 * 512 + TC],
                                     ones3[64:65, 0:16], dt_r,
                                     start=True, stop=True), "sPE")
                block.tensor(dtb16)

                def wBmul(v, b_r=b_r):
                    ctx.w(v, "sPE")
                    ctx.inc(v.tensor_mul(b_r, b_r, PS[0:16, 5 * 512:5 * 512 + TC]),
                            "sDVE")
                BV(wBmul)
            AR.pop()
            sc_dA = [AR.alloc(128, CT, F32) for _ in range(2)]
            sc_ub = [AR.alloc(128, CT, F32) for _ in range(2)]
            sc_h = [AR.alloc(128, CT, F32) for _ in range(2)]

            def sdt_red(v, ci=ci):
                v.tensor_reduce(sm[64:65, 428:429],
                                rows1[64:65, ci * CT:(ci + 1) * CT], AX.X, ALU.add)
                ctx.inc(v.engine_nop(), "sDVE")
            BV(sdt_red)

            def sdt_bc(t, ci=ci):
                ctx.w(t, "sDVE")
                t.matmul(bank(3)[:, 0:1], ones3[64:65, 0:128], sm[64:65, 428:429],
                         start=True, stop=True)
                ctx.inc(bcast(t, bank(0, 2),
                              rows1[64:65, ci * CT:(ci + 1) * CT], CT), "sPE")
            block.tensor(sdt_bc)

            def sdt_cp(v, ci=ci):
                ctx.w(v, "sPE")
                v.tensor_copy(sdt_sb[:, ci:ci + 1], bank(3)[:, 0:1])
                ctx.inc(v.tensor_copy(pub[:, ci * 257 + 256:ci * 257 + 257],
                                      bank(3)[:, 0:1]), "sDVE")
            BV(sdt_cp)

            for n in range(16):
                def nb(t, n=n, ci=ci):
                    ctx.w(t, "sDVE")
                    ctx.w(t, "sPOOL")
                    for i in range(2):
                        ins = t.matmul(PS[:, (2 + i) * 512:(3 + i) * 512],
                                       oneh[0:16, n * 128:(n + 1) * 128],
                                       rows1[0:16, ci * CT + i * 512:
                                             ci * CT + (i + 1) * 512],
                                       start=True, stop=True)
                    ctx.inc(ins, "sPE")
                block.tensor(nb)

                for db in range(16):
                    buf = (n * 16 + db) % 2
                    dA, ub, h = sc_dA[buf], sc_ub[buf], sc_h[buf]

                    def mk_dA(a, n=n, db=db, dA=dA):
                        ctx.w(a, "sPE")
                        ctx.w(a, "sDVE")
                        ctx.inc(a.activation(
                            dA[:, 0:CT], bank(0, 2), AF.Exp,
                            scale=A_sb[:, db * 16 + n:db * 16 + n + 1]), "sACT")
                    BS(mk_dA)

                    def mk_ub(v, db=db, ub=ub, xconv=xconv):
                        ctx.w(v, "sPE")
                        ctx.inc(v.tensor_mul(ub[:, 0:CT],
                                             xconv[:, db * CT:(db + 1) * CT],
                                             bank(2, 2)), "sDVE")
                    BV(mk_ub)

                    def do_scan(v, n=n, db=db, dA=dA, ub=ub, h=h, ci=ci):
                        ctx.w(v, "sACT")
                        ctx.w(v, "sPOOL")
                        ctx.w(v, "sDVE")
                        ctx.inc(v.tensor_tensor_scan(h[:, 0:CT], dA[:, 0:CT],
                                                     ub[:, 0:CT], 0.0,
                                                     ALU.mult, ALU.add), "sDVE")
                        ctx.w(v, "sDVE")
                        ctx.inc(v.tensor_copy(
                            pub[:, ci * 257 + db * 16 + n:ci * 257 + db * 16 + n + 1],
                            h[:, CT - 1:CT]), "sDVE")
                    BV(do_scan)

            def pubfence(v):
                ctx.w(v, "sPOOL")
                ctx.w(v, "sDVE")
                ctx.inc(v.engine_nop(), "sDVE")
            BV(pubfence)
            AR.pop()

        if upto in ("p0", "p1"):
            ctx.barrier()

            def out_dma0(s):
                ctx.inc(s.dma_start(out=yfm, in_=res[:, :]), "sOUT", 16)
                s.wait_ge(ctx.sems["sOUT"], ctx.cnt["sOUT"])
            block.sync(out_dma0)
            return nc

        def ag1(g):
            ctx.w(g, "sDVE")
            ctx.inc(g.dma_start(out=ag1_in[:, :], in_=pub[:, :]), "sDG", 16)
            g.wait_ge(ctx.sems["sDG"], ctx.cnt["sDG"])
            ctx.inc(g.collective_compute(
                "AllGather", ALU.bypass, replica_groups=[list(range(8))],
                ins=[ag1_in.ap()], outs=[ag1_out.ap()]), "sCC", 1)
            g.wait_ge(ctx.sems["sCC"], ctx.cnt["sCC"])
        BG(ag1)
        ctx.barrier()

        if upto == "ag":
            def out_dma1(s):
                ctx.inc(s.dma_start(out=yfm, in_=res[:, :]), "sOUT", 16)
                s.wait_ge(ctx.sems["sOUT"], ctx.cnt["sOUT"])
            block.sync(out_dma1)
            return nc

        for ci in range(NCH):
            AR.push()
            xconv3 = AR.alloc(128, 16 * CT, BF16)
            y_acc = AR.alloc(128, DGRP * CT, F32)
            xsc = AR.alloc(128, TC, F32)
            P_t = AR.alloc(128, 256, F32)

            def h0(v):
                ctx.inc(v.memset(hstart[:, :], 0.0), "sDVE")
            BV(h0)
            for s_ in range(7):
                rank = s_ if ci == 0 else 7 - s_

                def pld(g, rank=rank, ci=ci):
                    ctx.w(g, "sDVE")
                    ctx.inc(g.dma_start(
                        out=pred[:, :],
                        in_=ag1_out[rank * 128:(rank + 1) * 128,
                                    ci * 257:(ci + 1) * 257]), "sDG", 16)
                BG(pld)

                def hv1(v):
                    ctx.w(v, "sCC")
                    ctx.w(v, "sDG")
                    ctx.inc(v.tensor_scalar_mul(P_t, A_sb, pred[:, 256:257]),
                            "sDVE")
                BV(hv1)

                def ha(a):
                    ctx.w(a, "sDVE")
                    ctx.inc(a.activation(P_t, P_t, AF.Exp), "sACT")
                BS(ha)

                def hv2(v, s_=s_, ci=ci):
                    ctx.w(v, "sACT")
                    m = agm_sb[:, ci * 7 + s_:ci * 7 + s_ + 1]
                    v.tensor_scalar_add(P_t, P_t, -1.0)
                    v.tensor_scalar_mul(P_t, P_t, m)
                    v.tensor_scalar_add(P_t, P_t, 1.0)
                    v.tensor_mul(hstart[:, :], hstart[:, :], P_t)
                    ctx.inc(v.scalar_tensor_tensor(hstart[:, :], pred[:, 0:256], m,
                                                   hstart[:, :], ALU.mult, ALU.add),
                            "sDVE")
                BV(hv2)

            mu0 = rows2[0:1, ci * CT:(ci + 1) * CT]
            rs0 = rows2[32:33, ci * CT:(ci + 1) * CT]
            mui = rows3[0:1, ci * CT:(ci + 1) * CT]
            rsi = rows3[32:33, ci * CT:(ci + 1) * CT]

            for tc in range(2):
                t0 = tc * TC

                def xb1(t, t0=t0):
                    ctx.w(t, "sDVE")
                    bcast(t, bank(1), mu0[:, t0:t0 + TC], TC)
                    bcast(t, bank(2), rs0[:, t0:t0 + TC], TC)
                    bcast(t, bank(5), mui[:, t0:t0 + TC], TC)
                    ctx.inc(bcast(t, bank(6), rsi[:, t0:t0 + TC], TC), "sPE")
                block.tensor(xb1)

                def xv1(v, t0=t0, ci=ci):
                    ctx.w(v, "sPE")
                    for i in range(8):
                        o = xsc[:, 0:TC]
                        rv = res[:, col(ci, i, t0):col(ci, i, t0) + TC]
                        xno = xnbf[:, i * CT + t0:i * CT + t0 + TC]
                        v.tensor_sub(o, rv, bank(1)[:, 0:TC])
                        v.tensor_mul(o, o, bank(2)[:, 0:TC])
                        v.tensor_add(rv, rv, o)
                        v.tensor_sub(o, o, bank(5)[:, 0:TC])
                        v.tensor_mul(o, o, bank(6)[:, 0:TC])
                        ins = v.tensor_scalar(xno, o, gam_sb[:, i:i + 1],
                                              bet_sb[:, i:i + 1], ALU.mult, ALU.add)
                    ctx.inc(ins, "sDVE")
                BV(xv1)

            AR.push()
            sq3 = AR.alloc(128, 8 * 3, F32)
            lh3 = AR.alloc(128, 8 * 3, F32)
            xi3 = AR.alloc(128, TC + 3, F32)
            st3 = AR.alloc(128, 16 * 3, F32)
            hx3 = AR.alloc(128, 8 * 3, BF16)
            hf3 = AR.alloc(128, 8 * 3, F32)
            ca3 = AR.alloc(128, TC, F32)
            emit_halo(ci, hf3, hx3, st3, sq3, lh3)
            for tc in range(2):
                emit_xinner_conv(ci, tc * TC, xconv3, xi3, st3, ca3)
            AR.pop()

            AR.push()
            y3bf = AR.alloc(128, CT, BF16)
            s3_dA = [AR.alloc(128, CT, F32) for _ in range(2)]
            s3_ub = [AR.alloc(128, CT, F32) for _ in range(2)]
            s3_h = [AR.alloc(128, CT, F32) for _ in range(2)]
            zs_t = s3_dA[0]
            t2_t = s3_dA[1]

            for g0 in range(0, 16, DGRP):
                def dtb(t, ci=ci):
                    ctx.w(t, "sACT")
                    ctx.w(t, "sDVE")
                    ctx.inc(bcast(t, bank(0, 2),
                                  rows1[64:65, ci * CT:(ci + 1) * CT], CT), "sPE")
                block.tensor(dtb)

                scn3_h = {}
                dA3_h = {}
                ya_h = {}
                scn3_base = ctx.cnt["sDVE"]
                ya_base = ctx.cnt["sPOOL"]
                for n in range(16):
                    def nb(t, n=n, ci=ci):
                        ctx.w(t, "sDVE")
                        ctx.w(t, "sPOOL")
                        for i in range(2):
                            t.matmul(PS[:, (2 + i) * 512:(3 + i) * 512],
                                     oneh[0:16, n * 128:(n + 1) * 128],
                                     rows1[0:16, ci * CT + i * 512:
                                           ci * CT + (i + 1) * 512],
                                     start=True, stop=True)
                        for i in range(2):
                            ins = t.matmul(PS[:, (4 + i) * 512:(5 + i) * 512],
                                           oneh[32:48, n * 128:(n + 1) * 128],
                                           rows1[32:48, ci * CT + i * 512:
                                                 ci * CT + (i + 1) * 512],
                                           start=True, stop=True)
                        ctx.inc(ins, "sPE")
                    block.tensor(nb)
                    nb_pe3 = ctx.cnt["sPE"]

                    for db in range(g0, g0 + DGRP):
                        idx = n * DGRP + (db - g0)
                        buf = idx % 2
                        dA, ub, h = s3_dA[buf], s3_ub[buf], s3_h[buf]

                        def mk_dA(a, n=n, db=db, dA=dA, idx=idx, nb_pe3=nb_pe3):
                            ctx.w(a, "sPE", nb_pe3)
                            ctx.w(a, "sDVE", scn3_h.get(idx - 2, scn3_base))
                            ctx.inc(a.activation(
                                dA[:, 0:CT], bank(0, 2), AF.Exp,
                                scale=A_sb[:, db * 16 + n:db * 16 + n + 1]), "sACT")
                            dA3_h[idx] = ctx.cnt["sACT"]
                        BS(mk_dA)

                        def mk_ub(v, db=db, ub=ub, idx=idx, nb_pe3=nb_pe3):
                            ctx.w(v, "sPE", nb_pe3)
                            ctx.w(v, "sPOOL", ya_h.get(idx - 2, ya_base))
                            ctx.inc(v.tensor_mul(ub[:, 0:CT],
                                                 xconv3[:, db * CT:(db + 1) * CT],
                                                 bank(2, 2)), "sDVE")
                        BV(mk_ub)

                        def do_scan(v, n=n, db=db, dA=dA, ub=ub, h=h, idx=idx):
                            ctx.w(v, "sACT", dA3_h[idx])
                            v.tensor_tensor_scan(
                                h[:, 0:CT], dA[:, 0:CT], ub[:, 0:CT],
                                hstart[:, db * 16 + n:db * 16 + n + 1],
                                ALU.mult, ALU.add)
                            v.tensor_mul(ub[:, 0:CT], h[:, 0:CT],
                                         bank(4, 2))
                            scn3_h[idx] = ctx.cnt["sDVE"]
                        BV(do_scan)

                        def y_add(g_, n=n, db=db, ub=ub, g0=g0, idx=idx):
                            ctx.w(g_, "sDVE", scn3_h[idx])
                            ya = y_acc[:, (db - g0) * CT:(db - g0 + 1) * CT]
                            if n == 0:
                                ins = g_.tensor_copy(ya, ub[:, 0:CT])
                            else:
                                ins = g_.tensor_add(ya, ya, ub[:, 0:CT])
                            ya_h[idx] = ctx.inc(ins, "sPOOL")
                        BG(y_add)

                for db in range(g0, g0 + DGRP):
                    ring = wi_ring[db % 2]

                    def zld(s, db=db, ring=ring):
                        ctx.w(s, "sPE")
                        ctx.inc(s.dma_start(out=ring[:, 0:1024].rearrange("p (m f) -> p m f", m=8),
                                            in_=win_slice(db, zhalf=True)),
                                "sDW", 16)
                    block.sync(zld)

                    def zmm(t, db=db, ring=ring):
                        ctx.w(t, "sDW")
                        ctx.w(t, "sACT")
                        ctx.w(t, "sPOOL")
                        for m in range(8):
                            for half in range(2):
                                ins = t.matmul(
                                    PS[:, half * 512:(half + 1) * 512],
                                    ring[:, m * 128:(m + 1) * 128],
                                    xnbf[:, m * CT + half * TC:
                                         m * CT + (half + 1) * TC],
                                    start=(m == 0), stop=(m == 7))
                        ctx.inc(ins, "sPE")
                    block.tensor(zmm)

                    def zsilu(a):
                        ctx.w(a, "sPE")
                        a.activation(zs_t[:, 0:TC], bank(0), AF.Silu)
                        ctx.inc(a.activation(zs_t[:, TC:CT], bank(1), AF.Silu),
                                "sACT")
                    BS(zsilu)

                    def gate(v, db=db, g0=g0):
                        ctx.w(v, "sACT")
                        ctx.w(v, "sPOOL")
                        ctx.w(v, "sPE")
                        ya = y_acc[:, (db - g0) * CT:(db - g0 + 1) * CT]
                        xc = xconv3[:, db * CT:(db + 1) * CT]
                        v.scalar_tensor_tensor(t2_t[:, 0:CT], xc,
                                               D_sb[:, db:db + 1], ya,
                                               ALU.mult, ALU.add)
                        ctx.inc(v.tensor_mul(y3bf[:, 0:CT], t2_t[:, 0:CT],
                                             zs_t[:, 0:CT]), "sDVE")
                    BV(gate)

                    def wld(s, db=db):
                        ctx.w(s, "sPE")
                        ctx.inc(s.dma_start(
                            out=wo_ring[db % 2][:, 0:1024],
                            in_=woutb[db * 128:(db + 1) * 128, :]), "sDW", 16)
                    block.sync(wld)

                    for om in range(8):
                        def wmm(t, om=om, db=db):
                            ctx.w(t, "sDW")
                            ctx.w(t, "sDVE")
                            b0 = 2 + (om % 2) * 2
                            for half in range(2):
                                ins = t.matmul(
                                    PS[:, (b0 + half) * 512:(b0 + half + 1) * 512],
                                    wo_ring[db % 2][:, om * 128:(om + 1) * 128],
                                    y3bf[:, half * TC:(half + 1) * TC],
                                    start=True, stop=True)
                            ctx.inc(ins, "sPE")
                        block.tensor(wmm)

                        def wadd(v, om=om, ci=ci):
                            ctx.w(v, "sPE")
                            b0 = 2 + (om % 2) * 2
                            rv = res[:, col(ci, om, 0):col(ci, om, 0) + CT]
                            ctx.inc(v.tensor_add(
                                rv, rv, PS[:, b0 * 512:(b0 + 2) * 512]), "sDVE")
                        BV(wadd)
            AR.pop()
            AR.pop()
        ctx.barrier()

        def emit_out():
            def out_dma(s):
                ctx.inc(s.dma_start(out=yfm, in_=res[:, :]), "sOUT", 16)
                s.wait_ge(ctx.sems["sOUT"], ctx.cnt["sOUT"])
            block.sync(out_dma)

        if upto == "mamba":
            emit_out()
            return nc

        def emit_ln1(ci, sqf, osc):
            for tc in range(2):
                t0 = tc * TC
                srcv = [res[:, col(ci, dh, t0):col(ci, dh, t0) + TC]
                        for dh in range(8)]
                mu_r = rows2[0:1, ci * CT + t0:ci * CT + t0 + TC]
                rs_r = rows2[32:33, ci * CT + t0:ci * CT + t0 + TC]

                def s_sq(a, srcv=srcv):
                    ctx.w(a, "sDVE")
                    for i in range(8):
                        a.activation(sqf[:, i * TC:(i + 1) * TC], srcv[i], AF.Square)
                BS(s_sq)

                def s_pe(t, srcv=srcv):
                    ctx.w(t, "sACT")
                    ctx.w(t, "sDVE")
                    for i in range(8):
                        t.matmul(PS[0:1, 0:TC], ones_col, srcv[i],
                                 start=(i == 0), stop=(i == 7))
                    for i in range(8):
                        ins = t.matmul(PS[32:33, 0:TC], ones_col,
                                       sqf[:, i * TC:(i + 1) * TC],
                                       start=(i == 0), stop=(i == 7))
                    ctx.inc(ins, "sPE")
                block.tensor(s_pe)

                def s_rows(v, mu_r=mu_r, rs_r=rs_r):
                    ctx.w(v, "sPE")
                    v.tensor_scalar_mul(mu_r, PS[0:1, 0:TC], 1.0 / D)
                    v.tensor_scalar_mul(rs_r, PS[32:33, 0:TC], 1.0 / D)
                    tmp = PS[32:33, 3 * 512:3 * 512 + TC]
                    v.tensor_mul(tmp, mu_r, mu_r)
                    v.tensor_sub(rs_r, rs_r, tmp)
                BV(s_rows)

                def s_rstd(a, rs_r=rs_r):
                    ctx.w(a, "sDVE")
                    a.activation(rs_r, rs_r, AF.Ln, bias=eps_col[0:1, 0:1])
                    a.activation(rs_r, rs_r, AF.Exp, scale=-0.5)
                BS(s_rstd)

                def s_bc(t, mu_r=mu_r, rs_r=rs_r):
                    ctx.w(t, "sACT")
                    bcast(t, bank(1), mu_r, TC)
                    ctx.inc(bcast(t, bank(2), rs_r, TC), "sPE")
                block.tensor(s_bc)

                def s_fin(v, srcv=srcv, t0=t0, ci=ci):
                    ctx.w(v, "sPE")
                    for i in range(8):
                        v.tensor_sub(osc[:, 0:TC], srcv[i], bank(1)[:, 0:TC])
                        v.tensor_mul(xnbf[:, i * CT + t0:i * CT + t0 + TC],
                                     osc[:, 0:TC], bank(2)[:, 0:TC])
                BV(s_fin)

        gate_sb = AR.alloc(128, 64, BF16)
        gateb = d_bf["gate_w"]
        w1b = d_bf["w1"]
        w2b = d_bf["w2"]

        def p_load_gate(s):
            ctx.inc(s.dma_start(
                out=gate_sb[:, :].rearrange("p (m e) -> p m e", m=8),
                in_=gateb.rearrange("(m p) e -> p m e", p=128)), "sDW", 16)
        block.sync(p_load_gate)

        lq_state = [0]
        lq_vals = {}
        lq_done = {}
        lq_tick = {}

        def emit_ffn(ci):
            AR.push()
            sqf = AR.alloc(128, 8 * TC, F32)
            osc = AR.alloc(128, TC, F32)
            emit_ln1(ci, sqf, osc)
            AR.pop()
            AR.push()
            glt = AR.alloc(128, CT, F32)
            h_e = AR.alloc(128, 32 * CT, BF16)
            w2r = AR.alloc(128, 32 * 128, BF16)

            lg = tiny[:, 0:8]
            m8 = tiny[:, 8:16]
            mv = tiny[:, 16:24]
            pr = tiny[:, 24:32]
            cmb = tiny[:, 32:40]
            ngm = tiny[:, 40:41]
            ssum = tiny[:, 41:42]
            den = tiny[:, 42:43]
            for tt in range(8):
                def g_pe(t, tt=tt, ci=ci):
                    ctx.w(t, "sDVE")
                    ctx.w(t, "sDW")
                    for m in range(8):
                        ins = t.matmul(
                            PS[:, 5 * 512:5 * 512 + 8],
                            xnbf[:, m * CT + tt * 128:m * CT + (tt + 1) * 128],
                            gate_sb[:, m * 8:(m + 1) * 8],
                            start=(m == 0), stop=(m == 7))
                    ctx.inc(ins, "sPE")
                block.tensor(g_pe)

                def g_v1(v):
                    ctx.w(v, "sPE")
                    v.tensor_copy(lg, PS[:, 5 * 512:5 * 512 + 8])
                    v.reduce_max(ngm, lg, AX.X, negate=True)
                    v.tensor_scalar_add(lg, lg, ngm[:, 0:1])
                BV(g_v1)

                def g_a1(a):
                    ctx.w(a, "sDVE")
                    a.activation(lg, lg, AF.Exp)
                BS(g_a1)

                def g_v2(v):
                    ctx.w(v, "sACT")
                    v.reduce_sum(ssum, lg, AX.X)
                    v.reciprocal(ssum, ssum)
                    v.tensor_scalar_mul(lg, lg, ssum[:, 0:1])
                    v.max(out=m8, in_=lg)
                    v.tensor_copy(mv[:, 0:2], m8[:, 0:2])
                    v.tensor_copy(mv[:, 2:8], neg1_sb[:, 2:8])
                    v.match_replace(pr, mv, lg, -1.0)
                    v.tensor_sub(pr, lg, pr)
                    v.tensor_scalar_min(pr, pr, 1.0)
                    v.tensor_add(den, m8[:, 0:1], m8[:, 1:2])
                    v.reciprocal(den, den)
                    v.tensor_mul(cmb, pr, lg)
                    v.tensor_scalar_mul(cmb, cmb, den[:, 0:1])
                BV(g_v2)

                def g_tr(t, tt=tt):
                    ctx.w(t, "sDVE")
                    ctx.inc(t.transpose(PS[0:8, 4 * 512:4 * 512 + 128], cmb,
                                        ident_sb), "sPE")
                block.tensor(g_tr)

                def g_cp(v, tt=tt, ci=ci):
                    ctx.w(v, "sPE")
                    v.tensor_copy(rows1[0:8, ci * CT + tt * 128:
                                        ci * CT + (tt + 1) * 128],
                                  PS[0:8, 4 * 512:4 * 512 + 128])
                BV(g_cp)

            for e in range(E):
                def c_bc(t, e=e, ci=ci):
                    ctx.w(t, "sDVE")
                    for i in range(2):
                        ins = t.matmul(PS[:, (6 + i) * 512:(7 + i) * 512],
                                       oneh[0:16, e * 128:(e + 1) * 128],
                                       rows1[0:16, ci * CT + i * 512:
                                             ci * CT + (i + 1) * 512],
                                       start=True, stop=True)
                    ctx.inc(ins, "sPE")
                block.tensor(c_bc)

                pe_hist = {}
                act_hist = {}
                dve_hist = {}
                pe_base = ctx.cnt["sPE"]
                act_base = ctx.cnt["sACT"]
                dve_base = ctx.cnt["sDVE"]

                def lq_issue(s, out, in_):
                    i = lq_state[0]
                    sem = "sLA" if i % 2 == 0 else "sLB"
                    if i >= 1:
                        psem = "sLA" if (i - 1) % 2 == 0 else "sLB"
                        s.wait_ge(ctx.sems[psem], lq_vals[i - 1])
                        s.sem_inc(ctx.sems["sLD"], 1)
                        ctx.cnt["sLD"] += 1
                        lq_done[i - 1] = ctx.cnt["sLD"]
                    lq_vals[i] = ctx.inc(s.dma_start(out=out, in_=in_), sem, 16)
                    lq_state[0] = i + 1

                def lq_flush(s):
                    i = lq_state[0] - 1
                    if i >= 0 and i not in lq_done:
                        psem = "sLA" if i % 2 == 0 else "sLB"
                        s.wait_ge(ctx.sems[psem], lq_vals[i])
                        s.sem_inc(ctx.sems["sLD"], 1)
                        ctx.cnt["sLD"] += 1
                        lq_done[i] = ctx.cnt["sLD"]

                def w1mm(t, fi, e):
                    ctx.w(t, "sLD", lq_done[lq_tick[("w1", e, fi)]])
                    ctx.w(t, "sACT", act_hist.get(fi - 2, act_base))
                    if fi < 2:
                        ctx.w(t, "sDVE", dve_base)
                    b0 = (fi % 2) * 2
                    for m in range(8):
                        for half in range(2):
                            ins = t.matmul(
                                PS[:, (b0 + half) * 512:(b0 + half + 1) * 512],
                                wi_ring[fi % 2][:, m * 128:(m + 1) * 128],
                                xnbf[:, m * CT + half * TC:
                                     m * CT + (half + 1) * TC],
                                start=(m == 0), stop=(m == 7))
                    pe_hist[fi] = ctx.inc(ins, "sPE")

                def w1ge(a, fi):
                    ctx.w(a, "sPE", pe_hist[fi])
                    ctx.w(a, "sDVE", dve_hist.get(fi - 1, dve_base))
                    b0 = (fi % 2) * 2
                    a.activation(glt[:, 0:TC], bank(b0), AF.Gelu)
                    a.activation(glt[:, TC:CT], bank(b0 + 1), AF.Gelu)
                    act_hist[fi] = ctx.cnt["sACT"]

                def w1cb(v, fi):
                    ctx.w(v, "sACT", act_hist[fi])
                    if fi == 0:
                        ctx.w(v, "sPE")
                    v.tensor_mul(h_e[:, fi * CT:(fi + 1) * CT],
                                 glt[:, 0:CT], bank(6, 2))
                    dve_hist[fi] = ctx.cnt["sDVE"]

                for fi in range(33):
                    if fi < 32:
                        def w1ld(s, e=e, fi=fi):
                            ctx.w(s, "sW12", w12_done[0])
                            ctx.w(s, "sPE", pe_hist.get(fi - 2, pe_base))
                            lq_tick[("w1", e, fi)] = lq_state[0]
                            lq_issue(s, wi_ring[fi % 2][:, 0:1024].rearrange(
                                         "p (m f) -> p m f", m=8),
                                     w1b[e * D:(e + 1) * D,
                                         fi * 128:(fi + 1) * 128].rearrange(
                                         "(m p) f -> p m f", p=128))
                        block.sync(w1ld)
                    else:
                        block.sync(lambda s: lq_flush(s))
                    if fi >= 1:
                        block.tensor(lambda t, fj=fi - 1, e=e: w1mm(t, fj, e))
                        BS(lambda a, fj=fi - 1: w1ge(a, fj))
                        BV(lambda v, fj=fi - 1: w1cb(v, fj))

                pe2_hist = {}
                dve2_hist = {}
                pe2_base = ctx.cnt["sPE"]
                dve2_base = ctx.cnt["sDVE"]

                def w2mm(t, om, hh, e):
                    ctx.w(t, "sLD", lq_done[lq_tick[("w2", e, om, hh)]])
                    if hh == 0:
                        ctx.w(t, "sDVE", dve2_hist.get(om - 2, dve2_base))
                    b0 = 2 + (om % 2) * 2
                    for fh in range(16):
                        fg = hh * 16 + fh
                        for half in range(2):
                            ins = t.matmul(
                                PS[:, (b0 + half) * 512:(b0 + half + 1) * 512],
                                w2r[:, hh * 2048 + fh * 128:
                                    hh * 2048 + (fh + 1) * 128],
                                h_e[:, fg * CT + half * TC:
                                    fg * CT + (half + 1) * TC],
                                start=(fg == 0), stop=(fg == 31))
                    pe2_hist[(om, hh)] = ctx.inc(ins, "sPE")

                def w2add(v, om, ci):
                    ctx.w(v, "sPE", pe2_hist[(om, 1)])
                    b0 = 2 + (om % 2) * 2
                    rv = res[:, col(ci, om, 0):col(ci, om, 0) + CT]
                    v.tensor_add(rv, rv, PS[:, b0 * 512:(b0 + 2) * 512])
                    dve2_hist[om] = ctx.cnt["sDVE"]

                jobs = [(om, hh) for om in range(8) for hh in range(2)]
                for j in range(17):
                    if j < 16:
                        om, hh = jobs[j]

                        def w2ld(s, e=e, om=om, hh=hh):
                            ctx.w(s, "sW12", w12_done[0])
                            ctx.w(s, "sPE", pe2_hist.get((om, hh - 2),
                                  pe2_hist.get((om - 1, hh), pe2_base)))
                            lq_tick[("w2", e, om, hh)] = lq_state[0]
                            lq_issue(s, w2r[:, hh * 2048:(hh + 1) * 2048].rearrange(
                                         "p (fh o) -> p fh o", fh=16),
                                     w2b[e * FF + hh * 2048:
                                         e * FF + (hh + 1) * 2048,
                                         om * 128:(om + 1) * 128].rearrange(
                                         "(fh p) o -> p fh o", p=128))
                        block.sync(w2ld)
                    else:
                        block.sync(lambda s: lq_flush(s))
                    if j >= 1:
                        pom, phh = jobs[j - 1]
                        block.tensor(lambda t, pom=pom, phh=phh, e=e:
                                     w2mm(t, pom, phh, e))
                        if phh == 1:
                            BV(lambda v, pom=pom, ci=ci: w2add(v, pom, ci))
            AR.pop()

        for ci in range(NCH):
            emit_ffn(ci)
        ctx.barrier()
        if upto == "ffn1":
            emit_out()
            return nc

        kvdb = d_bf["w_kvd"]
        kupb_d = d_bf["w_kup"]
        vupb_d = d_bf["w_vup"]
        wqb_d = d_bf["w_q"]
        wob_d = d_bf["w_o"]
        wqrb_d = d_bf["wqr_ext"]
        wkrb_d = d_bf["wkr_ext"]

        AR.push()
        sqf = AR.alloc(128, 8 * TC, F32)
        osc = AR.alloc(128, TC, F32)
        kvd_sb = AR.alloc(128, 1024, BF16)
        ckv_own = AR.alloc(128, NCH * CT, BF16)

        def kvd_ld(s):
            ctx.inc(s.dma_start(
                out=kvd_sb[:, :].rearrange("p (m f) -> p m f", m=8),
                in_=kvdb.rearrange("(m p) f -> p m f", p=128)), "sDW", 16)
        block.sync(kvd_ld)

        for ci in range(NCH):
            emit_ln1(ci, sqf, osc)

            def ckv_mm(t, ci=ci):
                ctx.w(t, "sDW")
                ctx.w(t, "sDVE")
                for m in range(8):
                    for half in range(2):
                        ins = t.matmul(PS[:, half * 512:(half + 1) * 512],
                                       kvd_sb[:, m * 128:(m + 1) * 128],
                                       xnbf[:, m * CT + half * TC:
                                            m * CT + (half + 1) * TC],
                                       start=(m == 0), stop=(m == 7))
                ctx.inc(ins, "sPE")
            block.tensor(ckv_mm)

            def ckv_ev(v, ci=ci):
                ctx.w(v, "sPE")
                v.tensor_copy(ckv_own[:, ci * CT:(ci + 1) * CT], bank(0, 2))
            BV(ckv_ev)

            def ckv_st(s, ci=ci):
                ctx.w(s, "sDVE")
                ctx.inc(s.dma_start(out=ag2_in[:, ci * CT:(ci + 1) * CT],
                                    in_=ckv_own[:, ci * CT:(ci + 1) * CT]),
                        "sDX", 16)
            block.sync(ckv_st)

        def ag2(g):
            ctx.w(g, "sDX")
            ctx.inc(g.collective_compute(
                "AllGather", ALU.bypass, replica_groups=[list(range(8))],
                ins=[ag2_in.ap()], outs=[ag2_out.ap()]), "sCC", 1)
            g.wait_ge(ctx.sems["sCC"], ctx.cnt["sCC"])
        block.gpsimd(ag2)
        AR.pop()
        ctx.barrier()

        AR.push()
        kupb = AR.alloc(128, 1024, BF16)
        vupb = AR.alloc(128, 1024, BF16)
        wkrb = AR.alloc(128, 64, BF16)
        ckv_own2 = AR.alloc(128, NCH * CT, BF16)
        k97 = AR.alloc(128, T, BF16)
        ckv_kv = AR.alloc(128, T, BF16)
        ropesK_b = [AR.alloc(64, TC, BF16) for _ in range(2)]
        ropeown = AR.alloc(128, CT, BF16)
        masks_sb = AR.alloc(128, 2048, BF16)
        v_tok = AR.alloc(128, 64 * 65, BF16)
        q97 = AR.alloc(128, CT, BF16)
        kdiag = AR.alloc(128, CT, BF16)
        PT = [AR.alloc(128, CT, BF16) for _ in range(2)]
        ptb = AR.alloc(128, 2 * CT, BF16)
        PT.append(ptb[:, 0:CT])
        PT.append(ptb[:, CT:2 * CT])
        T1 = ptb[0:32, 0:CT].bitcast(F32)
        T2 = ptb[0:32, CT:2 * CT].bitcast(F32)
        vdiag = AR.alloc(128, 8 * 65, BF16)
        attnp = AR.alloc(128, CT, BF16)
        qsq = AR.alloc(128, TC, F32)
        qsq2 = AR.alloc(128, TC, F32)
        attn_tmp = qsq[0:64, 0:TC]
        wq_r = wring_t.ap()[:, 0:512]
        wqrb = wring_t.ap()[:, 512:1024]
        kvd_sb2 = wring_t.ap()[:, 1024:2048]

        def attn_ld0(s):
            ctx.w(s, "sPE")
            ctx.w(s, "sDVE")
            ctx.w(s, "sACT")
            ctx.inc(s.dma_start(out=kupb[:, :], in_=kupb_d[:, :]), "sDW", 16)
            ctx.inc(s.dma_start(out=vupb[:, :], in_=vupb_d[:, :]), "sDW", 16)
            ctx.inc(s.dma_start(out=wkrb[:, :], in_=wkrb_d[:, :]), "sDW", 16)
            ctx.inc(s.dma_start(
                out=wqrb.rearrange("p (m f) -> p m f", m=8),
                in_=wqrb_d.rearrange("(m p) f -> p m f", p=128)), "sDW", 16)
            ctx.inc(s.dma_start(out=masks_sb[:, :], in_=d_in["masksbf"]), "sDW", 16)
            ctx.inc(s.dma_start(
                out=kvd_sb2.rearrange("p (m f) -> p m f", m=8),
                in_=kvdb.rearrange("(m p) f -> p m f", p=128)), "sDW", 16)
        block.sync(attn_ld0)

        sqa = k97.bitcast(F32)
        osca = ckv_kv[:, 0:1024].bitcast(F32)

        for ci in range(NCH):
            emit_ln1(ci, sqa, osca)

            def ckv_mm2(t, ci=ci):
                ctx.w(t, "sDW")
                ctx.w(t, "sDVE")
                for m in range(8):
                    for half in range(2):
                        ins = t.matmul(PS[:, half * 512:(half + 1) * 512],
                                       kvd_sb2[:, m * 128:(m + 1) * 128],
                                       xnbf[:, m * CT + half * TC:
                                            m * CT + (half + 1) * TC],
                                       start=(m == 0), stop=(m == 7))
                ctx.inc(ins, "sPE")
            block.tensor(ckv_mm2)

            def ckv_ev2(v, ci=ci):
                ctx.w(v, "sPE")
                v.tensor_copy(ckv_own2[:, ci * CT:(ci + 1) * CT], bank(0, 2))
            BV(ckv_ev2)

        for ci in range(NCH):
            emit_ln1(ci, sqa, osca)

            def kv_loads(g, ci=ci):
                ctx.w(g, "sACT")
                ctx.w(g, "sPE")
                ctx.w(g, "sDVE")
                for kb in range(8):
                    rank = kb if ci == 0 else 7 - kb
                    ctx.inc(g.dma_start(
                        out=ckv_kv[:, kb * CT:(kb + 1) * CT],
                        in_=ag2_out[rank * 128:(rank + 1) * 128,
                                    ci * CT:(ci + 1) * CT]), "sDG", 16)
            block.gpsimd(kv_loads)

            def ro_ld(s, ci=ci):
                ctx.w(s, "sPE")
                ctx.w(s, "sACT")
                ctx.inc(s.dma_start(
                    out=ropeown[:, :],
                    in_=d_in["ropesbf_own"][:, ci * CT:(ci + 1) * CT]),
                    "sDX", 16)
            block.sync(ro_ld)

            def kq_init(v, ci=ci):
                ctx.w(v, "sDG")
                ctx.w(v, "sDX")
                ctx.w(v, "sPE")
                v.memset(k97[0:32, :], 0.0)
                v.memset(q97[0:32, :], 0.0)
                v.memset(kdiag[0:32, :], 0.0)
                v.memset(kdiag[0:1, :], 1.0)
                v.memset(v_tok[:, :], 1.0)
                v.memset(vdiag[:, :], 1.0)
            BV(kq_init)

            def aug_ld(g, ci=ci):
                ctx.w(g, "sDVE")
                ctx.inc(g.dma_start(out=k97[0:4, :],
                                    in_=d_in["kaug"][ci * 4:(ci + 1) * 4, :]),
                        "sDG", 16)
                ctx.inc(g.dma_start(out=q97[0:4, :],
                                    in_=d_in["qaug"][:, :]), "sDG", 16)
            block.gpsimd(aug_ld)

            for s_ in range(16):
                def kr_ld(s, s_=s_):
                    ctx.w(s, "sDVE")
                    ctx.inc(s.dma_start(
                        out=ropesK_b[s_ % 2][:, :],
                        in_=d_in["ropesbf"][64:128, s_ * 512:(s_ + 1) * 512]),
                        "sDX", 16)
                block.sync(kr_ld)

                def kr_mm(t, s_=s_):
                    ctx.w(t, "sDVE")
                    ctx.w(t, "sDG")
                    ctx.inc(t.matmul(
                        PS[0:64, (s_ % 2) * 1024:(s_ % 2) * 1024 + 512],
                        wkrb[:, 0:64],
                        ckv_kv[:, s_ * 512:(s_ + 1) * 512],
                        start=True, stop=True), "sPE")
                block.tensor(kr_mm)

                def kr_rot(v, s_=s_):
                    ctx.w(v, "sPE")
                    ctx.w(v, "sDX")
                    b0 = (s_ % 2) * 1024
                    v.tensor_mul(T1[:, 0:TC], PS[0:32, b0:b0 + 512],
                                 ropesK_b[s_ % 2][0:32, :])
                    v.tensor_mul(T2[:, 0:TC], PS[32:64, b0:b0 + 512],
                                 ropesK_b[s_ % 2][32:64, :])
                    v.tensor_add(k97[32:64, s_ * 512:(s_ + 1) * 512],
                                 T1[:, 0:TC], T2[:, 0:TC])
                BV(kr_rot)

            for half in range(2):
                def kdr_mm(t, half=half, ci=ci):
                    ctx.w(t, "sDVE")
                    ctx.inc(t.matmul(
                        PS[0:64, half * 1024:half * 1024 + 512],
                        wkrb[:, 0:64],
                        ckv_own2[:, ci * CT + half * TC:
                                 ci * CT + (half + 1) * TC],
                        start=True, stop=True), "sPE")
                block.tensor(kdr_mm)

                def kdr_rot(v, half=half):
                    ctx.w(v, "sPE")
                    b0 = half * 1024
                    v.tensor_mul(T1[:, 0:TC], PS[0:32, b0:b0 + 512],
                                 ropeown[64:96, half * TC:(half + 1) * TC])
                    v.tensor_mul(T2[:, 0:TC], PS[32:64, b0:b0 + 512],
                                 ropeown[96:128, half * TC:(half + 1) * TC])
                    v.tensor_add(kdiag[32:64, half * TC:(half + 1) * TC],
                                 T1[:, 0:TC], T2[:, 0:TC])
                BV(kdr_rot)

            for half in range(2):
                def qr_mm(t, half=half):
                    ctx.w(t, "sDVE")
                    ctx.w(t, "sDW")
                    for m in range(8):
                        ins = t.matmul(
                            PS[0:64, half * 1024:half * 1024 + 512],
                            wqrb[:, m * 64:(m + 1) * 64],
                            xnbf[:, m * CT + half * TC:
                                 m * CT + (half + 1) * TC],
                            start=(m == 0), stop=(m == 7))
                    ctx.inc(ins, "sPE")
                block.tensor(qr_mm)

                def qr_rot(v, half=half):
                    ctx.w(v, "sPE")
                    b0 = half * 1024
                    v.tensor_mul(T1[:, 0:TC], PS[0:32, b0:b0 + 512],
                                 ropeown[0:32, half * TC:(half + 1) * TC])
                    v.tensor_mul(T2[:, 0:TC], PS[32:64, b0:b0 + 512],
                                 ropeown[32:64, half * TC:(half + 1) * TC])
                    v.tensor_add(q97[32:64, half * TC:(half + 1) * TC],
                                 T1[:, 0:TC], T2[:, 0:TC])
                BV(qr_rot)

            q2r = rows2[0:1, 0:1024]
            kn2 = rows2[0:1, 1024:2048]
            kmaxc = rows2[0:1, 2047:2048]

            for h in range(NH):
                def q_ld(s, h=h):
                    ctx.w(s, "sPE")
                    ctx.inc(s.dma_start(
                        out=wq_r.rearrange("p (m f) -> p m f", m=8),
                        in_=wqb_d[:, h * 64:(h + 1) * 64].rearrange(
                            "(m p) f -> p m f", p=128)), "sDW", 16)
                block.sync(q_ld)

                def q_mm(t, h=h):
                    ctx.w(t, "sDW")
                    ctx.w(t, "sACT")
                    ctx.w(t, "sDVE")
                    for m in range(8):
                        for half in range(2):
                            ins = t.matmul(
                                PS[64:128, half * 512:half * 512 + 512],
                                wq_r[:, m * 64:(m + 1) * 64],
                                xnbf[:, m * CT + half * TC:
                                     m * CT + (half + 1) * TC],
                                start=(m == 0), stop=(m == 7))
                    ctx.inc(ins, "sPE")
                block.tensor(q_mm)

                def q_ev(v, h=h):
                    ctx.w(v, "sPE")
                    v.tensor_scalar_mul(q97[64:128, :], PS[64:128, 0:1024],
                                        float(S96))
                BV(q_ev)

                for half in range(2):
                    def q_sq(v, half=half):
                        ctx.w(v, "sPE")
                        qa = q97[64:128, half * TC:(half + 1) * TC]
                        qb = q97[32:64, half * TC:(half + 1) * TC]
                        v.tensor_mul(qsq[64:128, :], qa, qa)
                        v.tensor_mul(qsq[32:64, :], qb, qb)
                    BV(q_sq)

                    def q_n(t, half=half):
                        ctx.w(t, "sDVE")
                        t.matmul(PS[0:1, 4 * 512 + half * 512:
                                     4 * 512 + half * 512 + 512],
                                 ones_col[64:128, 0:1], qsq[64:128, :],
                                 start=True, stop=False)
                        ins = t.matmul(PS[0:1, 4 * 512 + half * 512:
                                          4 * 512 + half * 512 + 512],
                                       ones_col[32:64, 0:1], qsq[32:64, :],
                                       start=False, stop=True)
                        ctx.inc(ins, "sPE")
                    block.tensor(q_n)

                def q_n2(v, h=h):
                    ctx.w(v, "sPE")
                    v.tensor_copy(q2r, PS[0:1, 4 * 512:4 * 512 + 1024])
                BV(q_n2)

                kev_h = {}
                knm_h = {}
                knx_h = {}
                kev_base = ctx.cnt["sDVE"]
                for s_ in range(16):
                    def k_mm(t, h=h, s_=s_):
                        ctx.w(t, "sDVE", kev_h.get(s_ - 2, kev_base))
                        if s_ == 0:
                            ctx.w(t, "sACT")
                            ctx.w(t, "sDVE")
                        ctx.inc(t.matmul(
                            PS[64:128, (s_ % 2) * 512:(s_ % 2) * 512 + 512],
                            kupb[:, h * 64:(h + 1) * 64],
                            ckv_kv[:, s_ * 512:(s_ + 1) * 512],
                            start=True, stop=True), "sPE")
                        knm_h[("k", s_)] = ctx.cnt["sPE"]
                    block.tensor(k_mm)

                    def k_ev(v, s_=s_):
                        ctx.w(v, "sPE", knm_h[("k", s_)])
                        sl = slice(s_ * 512, (s_ + 1) * 512)
                        qs = qsq if s_ % 2 == 0 else qsq2
                        v.tensor_copy(k97[64:128, sl],
                                      PS[64:128, (s_ % 2) * 512:
                                         (s_ % 2) * 512 + 512])
                        v.tensor_mul(qs[64:128, 0:512],
                                     k97[64:128, sl], k97[64:128, sl])
                        v.tensor_mul(qs[32:64, 0:512],
                                     k97[32:64, sl], k97[32:64, sl])
                        kev_h[s_] = ctx.cnt["sDVE"]
                    BV(k_ev)

                    def kn_mm(t, s_=s_):
                        ctx.w(t, "sDVE", kev_h[s_])
                        bnk = 4 if s_ % 2 == 0 else 5
                        qs = qsq if s_ % 2 == 0 else qsq2
                        t.matmul(PS[0:1, bnk * 512:bnk * 512 + 512],
                                 ones_col[64:128, 0:1], qs[64:128, :],
                                 start=True, stop=False)
                        ins = t.matmul(PS[0:1, bnk * 512:bnk * 512 + 512],
                                       ones_col[32:64, 0:1], qs[32:64, :],
                                       start=False, stop=True)
                        knm_h[s_] = ctx.inc(ins, "sPE")
                    block.tensor(kn_mm)

                    def kn_max(v, s_=s_):
                        ctx.w(v, "sPE", knm_h[s_])
                        bnk = 4 if s_ % 2 == 0 else 5
                        if s_ == 0:
                            v.tensor_copy(kn2[:, 0:512],
                                          PS[0:1, bnk * 512:bnk * 512 + 512])
                        else:
                            v.tensor_max(kn2[:, 0:512], kn2[:, 0:512],
                                         PS[0:1, bnk * 512:bnk * 512 + 512])
                        knx_h[s_] = ctx.cnt["sDVE"]
                    BV(kn_max)

                def kmax_fin(v):
                    v.reduce_max(kmaxc, kn2[:, 0:512], AX.X)
                    v.tensor_scalar_mul(q2r, q2r, kmaxc[0:1, 0:1])
                BV(kmax_fin)

                def mrow_s(a):
                    ctx.w(a, "sDVE")
                    a.activation(q2r, q2r, AF.Sqrt)
                BS(mrow_s)

                def mrow(v, h=h):
                    ctx.w(v, "sACT")
                    v.tensor_scalar_mul(q97[0:1, :], q2r, -1.0)
                BV(mrow)

                def kd_mm(t, h=h, ci=ci):
                    ctx.w(t, "sDVE")
                    ctx.w(t, "sACT")
                    for half in range(2):
                        ins = t.matmul(
                            PS[64:128, half * 512:half * 512 + 512],
                            kupb[:, h * 64:(h + 1) * 64],
                            ckv_own2[:, ci * CT + half * TC:
                                     ci * CT + (half + 1) * TC],
                            start=True, stop=True)
                    ctx.inc(ins, "sPE")
                block.tensor(kd_mm)

                def kd_ev(v, h=h):
                    ctx.w(v, "sPE")
                    v.tensor_copy(kdiag[64:128, :], PS[64:128, 0:1024])
                BV(kd_ev)

                for tb4 in range(2):
                    def vd_mm(t, h=h, ci=ci, tb4=tb4):
                        ctx.w(t, "sDVE")
                        ctx.w(t, "sACT")
                        for j in range(4):
                            tb = tb4 * 4 + j
                            ins = t.matmul(PS[:, 5 * 512 + j * 64:
                                              5 * 512 + (j + 1) * 64],
                                           ckv_own2[:, ci * CT + tb * 128:
                                                    ci * CT + (tb + 1) * 128],
                                           vupb[:, h * 64:(h + 1) * 64],
                                           start=True, stop=True)
                        ctx.inc(ins, "sPE")
                    block.tensor(vd_mm)

                    def vd_ev(v, tb4=tb4):
                        ctx.w(v, "sPE")
                        v.tensor_copy(
                            vdiag[:, tb4 * 4 * 65:tb4 * 4 * 65 + 260].rearrange(
                                "p (b c) -> p b c", b=4)[:, :, 0:64],
                            PS[:, 5 * 512:5 * 512 + 256].rearrange(
                                "p (b c) -> p b c", b=4))
                    BV(vd_ev)

                for tb4 in range(16):
                    def v_mm(t, tb4=tb4, h=h):
                        ctx.w(t, "sDVE")
                        for j in range(4):
                            tb = tb4 * 4 + j
                            ins = t.matmul(PS[:, 5 * 512 + j * 64:
                                              5 * 512 + (j + 1) * 64],
                                           ckv_kv[:, tb * 128:(tb + 1) * 128],
                                           vupb[:, h * 64:(h + 1) * 64],
                                           start=True, stop=True)
                        ctx.inc(ins, "sPE")
                    block.tensor(v_mm)

                    def v_ev(v, tb4=tb4):
                        ctx.w(v, "sPE")
                        v.tensor_copy(
                            v_tok[:, tb4 * 4 * 65:tb4 * 4 * 65 + 260].rearrange(
                                "p (b c) -> p b c", b=4)[:, :, 0:64],
                            PS[:, 5 * 512:5 * 512 + 256].rearrange(
                                "p (b c) -> p b c", b=4))
                    BV(v_ev)

                smm_h = {}
                sec_h = {}
                omm_h = {}

                def s_mm1(t, kvb):
                    if kvb >= 3:
                        ctx.w(t, "sACT", sec_h[(kvb - 3) // 2])
                    if kvb == 0:
                        ctx.w(t, "sACT")
                        ctx.w(t, "sDVE")
                        ctx.w(t, "sDG")
                    b0 = (kvb % 3) * 2
                    for qt in range(2):
                        ins = t.matmul(PS[:, (b0 + qt) * 512:
                                          (b0 + qt + 1) * 512],
                                       k97[:, kvb * 128:(kvb + 1) * 128],
                                       q97[:, qt * 512:(qt + 1) * 512],
                                       start=True, stop=True)
                    smm_h[kvb] = ctx.inc(ins, "sPE")

                def o_mm1(t, kvb):
                    for qt in range(2):
                        ins = t.matmul(
                            PS[0:65, (6 + qt) * 512:(6 + qt) * 512 + 512],
                            v_tok[:, kvb * 65:(kvb + 1) * 65],
                            PT[kvb % 4][:, qt * 512:(qt + 1) * 512],
                            start=(kvb == 0), stop=False)
                    omm_h[kvb] = ctx.inc(ins, "sPE")

                def kv_pro(t):
                    s_mm1(t, 0)
                    s_mm1(t, 1)
                block.tensor(kv_pro)
                for j in range(32):
                    def kv_sec(a, j=j):
                        w = smm_h[2 * j + 1]
                        if j >= 2:
                            w = max(w, omm_h[2 * j - 3])
                        ctx.w(a, "sPE", w)
                        for i in range(2):
                            kvb = 2 * j + i
                            b0 = (kvb % 3) * 2
                            a.activation(PT[kvb % 4][:, :],
                                         PS[:, b0 * 512:(b0 + 2) * 512],
                                         AF.Exp)
                        sec_h[j] = ctx.cnt["sACT"]
                    BS(kv_sec)

                    def kv_ten(t, j=j):
                        if j < 31:
                            s_mm1(t, 2 * j + 2)
                            s_mm1(t, 2 * j + 3)
                        ctx.w(t, "sACT", sec_h[j])
                        o_mm1(t, 2 * j)
                        o_mm1(t, 2 * j + 1)
                    block.tensor(kv_ten)

                dmm_h = {}
                dmsk_h = {}
                dsec_h = {}
                domm_h = {}
                for qt in range(2):
                    def d_mm4(t, qt=qt):
                        if qt == 0:
                            ctx.w(t, "sACT")
                            ctx.w(t, "sDVE")
                        else:
                            ctx.w(t, "sACT", dsec_h[0])
                        for b4 in range(4):
                            ins = t.matmul(
                                PS[:, b4 * 512:(b4 + 1) * 512],
                                kdiag[:, qt * 512 + b4 * 128:
                                      qt * 512 + (b4 + 1) * 128],
                                q97[:, qt * 512:(qt + 1) * 512],
                                start=True, stop=True)
                        dmm_h[qt] = ctx.inc(ins, "sPE")
                    block.tensor(d_mm4)

                    def d_msk4(v, qt=qt):
                        ctx.w(v, "sPE", dmm_h[qt])
                        for b4 in range(4):
                            sl = slice(b4 * 512, (b4 + 1) * 512)
                            v.tensor_add(PS[:, sl], PS[:, sl],
                                         masks_sb[:, b4 * 512:(b4 + 1) * 512])
                        dmsk_h[qt] = ctx.cnt["sDVE"]
                    BV(d_msk4)

                    def d_exp4(a, qt=qt):
                        ctx.w(a, "sDVE", dmsk_h[qt])
                        if qt == 0:
                            ctx.w(a, "sPE")
                        else:
                            ctx.w(a, "sPE", domm_h[0])
                        a.activation(PT[0][:, :], PS[:, 0:1024], AF.Exp)
                        a.activation(PT[1][:, :], PS[:, 1024:2048], AF.Exp)
                        dsec_h[qt] = ctx.cnt["sACT"]
                    BS(d_exp4)

                    def d_omm4(t, qt=qt):
                        ctx.w(t, "sACT", dsec_h[qt])
                        for b4 in range(4):
                            kb = qt * 4 + b4
                            ins = t.matmul(
                                PS[0:65, (6 + qt) * 512:(6 + qt) * 512 + 512],
                                vdiag[:, kb * 65:(kb + 1) * 65],
                                PT[b4 // 2][:, (b4 % 2) * 512:
                                             (b4 % 2 + 1) * 512],
                                start=False, stop=(b4 == 3))
                        domm_h[qt] = ctx.inc(ins, "sPE")
                    block.tensor(d_omm4)

                for qt in range(2):
                    rno = rows2[64:65, qt * 512:(qt + 1) * 512]

                    def n_rc(v, qt=qt):
                        ctx.w(v, "sPE")
                        v.reciprocal(rno, PS[64:65, (6 + qt) * 512:
                                             (6 + qt) * 512 + 512])
                    BV(n_rc)

                    def n_bc(t, qt=qt):
                        ctx.w(t, "sDVE")
                        ctx.inc(t.matmul(PS[0:64, 2 * 512:2 * 512 + 512],
                                         ones3[64:65, 0:64], rno,
                                         start=True, stop=True), "sPE")
                    block.tensor(n_bc)

                    def n_ev(v, qt=qt):
                        ctx.w(v, "sPE")
                        v.tensor_copy(attn_tmp[:, 0:512],
                                      PS[0:64, (6 + qt) * 512:
                                         (6 + qt) * 512 + 512])
                    BV(n_ev)

                    def n_mul(v, qt=qt, h=h):
                        ctx.w(v, "sPE")
                        v.tensor_mul(
                            attnp[(h % 2) * 64:(h % 2) * 64 + 64,
                                  qt * 512:(qt + 1) * 512],
                            attn_tmp[:, 0:512], PS[0:64, 2 * 512:2 * 512 + 512])
                    BV(n_mul)

                if h % 2 == 1:
                    hp = h // 2

                    def wo_ld(s, hp=hp):
                        ctx.w(s, "sPE")
                        ctx.w(s, "sDVE")
                        ctx.inc(s.dma_start(
                            out=wo_ring[hp % 2][:, 0:1024],
                            in_=wob_d[hp * 128:(hp + 1) * 128, :]), "sDW", 16)
                    block.sync(wo_ld)

                    for om in range(8):
                        def wo_mm(t, om=om, hp=hp):
                            ctx.w(t, "sDW")
                            ctx.w(t, "sDVE")
                            for half in range(2):
                                ins = t.matmul(
                                    PS[:, (2 + half) * 512:(3 + half) * 512],
                                    wo_ring[hp % 2][:, om * 128:(om + 1) * 128],
                                    attnp[:, half * TC:(half + 1) * TC],
                                    start=True, stop=True)
                            ctx.inc(ins, "sPE")
                        block.tensor(wo_mm)

                        def wo_add(v, om=om, ci=ci):
                            ctx.w(v, "sPE")
                            rv = res[:, col(ci, om, 0):col(ci, om, 0) + CT]
                            v.tensor_add(rv, rv, bank(2, 2))
                        BV(wo_add)
        AR.pop()
        ctx.barrier()
        if upto == "attn":
            emit_out()
            return nc

        for ci in range(NCH):
            emit_ffn(ci)
        ctx.barrier()

        emit_out()
    return nc


def kernel(**inputs):
    per_core = make_host_inputs(inputs)
    nc = build_nc()
    r = run_bass_kernel_spmd(nc, per_core, core_ids=list(range(8)))
    return assemble_output(r.results)
